# version 08
# baseline (speedup 1.0000x reference)
import math
import numpy as np
from contextlib import ExitStack

import concourse.bass as bass
import concourse.mybir as mybir

F32 = mybir.dt.float32
BF16 = mybir.dt.bfloat16
AF = mybir.ActivationFunctionType
ALU = mybir.AluOpType

T = 8192
D = 1024
NT = 64
NG = 16
TOWN = 2048
TGW = 256
NTG = T // TGW
S = 64
LV = 12
LN8 = math.log(0.125)


def host_prep(inp, core):
    b, q = core // 4, core % 4
    hs = slice(4 * q * S, (4 * q + 4) * S)
    hl = slice(4 * q * LV, (4 * q + 4) * LV)
    hh = slice(4 * q, 4 * q + 4)
    C = np.ascontiguousarray
    A = lambda k: np.asarray(inp[k], dtype=np.float32)
    d = {}
    d["xT"] = C(A("L0_attn_a_x")[b].T)
    d["aWqT"] = C(A("L0_attn_a_Wq")[hs].T)
    d["aWkT"] = C(A("L0_attn_a_Wk")[hs].T)
    d["aWvT"] = C(A("L0_attn_a_Wv")[hs].T)
    d["aWdlT"] = C(A("L0_attn_a_Wdl")[hl].T)
    d["aWdtT"] = C(A("L0_attn_a_Wdt")[hh].T)
    hl_, r_, d_ = np.meshgrid(np.arange(4), np.arange(4), np.arange(64),
                              indexing="ij")
    dperm = ((r_ * 4 + hl_) * 64 + d_).ravel()
    d["aWoutT"] = C(A("L0_attn_a_Wout").T[dperm])
    d["aA_log"] = C(A("L0_attn_a_A_log")[hh].reshape(4, 1))
    d["adt_bias"] = C(A("L0_attn_a_dt_bias")[hh].reshape(4, 1))
    d["aLp"] = C(A("L0_attn_a_Lp")[hh].reshape(48, 1))
    d["aln_g"] = C(A("L0_attn_a_ln_g").reshape(8, 128).T)
    d["aln_b"] = C(A("L0_attn_a_ln_b").reshape(8, 128).T)
    for nm in ("q", "k", "v"):
        d[f"bW{nm}T"] = C(A(f"L0_attn_b_w_{nm}")[hs].T)
    d["bWoT"] = C(A("L0_attn_b_w_o").T[dperm])
    for nm in ("q", "k", "v", "o"):
        d[f"bs_{nm}"] = C(A(f"L0_attn_b_s_{nm}").reshape(1, 1))
    d["bb_q"] = C(A("L0_attn_b_b_q")[hs].reshape(2, 128).T)
    d["bb_k"] = C(A("L0_attn_b_b_k")[hs].reshape(2, 128).T)
    d["bb_v"] = C(A("L0_attn_b_b_v")[hs].reshape(1, 256))
    d["bb_o"] = C(A("L0_attn_b_b_o").reshape(8, 128).T)
    d["makerT"] = C(np.transpose(A("L0_ffn_rail_maker"), (0, 1, 3, 2)))
    d["upT"] = C(np.transpose(A("L0_ffn_rail_up_proj"), (0, 1, 3, 2)))
    h_, r_, k_ = np.meshgrid(np.arange(4), np.arange(64), np.arange(64), indexing="ij")
    old = h_ * 4096 + r_ * 64 + k_
    new = (r_ * 2 + h_ // 2) * 128 + (h_ % 2) * 64 + k_
    perm = np.empty(16384, dtype=np.int64)
    perm[new.ravel()] = old.ravel()
    d["wpT"] = C(A("L0_ffn_w_cproj")[:, perm].T)
    return d


def make_consts():
    d = {}
    d["ident"] = np.eye(128, dtype=np.float32)
    d["ones1"] = np.ones((1, 128), dtype=np.float32)
    d["onescol"] = np.ones((128, 1), dtype=np.float32)
    t = np.arange(128)
    xor = t[:, None] ^ t[None, :]
    lv = np.zeros((128, 128), dtype=np.int64)
    for k in range(LV - 1):
        lv += (xor >= (1 << k))
    causal = (t[None, :] <= t[:, None])
    mt = np.zeros((128, 8 * 128), dtype=np.float32)
    for l in range(8):
        mt[:, l * 128:(l + 1) * 128] = ((lv == l) & causal).astype(np.float32)
    import ml_dtypes
    d["maskT"] = mt.astype(ml_dtypes.bfloat16)
    return d


INPUT_SPECS = [
    ("xT", [D, T]), ("aWqT", [D, 256]), ("aWkT", [D, 256]), ("aWvT", [D, 256]),
    ("aWdlT", [D, 48]), ("aWdtT", [D, 4]), ("aWoutT", [D, D]),
    ("aA_log", [4, 1]), ("adt_bias", [4, 1]), ("aLp", [48, 1]),
    ("aln_g", [128, 8]), ("aln_b", [128, 8]),
    ("bWqT", [D, 256]), ("bWkT", [D, 256]), ("bWvT", [D, 256]), ("bWoT", [D, D]),
    ("bs_q", [1, 1]), ("bs_k", [1, 1]), ("bs_v", [1, 1]), ("bs_o", [1, 1]),
    ("bb_q", [128, 2]), ("bb_k", [128, 2]), ("bb_v", [1, 256]), ("bb_o", [128, 8]),
    ("makerT", [4, 64, 16, 16]), ("upT", [4, 64, 16, 64]), ("wpT", [16384, D]),
    ("ident", [128, 128]), ("ones1", [1, 128]), ("onescol", [128, 1]),
    ("maskT", [128, 8 * 128]),
]


class Sem:
    def __init__(self, h, name):
        self.h, self.name, self.val = h, name, 0


class MK:
    ENGS = ("sync", "tensor", "vector", "scalar", "gpsimd")

    def __init__(self, nc, phases=99, dbg=()):
        self.nc = nc
        self.phases = phases
        self.dbg = set(dbg)
        self.cmds = {e: [] for e in self.ENGS}

    def on(self, eng, fn, incs=()):
        for sem, n in incs:
            sem.val += n
        vals = [(sem.h, n) for sem, n in incs]

        def cmd(e, fn=fn, vals=vals):
            inst = fn(e)
            for h, n in vals:
                inst = inst.then_inc(h, n)
        self.cmds[eng].append(cmd)

    def wait(self, eng, sem, val=None):
        v = sem.val if val is None else val
        h = sem.h
        self.cmds[eng].append(lambda e, h=h, v=v: e.wait_ge(h, v))

    def dma(self, eng, out, in_, sem):
        self.on(eng, lambda e, out=out, in_=in_: e.dma_start(out=out, in_=in_),
                [(sem, 16)])

    def mm(self, out, lhsT, rhs, start=True, stop=True, inc=()):
        self.on("tensor", lambda e, o=out, l=lhsT, r=rhs, s1=start, s2=stop:
                e.matmul(o, l, r, start=s1, stop=s2), inc)

    def tr(self, out, in_, n, inc=()):
        idn = self.ident_ap[0:n, 0:n]
        self.on("tensor", lambda e, o=out, i=in_, d=idn: e.transpose(o, i, d), inc)

    def v(self, fn, inc=()):
        self.on("vector", fn, inc)

    def a(self, fn, inc=()):
        self.on("scalar", fn, inc)

    def g(self, fn, inc=()):
        self.on("gpsimd", fn, inc)

    def qsem(self, eng):
        return {"sync": self.sDs, "gpsimd": self.sDg, "scalar": self.sDc}[eng]

    def build(self):
        nc = self.nc
        st = self.stack = ExitStack()
        st.__enter__()
        dram = lambda name, shape, dt, kind="Internal": nc.dram_tensor(
            name, list(shape), dt, kind=kind).ap()
        self.i = {n: dram(n, s, BF16 if n == "maskT" else F32,
                          kind="ExternalInput") for n, s in INPUT_SPECS}
        self.out_d = dram("out", [D, TOWN], F32, kind="ExternalOutput")
        self.Qa_d = dram("Qa_d", [2, 128, T], BF16)
        self.Ya_d = dram("Ya_d", [256, T], BF16)
        self.Yg_d = dram("Yg_d", [4 * 256, T], BF16)
        self.x2_d = dram("x2_d", [D, TOWN], BF16)
        self.c3_d = dram("c3_d", [D, TOWN], BF16)
        self.xg3_d = dram("xg3_d", [4 * D, TOWN], BF16)
        self.wp_bf = dram("wp_bf", [16384, D], BF16)
        self.Yo_d = dram("Yo_d", [4 * 256, TOWN], BF16)

        sb = lambda name, shape, dt: st.enter_context(
            nc.sbuf_tensor(name, list(shape), dt))
        self.R = sb("R", [128, 8 * TOWN], F32)
        self.KV = sb("KV", [128, 33024], BF16)
        self.VOFF = 16384
        self.MISC = sb("MISC", [128, 5376], BF16)
        self.CRL = self.MISC[:, 0:2048]
        self.XR = self.MISC[:, 2048:4096]
        self.SCS0, self.QF11, self.QF10 = 2048, 3584, 4096
        self.QN8, self.QN9, self.SCS2 = 4608, 4736, 4864
        self.F32S = sb("F32S", [128, 5456], F32)
        self.DD0, self.FF0, self.MF = 0, 512, 1024
        self.CGT, self.LAMT, self.CSC, self.RSC = 2048, 2304, 5376, 5440
        self.WSB = sb("WSB", [128, 6560], BF16)
        self.wq_off, self.wk_off, self.wv_off = 0, 2048, 4096
        self.wdl_off, self.wdt_off = 6144, 6528
        self.QSTG = sb("QSTG", [128, 4096], BF16)
        self.T48 = sb("T48", [48, TGW], F32)
        self.T48B = sb("T48B", [48, TGW], F32)
        self.T4 = sb("T4", [4, TGW], F32)
        self.T4B = sb("T4B", [4, TGW], F32)
        self.ONES4 = sb("ONES4", [4, TGW], F32)
        self.CARRY = sb("CARRY", [4, 1], F32)
        self.PP = sb("PP", [48, 1], F32)
        self.AA = sb("AA", [4, 2], F32)
        self.CONSTB = sb("CONSTB", [128, 1024], BF16)
        self.IDENT = sb("IDENT", [128, 128], F32)
        self.ONE1 = sb("ONE1", [1, 128], F32)
        self.ONEC = sb("ONEC", [128, 1], F32)
        self.ONET = sb("ONET", [128, 128], F32)
        self.LNG = sb("LNG", [128, 16], F32)
        self.BBS = sb("BBS", [128, 16], F32)
        self.BVB = sb("BVB", [128, 256], F32)
        self.STG = sb("STG", [128, 512], F32)
        self.STG2 = sb("STG2", [128, 512], F32)
        self.WPC = sb("WPC", [128, 512], BF16)
        self.Y2 = sb("Y2", [128, 1024], BF16)
        self.HCH = sb("HCH", [128, 2 * 256], BF16)
        self.RB = sb("RB", [128, 1024], F32)
        self.SCL = sb("SCL", [128, 4], F32)
        self.mu_r = self.T48B[0:1, 0:TGW]
        self.rstd_r = self.T4B[0:1, 0:TGW]
        self.nu_r = self.BVB[0:1, 0:TGW]
        self.tmp_r = self.STG[0:1, 0:TGW]

        self.ident_ap = self.IDENT[:, :]
        self.PB = [st.enter_context(nc.psum_tensor(f"pb{i}", [128, 512], F32))
                   for i in range(8)]
        for nm in ("sDs", "sDg", "sDc", "sT", "sV", "sA", "sG", "sCC",
                   "sQ0", "sQ1"):
            setattr(self, nm, Sem(st.enter_context(nc.semaphore(nm)), nm))

        self.build_program()

        with nc.Block() as block:
            for eng in self.ENGS:
                cmds = self.cmds[eng]

                def emit(e, cmds=cmds):
                    for c in cmds:
                        c(e)
                getattr(block, eng)(emit)
        st.__exit__(None, None, None)
        return nc

    def tap_dram(self, name, src, shape, dtype):
        t = self.nc.dram_tensor("dbg_" + name, list(shape), dtype,
                                kind="ExternalOutput").ap()
        p, f = shape
        for r0 in range(0, p, 128):
            rp = min(128, p - r0)
            for c0 in range(0, f, 512):
                w = min(512, f - c0)
                buf = self.Y2[0:rp, 0:w] if dtype == BF16 else self.STG[0:rp, 0:w]
                self.dma("sync", buf, src[r0:r0 + rp, c0:c0 + w], self.sDs)
                self.wait("sync", self.sDs)
                self.dma("sync", t[r0:r0 + rp, c0:c0 + w], buf, self.sDs)
                self.wait("sync", self.sDs)

    def tap_sbuf(self, name, src, shape, dtype):
        t = self.nc.dram_tensor("dbg_" + name, list(shape), dtype,
                                kind="ExternalOutput").ap()
        self.dma("sync", t[0:shape[0], 0:shape[1]], src, self.sDs)
        self.wait("sync", self.sDs)

    def build_program(self):
        self.p0_consts_weights()
        self.p1_xln1_proj()
        if self.phases >= 2:
            self.p2_attn_a()
        if self.phases >= 4:
            self.p4_close_attn(kind="a")
            if "p4" in self.dbg:
                self.wait("sync", self.sV, self.sV.val)
                self.tap_sbuf("R1", self.R[:, :], (128, 8 * TOWN), F32)
        if self.phases >= 5:
            self.p5_ffn(0)
            if "p5" in self.dbg:
                self.wait("sync", self.sV, self.sV.val)
                self.tap_sbuf("R2", self.R[:, :], (128, 8 * TOWN), F32)
        if self.phases >= 6:
            self.p6_attn_b()
        if self.phases >= 7:
            self.p5_ffn(1)
        if self.phases >= 8:
            self.p7_out()

    def p0_consts_weights(self):
        i = self.i

        sDs_h = self.sDs.h

        def r_init(e):
            q = e.partition_id() % 4
            self._q_sync = q
            inst = None
            for ch in range(8):
                inst = e.dma_start(
                    out=self.R[:, ch * TOWN:(ch + 1) * TOWN],
                    in_=i["xT"][ch * 128:(ch + 1) * 128, bass.ts(q, TOWN)])
                inst.then_inc(sDs_h, 16)
            return inst
        self.sDs.val += 16 * 8
        self.on("sync", r_init)
        self.r_init_done = self.sDs.val

        for dst, src in [(self.CONSTB[:, 0:1024], i["maskT"][:, :]),
                         (self.IDENT[:, :], i["ident"][:, :]),
                         (self.ONE1[:, :], i["ones1"][:, :]),
                         (self.ONEC[:, :], i["onescol"][:, :]),
                         (self.PP[:, :], i["aLp"][:, :]),
                         (self.AA[0:4, 1:2], i["adt_bias"][:, :]),
                         (self.STG[0:4, 0:1], i["aA_log"][:, :]),
                         (self.LNG[:, 0:8], i["aln_g"][:, :]),
                         (self.LNG[:, 8:16], i["aln_b"][:, :])]:
            self.dma("sync", dst, src, self.sDs)
        const_done = self.const_done = self.sDs.val
        self.wait("sync", self.sDs, const_done)
        self.wait("vector", self.sDs, const_done)
        self.v(lambda e: e.memset(self.ONET[:, :], 1.0))
        self.v(lambda e: e.memset(self.ONES4[:, :], 1.0))
        self.v(lambda e: e.memset(self.CARRY[:, :], 0.0), [(self.sV, 1)])
        self.wait("scalar", self.sDs, const_done)
        self.a(lambda e: e.activation(self.AA[0:4, 0:1], self.STG[0:4, 0:1], AF.Exp),
               [(self.sA, 1)])
        self.wait("vector", self.sA)
        self.v(lambda e: e.tensor_scalar(self.AA[0:4, 0:1], self.AA[0:4, 0:1],
                                         -1.0, None, ALU.mult), [(self.sV, 1)])
        self.consts_ready = self.sV.val

        casts = []
        for ch in range(8):
            casts += [(i["aWqT"][ch * 128:(ch + 1) * 128, :], self.wq_off + ch * 256, 256),
                      (i["aWkT"][ch * 128:(ch + 1) * 128, :], self.wk_off + ch * 256, 256),
                      (i["aWvT"][ch * 128:(ch + 1) * 128, :], self.wv_off + ch * 256, 256),
                      (i["aWdlT"][ch * 128:(ch + 1) * 128, :], self.wdl_off + ch * 48, 48),
                      (i["aWdtT"][ch * 128:(ch + 1) * 128, :], self.wdt_off + ch * 4, 4)]
        self.emit_casts(casts, self.WSB)
        self.w_cast_done = self.wcast_g

    def emit_casts(self, casts, dstbuf):
        pos, group, flushes = 0, [], []
        for src, off, w in casts:
            if pos + w > 512:
                flushes.append(group)
                group, pos = [], 0
            group.append((src, off, w, pos))
            pos += w
        if group:
            flushes.append(group)
        for gi, grp in enumerate(flushes):
            if gi >= 1:
                self.wait("gpsimd", self.sG, self.sG.val)
            for src, off, w, p0 in grp:
                self.dma("gpsimd", self.STG2[:, p0:p0 + w], src, self.sDg)
            self.wait("gpsimd", self.sDg, self.sDg.val)
            for k, (src, off, w, p0) in enumerate(grp):
                self.g(lambda e, off=off, w=w, p0=p0, d=dstbuf:
                       e.tensor_copy(d[:, off:off + w], self.STG2[:, p0:p0 + w]),
                       [(self.sG, 1)])
            self.wcast_g = self.sG.val

    def emit_wp_prep(self):
        it = self._wp_it
        self._wp_it += 1
        i = self.i
        for half in range(2):
            stg = self.STG2[:, 0:512]
            wpc = self.WPC[:, 0:512]
            if self._wp_guard[0] is not None:
                self.wait("gpsimd", self.sG, self._wp_guard[0])
            elif getattr(self, "wcast_g", 0):
                self.wait("gpsimd", self.sG, self.wcast_g)
            if self._wp_outg[0] is not None:
                self.wait("gpsimd", self.sDg, self._wp_outg[0])
            self.dma("gpsimd", stg,
                     i["wpT"][it * 128:(it + 1) * 128, half * 512:(half + 1) * 512],
                     self.sDg)
            self.wait("gpsimd", self.sDg, self.sDg.val)
            self.g(lambda e, wpc=wpc, stg=stg: e.tensor_copy(wpc, stg),
                   [(self.sG, 1)])
            self._wp_guard[0] = self.sG.val
            self.wait("gpsimd", self.sG, self.sG.val)
            self.dma("gpsimd",
                     self.wp_bf[it * 128:(it + 1) * 128, half * 512:(half + 1) * 512],
                     wpc, self.sDg)
            self._wp_outg[0] = self.sDg.val

    def emit_ln(self, src_pair, dst_pair, affine=None):
        PB, W = self.PB, TGW
        mu, rstd, nu, tmp = self.mu_r, self.rstd_r, self.nu_r, self.tmp_r
        p6f = PB[6][:, 0:2 * W].rearrange("p (c t) -> p c t", c=2)
        p7f = PB[7][:, 0:2 * W].rearrange("p (c t) -> p c t", c=2)
        self.wait("scalar", self.sV, self.sV.val)
        mmdone = []
        for pi in range(4):
            rb = self.RB[:, (pi % 2) * 2 * W:(pi % 2) * 2 * W + 2 * W]
            rb3 = rb.rearrange("p (c t) -> p c t", c=2)
            if pi >= 2:
                self.wait("scalar", self.sT, mmdone[pi - 2])
            self.a(lambda e, s=src_pair(pi), rb3=rb3: e.activation(
                rb3, s, AF.Square), [(self.sA, 1)])
            self.wait("tensor", self.sA, self.sA.val)
            self.mm(PB[6][0:1, 0:2 * W], self.ONEC[:, :], src_pair(pi),
                    start=(pi == 0), stop=(pi == 3))
            self.mm(PB[7][0:1, 0:2 * W], self.ONEC[:, :], rb3,
                    start=(pi == 0), stop=(pi == 3), inc=[(self.sT, 1)])
            mmdone.append(self.sT.val)
        self.wait("vector", self.sT, self.sT.val)
        self.v(lambda e: e.tensor_scalar(mu, PB[6][0:1, 0:W], 1.0 / D, None,
                                         ALU.mult))
        self.v(lambda e: e.tensor_scalar(tmp, PB[7][0:1, 0:W], 1.0 / D, None,
                                         ALU.mult), [(self.sV, 1)])
        self.wait("vector", self.sV, self.sV.val)
        self.v(lambda e: e.scalar_tensor_tensor(mu, PB[6][0:1, W:2 * W], 1.0 / D,
                                                mu, ALU.mult, ALU.add))
        self.v(lambda e: e.scalar_tensor_tensor(tmp, PB[7][0:1, W:2 * W], 1.0 / D,
                                                tmp, ALU.mult, ALU.add),
               [(self.sV, 1)])
        self.wait("scalar", self.sV, self.sV.val)
        self.a(lambda e: e.activation(self.RB[0:1, 0:W], mu, AF.Square), [(self.sA, 1)])
        self.wait("vector", self.sA, self.sA.val)
        self.wait("vector", self.sV, self.sV.val)
        self.v(lambda e: e.scalar_tensor_tensor(tmp, tmp, 1e-5, self.RB[0:1, 0:W],
                                                ALU.add, ALU.subtract), [(self.sV, 1)])
        self.wait("scalar", self.sV, self.sV.val)
        self.a(lambda e: e.activation(tmp, tmp, AF.Ln), [(self.sA, 1)])
        self.wait("scalar", self.sA, self.sA.val)
        self.a(lambda e: e.activation(rstd, tmp, AF.Exp, scale=-0.5), [(self.sA, 1)])
        self.wait("vector", self.sA, self.sA.val)
        self.v(lambda e: e.tensor_mul(nu, mu, rstd), [(self.sV, 1)])
        self.wait("tensor", self.sV, self.sV.val)
        self.mm(PB[6][:, 0:W], self.ONE1[:, :], rstd)
        self.mm(PB[6][:, W:2 * W], self.ONE1[:, :], rstd)
        self.mm(PB[7][:, 0:W], self.ONE1[:, :], nu)
        self.mm(PB[7][:, W:2 * W], self.ONE1[:, :], nu, inc=[(self.sT, 1)])
        self.wait("vector", self.sT, self.sT.val)
        if affine == "stats":
            return
        tmpt = self.RB[:, 0:2 * W]
        tmpt3 = tmpt.rearrange("p (c t) -> p c t", c=2)
        for pi in range(4):
            if pi > 0:
                self.wait("vector", self.sV, self.sV.val)
            self.v(lambda e, s=src_pair(pi), t=tmpt3: e.tensor_mul(t, s, p6f),
                   [(self.sV, 1)])
            self.wait("vector", self.sV, self.sV.val)
            self.v(lambda e, t=tmpt3, d=dst_pair(pi): e.tensor_sub(d, t, p7f),
                   [(self.sV, 1)])

    def emit_ln_pairs_from_R(self, og):
        W = TGW
        r3 = self.R[:, :].rearrange("p (c t) -> p c t", c=8)
        return lambda pi: r3[:, 2 * pi:2 * pi + 2, og * W:(og + 1) * W]

    def emit_softplus(self, x, tmp):
        self.wait("vector", self.sV, self.sV.val)
        self.v(lambda e: e.tensor_scalar(tmp, x, -1.0, None, ALU.mult), [(self.sV, 1)])
        self.wait("vector", self.sV, self.sV.val)
        self.v(lambda e: e.tensor_max(tmp, x, tmp), [(self.sV, 1)])
        self.wait("scalar", self.sV, self.sV.val)
        self.a(lambda e: e.activation(tmp, tmp, AF.Exp, scale=-1.0), [(self.sA, 1)])
        self.wait("scalar", self.sA, self.sA.val)
        self.a(lambda e: e.activation(tmp, tmp, AF.Ln, bias=1.0), [(self.sA, 1)])
        self.wait("vector", self.sA, self.sA.val)
        self.v(lambda e: e.tensor_scalar(x, x, 0.0, None, ALU.max), [(self.sV, 1)])
        self.wait("vector", self.sV, self.sV.val)
        self.v(lambda e: e.tensor_add(x, x, tmp), [(self.sV, 1)])

    def p1_xln1_proj(self):
        i, PB = self.i, self.PB
        W = TGW
        xt = lambda ch: self.F32S[:, ch * W:(ch + 1) * W]
        xlb = lambda ch: self.MISC[:, ch * W:(ch + 1) * W]
        wsb = self.WSB
        self._wp_it, self._wp_guard, self._wp_outg = 0, [None, None], [None, None]
        self.wait("tensor", self.sG, self.w_cast_done)
        self.v(lambda e: e.memset(
            self.KV[:, self.VOFF + 64:self.VOFF + 16640:65], 1.0), [(self.sV, 1)])
        self.wait("tensor", self.sV, self.consts_ready)
        proj_done = 0
        v_evac_done = 0
        lamcg_done = 0
        for tg in range(NTG):
            for ch in range(8):
                if tg > 0 and ch == 0:
                    self.wait("sync", self.sV, self.xln_consumed)
                self.dma("sync", xt(ch), i["xT"][ch * 128:(ch + 1) * 128,
                                                 tg * W:(tg + 1) * W], self.sDs)
            self.wait("scalar", self.sDs, self.sDs.val)
            if tg > 0:
                self.wait("tensor", self.sV, v_evac_done)
                self.wait("vector", self.sT, proj_done)
            self.emit_ln(lambda pi: self.F32S[:, pi * 512:(pi + 1) * 512],
                         lambda pi: self.MISC[:, pi * 512:(pi + 1) * 512])
            self.xln_consumed = self.sV.val
            xln_done = self.sV.val

            self.wait("tensor", self.sV, xln_done)
            if tg > 0:
                self.wait("tensor", self.sV, self.qk_evac_done)
                self.wait("tensor", self.sV, lamcg_done)
            for ch in range(8):
                for p in range(2):
                    self.mm(PB[0 + p][:, 0:W],
                            wsb[:, self.wq_off + ch * 256 + p * 128:
                                self.wq_off + ch * 256 + (p + 1) * 128],
                            xlb(ch), start=(ch == 0), stop=(ch == 7))
                    self.mm(PB[2 + p][:, 0:W],
                            wsb[:, self.wk_off + ch * 256 + p * 128:
                                self.wk_off + ch * 256 + (p + 1) * 128],
                            xlb(ch), start=(ch == 0), stop=(ch == 7))
                self.mm(PB[4][0:48, 0:W],
                        wsb[:, self.wdl_off + ch * 48:self.wdl_off + (ch + 1) * 48],
                        xlb(ch), start=(ch == 0), stop=(ch == 7))
                self.mm(PB[5][0:4, 0:W],
                        wsb[:, self.wdt_off + ch * 4:self.wdt_off + (ch + 1) * 4],
                        xlb(ch), start=(ch == 0), stop=(ch == 7))
                for sub in range(2):
                    self.mm(PB[6 + sub][:, 0:256],
                            xlb(ch)[:, sub * 128:(sub + 1) * 128],
                            wsb[:, self.wv_off + ch * 256:self.wv_off + (ch + 1) * 256],
                            start=(ch == 0), stop=(ch == 7),
                            inc=[(self.sT, 1)] if (ch == 7 and sub == 1) else ())
            proj_done = self.sT.val

            self.wait("vector", self.sT, proj_done)
            for p in range(2):
                self.v(lambda e, p=p, tg=tg: e.tensor_copy(
                    self.KV[:, p * T + tg * W:p * T + (tg + 1) * W],
                    PB[2 + p][:, 0:W]))
                self.v(lambda e, p=p: e.tensor_copy(
                    self.QSTG[:, p * W:(p + 1) * W], PB[0 + p][:, 0:W]),
                    [(self.sV, 1)] if p == 1 else ())
            self.qk_evac_done = self.sV.val
            self.wait("sync", self.sV, self.qk_evac_done)
            for p in range(2):
                self.dma("sync", self.Qa_d[p][:, tg * W:(tg + 1) * W],
                         self.QSTG[:, p * W:(p + 1) * W], self.sDs)
            self.q_dma_done = self.sDs.val
            for sub in range(2):
                j = tg * 2 + sub
                dst = self.KV[:, self.VOFF + j * 260:self.VOFF + (j + 1) * 260]
                dst3 = dst.rearrange("p (h c) -> p h c", h=4)[:, :, 0:64]
                src3 = PB[6 + sub][:, 0:256] \
                    .rearrange("p (h c) -> p h c", h=4)
                self.v(lambda e, d=dst3, s=src3: e.tensor_copy(d, s),
                       [(self.sV, 1)] if sub == 1 else ())
            v_evac_done = self.sV.val

            self.v(lambda e: e.tensor_scalar(self.T48[:, :], PB[4][0:48, 0:W],
                                             self.PP[0:48, 0:1], None, ALU.mult))
            self.v(lambda e: e.tensor_scalar(self.T4[:, :], PB[5][0:4, 0:W],
                                             self.AA[0:4, 1:2], None, ALU.add),
                   [(self.sV, 1)])
            self.emit_softplus(self.T48[:, :], self.T48B[:, :])
            self.emit_softplus(self.T4[:, :], self.T4B[:, :])
            self.wait("vector", self.sV, self.sV.val)
            self.v(lambda e: e.tensor_scalar(self.T4[:, :], self.T4[:, :],
                                             self.AA[0:4, 0:1], None, ALU.mult),
                   [(self.sV, 1)])
            self.wait("vector", self.sV, self.sV.val)
            self.v(lambda e: e.tensor_tensor_scan(
                self.T4B[:, :], self.ONES4[:, :], self.T4[:, :],
                self.CARRY[0:4, 0:1], ALU.mult, ALU.add), [(self.sV, 1)])
            self.wait("vector", self.sV, self.sV.val)
            self.v(lambda e: e.tensor_copy(self.CARRY[0:4, 0:1],
                                           self.T4B[0:4, W - 1:W]), [(self.sV, 1)])
            cg_done = self.sV.val

            self.wait("tensor", self.sV, cg_done)
            for sub in range(2):
                j = tg * 2 + sub
                self.tr(PB[4][:, 0:48], self.T48[:, sub * 128:(sub + 1) * 128], 48,
                        inc=[(self.sT, 1)])
                t_lam = self.sT.val
                self.tr(PB[5][:, 0:4], self.T4B[:, sub * 128:(sub + 1) * 128], 4,
                        inc=[(self.sT, 1)])
                t_cg = self.sT.val
                self.wait("vector", self.sT, t_lam)
                dlam = self.F32S[:, self.LAMT + j:self.LAMT + j + 48 * 64:64]
                self.v(lambda e, d=dlam: e.tensor_copy(d, PB[4][:, 0:48]))
                self.wait("vector", self.sT, t_cg)
                dcg = self.F32S[:, self.CGT + j:self.CGT + j + 4 * 64:64]
                self.v(lambda e, d=dcg: e.tensor_copy(d, PB[5][:, 0:4]),
                       [(self.sV, 1)])
                if sub == 0:
                    self.wait("tensor", self.sV, self.sV.val)
            lamcg_done = self.sV.val

            for _ in range(4):
                if self._wp_it < 128:
                    self.emit_wp_prep()

        self.p1_done = dict(sV=self.sV.val, sT=self.sT.val, sDs=self.sDs.val)
        if "p1" in self.dbg:
            self.wait("sync", self.sV, self.sV.val)
            self.tap_sbuf("K0", self.KV[:, 0:T], (128, T), BF16)
            self.tap_sbuf("K1", self.KV[:, T:2 * T], (128, T), BF16)
            self.tap_sbuf("Vtm", self.KV[:, self.VOFF:self.VOFF + 64 * 260],
                          (128, 64 * 260), BF16)
            self.tap_sbuf("cgT", self.F32S[:, self.CGT:self.CGT + 256], (128, 256), F32)
            self.tap_sbuf("LamT", self.F32S[:, self.LAMT:self.LAMT + 3072],
                          (128, 3072), F32)
            self.tap_dram("Qa0", self.Qa_d[0], (128, T), BF16)
            self.tap_dram("Qa1", self.Qa_d[1], (128, T), BF16)

    def p2_attn_a(self):
        PB = self.PB
        while self._wp_it < 128:
            self.emit_wp_prep()

        hm = [(h, m) for h in range(4) for m in range(NG)]
        self.yg_head_done = {}
        self.v(lambda e: e.memset(self.MISC[:, 3584:4864], 0.0), [(self.sV, 1)])
        self.wait("gpsimd", self.sV, self.p1_done["sV"])
        self.wait("gpsimd", self.sDs, self.const_done)
        self.wait("sync", self.sDs, self.sDs.val)
        self._qs_guard = [None, None]
        qsem = [self.sQ0, self.sQ1]

        def qs_dma(idx):
            h, m = hm[idx]
            slot = idx % 2
            if self._qs_guard[slot] is not None:
                tval, vval = self._qs_guard[slot]
                self.wait("sync", self.sT, tval)
                self.wait("sync", self.sV, vval)
            self.dma("sync", self.QSTG[:, slot * 512:(slot + 1) * 512],
                     self.Qa_d[h // 2][:, m * 512:(m + 1) * 512], qsem[slot])
            return qsem[slot].val
        qs_done = {0: qs_dma(0)}

        y_evac_done = 0
        for idx, (h, m) in enumerate(hm):
            slot = idx % 2
            if idx + 1 < len(hm):
                qs_done[idx + 1] = qs_dma(idx + 1)
            qrow = (h % 2) * 64
            QS = self.QSTG[qrow:qrow + 64, slot * 512:(slot + 1) * 512]
            kro = (h % 2) * 64
            kpo = (h // 2) * T
            if m == 0 and h > 0:
                qrow_old = ((h - 1) % 2) * 64
                self.wait("vector", self.sT, self.sT.val)
                self.v(lambda e, r=qrow_old: e.memset(
                    self.MISC[r:r + 64, 3584:4864], 0.0), [(self.sV, 1)])
            cgc = lambda j: self.F32S[:, self.CGT + h * 64 + j:
                                      self.CGT + h * 64 + j + 1]
            cgcols = lambda j0, n: self.F32S[:, self.CGT + h * 64 + j0:
                                             self.CGT + h * 64 + j0 + n]
            lamc = lambda l, j: self.F32S[:, self.LAMT + (h * 12 + l) * 64 + j:
                                          self.LAMT + (h * 12 + l) * 64 + j + 1]
            lamcols = lambda l, j0, n: self.F32S[:, self.LAMT + (h * 12 + l) * 64 + j0:
                                                 self.LAMT + (h * 12 + l) * 64 + j0 + n]

            self.wait("vector", qsem[slot], qs_done[idx])
            self.wait("tensor", self.sV, self.sV.val)
            self.wait("tensor", self.sA, self.sA.val)
            self.mm(PB[6][:, 0:1], self.ONE1[:, :],
                    self.F32S[0:1, self.CGT + h * 64 + 4 * m:
                              self.CGT + h * 64 + 4 * m + 1], inc=[(self.sT, 1)])
            self.wait("vector", self.sT, self.sT.val)
            self.v(lambda e: e.tensor_copy(self.SCL[:, 0:1], PB[6][:, 0:1]))
            self.v(lambda e: e.tensor_scalar(self.SCL[:, 1:2], PB[6][:, 0:1],
                                             -1.0, None, ALU.mult),
                   [(self.sV, 1)])
            scl_done = self.sV.val
            self._pb6_guard = scl_done
            csc, rsc = self.CSC, self.RSC
            self.wait("scalar", self.sV, scl_done)
            if m > 0:
                self.a(lambda e, n=4 * m, src_=cgcols(0, 4 * m): e.activation(
                    self.F32S[:, csc:csc + n], src_,
                    AF.Exp, scale=-1.0, bias=self.SCL[:, 0:1]), [(self.sA, 1)])
            self.a(lambda e, src_=cgcols(4 * m, 4): e.activation(
                self.F32S[:, rsc:rsc + 4], src_,
                AF.Exp, bias=self.SCL[:, 1:2]), [(self.sA, 1)])
            self.wait("vector", self.sA, self.sA.val)
            if m > 0:
                self.v(lambda e, src_=lamcols(11, 4 * m, 4):
                       e.scalar_tensor_tensor(
                    self.F32S[:, rsc + 4:rsc + 8], src_,
                    0.125, self.F32S[:, rsc:rsc + 4], ALU.mult, ALU.mult),
                    [(self.sV, 1)])
            if m % 2 == 1:
                self.v(lambda e, src_=self.F32S[:, self.LAMT + (h * 12 + 10) * 64
                                                + 4 * m:
                                                self.LAMT + (h * 12 + 10) * 64
                                                + 4 * m + 4]:
                       e.scalar_tensor_tensor(
                    self.F32S[:, rsc + 8:rsc + 12], src_,
                    0.125, self.F32S[:, rsc:rsc + 4], ALU.mult, ALU.mult),
                    [(self.sV, 1)])
            rs_done = self.sV.val

            if m > 0:
                self.wait("scalar", self.sV, rs_done)
                rounds = [(self.QF11, rsc + 4)]
                if m % 2 == 1:
                    rounds.append((self.QF10, rsc + 8))
                bct = self.F32S[:, self.MF:self.MF + 128]
                bank_guard = {6: self._pb6_guard, 7: self._pb6_guard}
                rno = 0
                for qreg, sc0 in rounds:
                    for jp in range(2):
                        if rno > 0:
                            self.wait("scalar", self.sT, self.sT.val)
                        for jj in range(2):
                            j = 2 * jp + jj
                            self.a(lambda e, jj=jj,
                                   c=self.F32S[:, sc0 + 2 * jp + jj:
                                               sc0 + 2 * jp + jj + 1]:
                                   e.activation(bct[:, jj * 64:(jj + 1) * 64],
                                                self.ONET[:, 0:64], AF.Copy,
                                                scale=c),
                                   [(self.sA, 1)] if jj == 1 else ())
                        self.wait("tensor", self.sA, self.sA.val)
                        bno = 6 + rno % 2
                        bank = PB[bno]
                        self.wait("tensor", self.sV, bank_guard[bno])
                        self.tr(bank[:, 0:128], bct[:, 0:128], 128,
                                inc=[(self.sT, 1)])
                        self.wait("vector", self.sT, self.sT.val)
                        for jj in range(2):
                            j = 2 * jp + jj
                            self.v(lambda e, j=j, jj=jj, qreg=qreg, bank=bank,
                                   QS=QS, qrow=qrow: e.tensor_mul(
                                self.MISC[qrow:qrow + 64,
                                          qreg + j * 128:qreg + (j + 1) * 128],
                                QS[0:64, j * 128:(j + 1) * 128],
                                bank[jj * 64:(jj + 1) * 64, 0:128]),
                                [(self.sV, 1)] if jj == 1 else ())
                        bank_guard[bno] = self.sV.val
                        rno += 1
            qf_done = self.sV.val

            dd = self.F32S[:, self.DD0:self.DD0 + 512]
            ff = self.F32S[:, self.FF0:self.FF0 + 512]
            if getattr(self, "_prev_av_all", None):
                self.wait("vector", self.sT, self._prev_av_all)
            _val_iq = [None] * 4
            _mask_ops = [(l, iq) for l in range(8) for iq in range(4)]

            def emit_mask_ops(n):
                for _ in range(min(n, len(_mask_ops))):
                    l, iq = _mask_ops.pop(0)
                    a_ = 4 * m + iq
                    tgt = ff[:, iq * 128:(iq + 1) * 128]
                    msk = self.CONSTB[:, l * 128:(l + 1) * 128]
                    lc = lamc(l, a_)
                    if _val_iq[iq] is not None:
                        self.wait("vector", self.sV, _val_iq[iq])
                    if l == 0:
                        self.v(lambda e, t=tgt, m_=msk, lc=lc: e.tensor_scalar(
                            t, m_, lc, None, ALU.mult), [(self.sV, 1)])
                    else:
                        self.v(lambda e, t=tgt, m_=msk, lc=lc:
                               e.scalar_tensor_tensor(
                            t, m_, lc, t, ALU.mult, ALU.add), [(self.sV, 1)])
                    _val_iq[iq] = self.sV.val

            first_av = [True]
            av_cnt = [0]
            n_far = 4 * m
            n_near = {0: 0, 1: 1, 2: 2, 3: 3}
            total_av = n_far + sum(n_near[i] for i in range(4)) + 4

            evac_val = {}
            qk_val = {}
            def scs_slot(k):
                i = k % 8
                c = i * 512 if i < 7 else self.SCS2
                return self.MISC[:, c:c + 512]
            av_done_at = {}

            def emit_av(rhs, j, width, start, stop, col0):
                vap = self.KV[:, self.VOFF + j * 260 + h * 65:
                              self.VOFF + j * 260 + h * 65 + 64]
                self.mm(PB[2][0:64, col0:col0 + width], vap, rhs,
                        start=start, stop=stop, inc=[(self.sT, 1)])

            if y_evac_done:
                self.wait("tensor", self.sV, y_evac_done)
            if m > 0:
                self.wait("tensor", self.sV, qf_done)
            SBANKS = [0, 1, 3, 5]
            nb = (n_far + 3) // 4
            ev_v = {}
            ev_a = {}

            def qk_of(k):
                lvl = 10 if (k // 4) == (m ^ 1) else 11
                qreg = self.QF11 if lvl == 11 else self.QF10
                return (PB[SBANKS[k % 4]],
                        self.KV[:, kpo + k * 128:kpo + (k + 1) * 128],
                        self.MISC[:, qreg:qreg + 512])
            for kb in range(nb):
                ks = list(range(kb * 4, min(kb * 4 + 4, n_far)))
                ks0, ks1 = ks[0:2], ks[2:4]
                if kb >= 1:
                    self.wait("tensor", self.sV, ev_v[kb - 1])
                for k in ks0:
                    bank, st, mv = qk_of(k)
                    self.mm(bank[:, 0:512], st, mv, start=True, stop=True,
                            inc=[(self.sT, 1)] if k == ks0[-1] else ())
                qk0 = self.sT.val
                if kb >= 1:
                    self.wait("tensor", self.sA, ev_a[kb - 1])
                for k in ks1:
                    bank, st, mv = qk_of(k)
                    self.mm(bank[:, 0:512], st, mv, start=True, stop=True,
                            inc=[(self.sT, 1)] if k == ks1[-1] else ())
                qk1 = self.sT.val
                if kb >= 1:
                    for pk in range((kb - 1) * 4, kb * 4):
                        emit_av(scs_slot(pk), pk, 512, first_av[0], False, 0)
                        first_av[0] = False
                self.wait("vector", self.sT, qk0)
                for k in ks0:
                    cs = self.F32S[:, csc + k:csc + k + 1]
                    self.v(lambda e, b=PB[SBANKS[k % 4]], s=scs_slot(k), cs=cs:
                           e.tensor_scalar(s, b[:, 0:512], cs, None, ALU.mult),
                           [(self.sV, 1)])
                ev_v[kb] = self.sV.val
                if ks1:
                    self.wait("scalar", self.sT, qk1)
                    for k in ks1:
                        cs = self.F32S[:, csc + k:csc + k + 1]
                        self.a(lambda e, b=PB[SBANKS[k % 4]], s=scs_slot(k), cs=cs:
                               e.activation(s, b[:, 0:512], AF.Copy, scale=cs),
                               [(self.sA, 1)])
                ev_a[kb] = self.sA.val
                emit_mask_ops(6)
            if nb >= 1:
                self.wait("tensor", self.sV, ev_v[nb - 1])
                self.wait("tensor", self.sA, ev_a[nb - 1])
                for pk in range((nb - 1) * 4, n_far):
                    emit_av(scs_slot(pk), pk, 512, first_av[0], False, 0)
                    first_av[0] = False

            for iq in range(4):
                a = 4 * m + iq
                col0 = iq * 128
                near = []
                if iq == 1:
                    near = [(a - 1, 8)]
                elif iq == 2:
                    near = [(a - 2, 9), (a - 1, 9)]
                elif iq == 3:
                    near = [(a - 3, 9), (a - 2, 9), (a - 1, 8)]
                if near:
                    self.mm(PB[6][:, 1:2], self.ONE1[:, :],
                            self.F32S[0:1, self.CGT + h * 64 + a:
                                      self.CGT + h * 64 + a + 1], inc=[(self.sT, 1)])
                    self.wait("vector", self.sT, self.sT.val)
                    self.v(lambda e: e.tensor_copy(self.SCL[:, 2:3], PB[6][:, 1:2]))
                    self.v(lambda e: e.tensor_scalar(
                        self.SCL[:, 3:4], PB[6][:, 1:2], -1.0, None, ALU.mult),
                        [(self.sV, 1)])
                    self.wait("scalar", self.sV, self.sV.val)
                    self.a(lambda e, src_=cgc(a): e.activation(
                        self.F32S[:, rsc + 12:rsc + 13], src_,
                        AF.Exp, bias=self.SCL[:, 3:4]), [(self.sA, 1)])
                    nn = len(near)
                    self.a(lambda e, nn=nn, src_=cgcols(a - nn, nn): e.activation(
                        self.F32S[:, csc + 60:csc + 60 + nn], src_,
                        AF.Exp, scale=-1.0, bias=self.SCL[:, 2:3]), [(self.sA, 1)])
                    self.wait("vector", self.sA, self.sA.val)
                    lvls = sorted(set(l for _, l in near))
                    for li, l in enumerate(lvls):
                        self.v(lambda e, li=li, src_=lamc(l, a):
                               e.scalar_tensor_tensor(
                            self.F32S[:, rsc + 13 + li:rsc + 14 + li],
                            src_, 0.125, self.F32S[:, rsc + 12:rsc + 13],
                            ALU.mult, ALU.mult), [(self.sV, 1)])
                    self.wait("scalar", self.sV, self.sV.val)
                    bct = self.F32S[:, self.MF:self.MF + 128]
                    for li, l in enumerate(lvls):
                        c0 = li * 64
                        self.a(lambda e, c0=c0, li=li: e.activation(
                            bct[:, c0:c0 + 64], self.ONET[:, 0:64],
                            AF.Copy, scale=self.F32S[:, rsc + 13 + li:rsc + 14 + li]),
                            [(self.sA, 1)] if li == len(lvls) - 1 else ())
                    self.wait("tensor", self.sA, self.sA.val)
                    self.wait("tensor", self.sV, self.sV.val)
                    for li, l in enumerate(lvls):
                        self.tr(PB[6 + li][0:64, 0:128],
                                bct[:, li * 64:li * 64 + 64], 128,
                                inc=[(self.sT, 1)] if li == len(lvls) - 1 else ())
                    self.wait("vector", self.sT, self.sT.val)
                    for li, l in enumerate(lvls):
                        qreg = self.QN8 if l == 8 else self.QN9
                        self.v(lambda e, qreg=qreg, li=li, iq=iq, QS=QS, qrow=qrow:
                               e.tensor_mul(
                            self.MISC[qrow:qrow + 64, qreg:qreg + 128],
                            QS[0:64, iq * 128:(iq + 1) * 128],
                            PB[6 + li][0:64, 0:128]),
                            [(self.sV, 1)] if li == len(lvls) - 1 else ())
                    nq_done_a, nq_done_v = self.sA.val, self.sV.val
                    self.wait("tensor", self.sA, nq_done_a)
                    self.wait("tensor", self.sV, nq_done_v)
                    for ki, (kk, l) in enumerate(near):
                        bank = PB[kk % 2]
                        qreg = self.QN8 if l == 8 else self.QN9
                        self.mm(bank[:, 0:128],
                                self.KV[:, kpo + kk * 128:kpo + (kk + 1) * 128],
                                self.MISC[:, qreg:qreg + 128],
                                start=True, stop=True, inc=[(self.sT, 1)])
                        myqk = self.sT.val
                        self.wait("vector", self.sT, myqk)
                        cs = self.F32S[:, csc + 60 + ki:csc + 61 + ki]
                        sl = scs_slot(ki)
                        self.v(lambda e, b=bank, s=sl, cs=cs: e.tensor_scalar(
                            s[:, 0:128], b[:, 0:128], cs, None, ALU.mult),
                            [(self.sV, 1)])
                        self.wait("tensor", self.sV, self.sV.val)
                        emit_av(sl[:, 0:128], kk, 128, first_av[0], False, col0)
                        first_av[0] = False

            for iq in range(4):
                a = 4 * m + iq
                self.mm(PB[4][:, iq * 128:(iq + 1) * 128],
                        self.KV[kro:kro + 64, kpo + a * 128:kpo + (a + 1) * 128],
                        QS[0:64, iq * 128:(iq + 1) * 128],
                        start=(iq == 0), stop=(iq == 3),
                        inc=[(self.sT, 1)] if iq == 3 else ())
            qk_diag = self.sT.val
            self.wait("scalar", self.sT, self.sT.val)
            for iq in range(4):
                a = 4 * m + iq
                self.a(lambda e, iq=iq, sc=cgc(a): e.activation(
                    dd[:, iq * 128:(iq + 1) * 128], self.ONET[:, 0:128], AF.Copy,
                    scale=sc), [(self.sA, 1)] if iq == 3 else ())
            dd_done = self.sA.val
            self.wait("tensor", self.sA, dd_done)
            for iq in range(4):
                a = 4 * m + iq
                bank = PB[6 + iq % 2]
                self.wait("tensor", self.sV, self.sV.val)
                self.tr(bank[:, 0:128], dd[:, iq * 128:(iq + 1) * 128], 128,
                        inc=[(self.sT, 1)])
                self.wait("vector", self.sT, self.sT.val)
                self.v(lambda e, iq=iq, bank=bank, sc=cgc(a): e.tensor_scalar(
                    dd[:, iq * 128:(iq + 1) * 128], bank[:, 0:128], sc,
                    0.0, ALU.subtract, ALU.max), [(self.sV, 1)])
            self.wait("scalar", self.sV, self.sV.val)
            self.a(lambda e: e.activation(dd[:, 0:512], dd[:, 0:512], AF.Exp,
                                          scale=-1.0), [(self.sA, 1)])
            e2_done = self.sA.val
            emit_mask_ops(32)
            lam_acc_v, lam_acc_g = self.sV.val, self.sG.val
            self.wait("vector", self.sA, e2_done)
            self.wait("vector", self.sG, lam_acc_g)
            self.wait("vector", self.sV, lam_acc_v)
            self.v(lambda e: e.scalar_tensor_tensor(
                ff[:, 0:512], dd[:, 0:512], 0.125, ff[:, 0:512],
                ALU.mult, ALU.mult), [(self.sV, 1)])
            f_done = self.sV.val
            self._prev_f_done = f_done
            self.wait("vector", self.sT, qk_diag)
            sl0 = self.MISC[:, self.SCS0:self.SCS0 + 512]
            self.v(lambda e, sl0=sl0: e.tensor_copy(sl0, PB[4][:, 0:512]),
                   [(self.sV, 1)])
            scd_done = self.sV.val
            self.wait("tensor", self.sV, f_done)
            self.wait("vector", self.sV, scd_done)
            wd = self.MISC[:, self.SCS2:self.SCS2 + 512]
            for iq in range(4):
                bank = PB[6 + iq % 2]
                if iq >= 2:
                    self.wait("tensor", self.sV, self.sV.val)
                self.tr(bank[:, 0:128], ff[:, iq * 128:(iq + 1) * 128], 128,
                        inc=[(self.sT, 1)])
                self.wait("vector", self.sT, self.sT.val)
                self.v(lambda e, iq=iq, bank=bank, sl0=sl0, wd=wd: e.tensor_mul(
                    wd[:, iq * 128:(iq + 1) * 128], sl0[:, iq * 128:(iq + 1) * 128],
                    bank[:, 0:128]), [(self.sV, 1)])
                self.wait("tensor", self.sV, self.sV.val)
                emit_av(wd[:, iq * 128:(iq + 1) * 128], 4 * m + iq, 128,
                        first_av[0], iq == 3, iq * 128)
                first_av[0] = False
            av_all = self.sT.val
            self._prev_av_all = av_all
            if "p2d" in self.dbg and (h, m) == (1, 15):
                self.wait("sync", self.sT, av_all)
                self.wait("sync", self.sV, self.sV.val)
                self.tap_sbuf("dd", dd, (128, 512), F32)
                self.tap_sbuf("ff", ff, (128, 512), F32)
                self.tap_sbuf("sl0", sl0, (128, 512), BF16)
                self.tap_sbuf("wd", wd, (128, 512), BF16)
                self.tap_sbuf("qsd", self.QSTG[:, slot * 512:(slot + 1) * 512],
                              (128, 512), BF16)
                self.tap_sbuf("qf11", self.MISC[:, self.QF11:self.QF11 + 512],
                              (128, 512), BF16)
                self.tap_sbuf("qf10", self.MISC[:, self.QF10:self.QF10 + 512],
                              (128, 512), BF16)
                self.tap_sbuf("rsc", self.F32S[:, self.RSC:self.RSC + 12],
                              (128, 12), F32)
                self.tap_sbuf("csc", self.F32S[:, self.CSC:self.CSC + 60],
                              (128, 60), F32)
                self.wait("gpsimd", self.sDs, self.sDs.val)
                self.wait("vector", self.sDs, self.sDs.val)
                self.wait("scalar", self.sDs, self.sDs.val)
                self.wait("tensor", self.sDs, self.sDs.val)
            self.wait("vector", self.sT, av_all)
            if y_evac_done:
                self.wait("vector", self.sDc, self.y_dma_done)
            self.v(lambda e: e.tensor_copy(self.Y2[0:64, 0:512], PB[2][0:64, 0:512]),
                   [(self.sV, 1)])
            y_evac_done = self.sV.val
            self.wait("scalar", self.sV, y_evac_done)
            self.dma("scalar", self.Ya_d[h * 64:(h + 1) * 64, m * 512:(m + 1) * 512],
                     self.Y2[0:64, 0:512], self.sDc)
            self.y_dma_done = self.sDc.val
            self._qs_guard[slot] = (self.sT.val, self.sV.val)
            if m == NG - 1:
                self.wait("gpsimd", self.sDc, self.y_dma_done)
                self.on("gpsimd", lambda e, h=h: e.collective_compute(
                    "AllGather", ALU.bypass,
                    replica_groups=[[0, 1, 2, 3], [4, 5, 6, 7]],
                    ins=[self.Ya_d[h * 64:(h + 1) * 64, :].opt()],
                    outs=[self.Yg_d[h * 256:(h + 1) * 256, :].opt()]),
                    [(self.sCC, 1)])
                self.yg_head_done[h] = self.sCC.val
        self.yg_ready = self.sCC.val
        self.p2_y_evac_done = y_evac_done
        if "p2" in self.dbg:
            self.wait("sync", self.sDc, self.y_dma_done)
            self.tap_dram("Ya", self.Ya_d, (256, T), BF16)

    def sEng(self, eng):
        return {"vector": self.sV, "scalar": self.sA, "tensor": self.sT,
                "gpsimd": self.sG}[eng]

    def p3_collect_y(self, src, dst):
        self.wait("gpsimd", self.sDc, self.y_dma_done)
        self.on("gpsimd", lambda e, s=src, d=dst: e.collective_compute(
            "AllGather", ALU.bypass,
            replica_groups=[[0, 1, 2, 3], [4, 5, 6, 7]],
            ins=[s[:, :].opt()], outs=[d[:, :].opt()]), [(self.sCC, 1)])
        self.yg_ready = self.sCC.val

    def emit_wcache(self, wdram):
        casts = []
        for ch in range(8):
            for ot in range(4):
                casts.append((wdram[ch * 128:(ch + 1) * 128,
                                    ot * 256:(ot + 1) * 256],
                              (ch * 8 + ot * 2) * 128, 256))
        self.emit_casts(casts, self.KV)

    def p4_close_attn(self, kind):
        PB, W = self.PB, TGW
        xlb = lambda ch: self.MISC[:, ch * W:(ch + 1) * W]
        yst = lambda ch: self.MISC[:, 2048 + ch * W:2048 + (ch + 1) * W]
        obuf = lambda ch: self.F32S[:, ch * W:(ch + 1) * W]
        wt = lambda ch, ot: self.KV[:, (ch * 8 + ot) * 128:(ch * 8 + ot) * 128 + 128]
        tmpt = self.STG2[:, 0:W]

        self.wait("gpsimd", self.sV, self.sV.val)
        self.wait("gpsimd", self.sT, self.sT.val)
        self.emit_wcache(self.i["aWoutT"] if kind == "a" else self.i["bWoT"])
        wc_done = self.wcast_g

        qslot = {"v": None}

        yo_done = {}
        for hh in range(4):
            self.wait("sync", self.sCC, self.yg_head_done[hh])
            self.on("sync", lambda e, hh=hh: e.dma_start(
                out=self.Yo_d[hh * 256:(hh + 1) * 256, :],
                in_=self.Yg_d[hh * 256:(hh + 1) * 256,
                              bass.ts(self._q_sync, TOWN)]), [(self.sDs, 16)])
            yo_done[hh] = self.sDs.val
        self.wait("sync", self.sDs, self.sDs.val)

        def dyn_yg(e, ch, og):
            return e.dma_start(out=yst(ch),
                               in_=self.Yo_d[ch * 128:(ch + 1) * 128,
                                             og * W:(og + 1) * W])
        self.wait("tensor", self.sG, wc_done)
        prev_mm = 0
        for og in range(8):
            if kind == "a":
                if og > 0:
                    self.wait("tensor", self.sV, self.sV.val)
                    self.wait("vector", self.sT, prev_mm)
                self.emit_ln(self.emit_ln_pairs_from_R(og),
                             lambda pi: self.MISC[:, pi * 512:(pi + 1) * 512])
            if og > 0:
                self.wait("sync", self.sT, prev_mm)
            for ch in range(8):
                self.on("sync", lambda e, ch=ch, og=og: dyn_yg(e, ch, og),
                        [(self.sDs, 16)])
            yg_ld = self.sDs.val
            self.wait("sync", self.sDs, yg_ld)
            self.wait("tensor", self.sDs, yg_ld)
            if og > 0:
                self.wait("tensor", self.sV, self.sV.val)
            for oh in range(2):
                if oh == 1:
                    self.wait("tensor", self.sV, self.sV.val)
                for ch in range(8):
                    for ot in range(4):
                        self.mm(PB[ot][:, 0:W], wt(ch, oh * 4 + ot), yst(ch),
                                start=(ch == 0), stop=(ch == 7),
                                inc=[(self.sT, 1)] if (ch == 7 and ot == 3) else ())
                mmv = self.sT.val
                self.wait("vector", self.sT, mmv)
                if oh == 0:
                    self.wait("vector", self.sA, self.sA.val)
                    self.wait("vector", self.sV, self.sV.val)
                for ot in range(4):
                    och = oh * 4 + ot
                    if kind == "a":
                        self.v(lambda e, ot=ot, och=och: e.tensor_add(
                            obuf(och), PB[ot][:, 0:W], xlb(och)),
                            [(self.sV, 1)])
                    else:
                        self.wait("vector", self.sV, self.sV.val)
                        self.v(lambda e, ot=ot, och=och: e.tensor_scalar(
                            tmpt, PB[ot][:, 0:W], self.BBS[:, 15:16],
                            self.BBS[:, 4 + och:5 + och], ALU.mult, ALU.add),
                            [(self.sV, 1)])
                        self.wait("vector", self.sV, self.sV.val)
                        self.v(lambda e, och=och, og=og: e.tensor_add(
                            self.R[:, och * TOWN + og * W:
                                   och * TOWN + (og + 1) * W],
                            self.R[:, och * TOWN + og * W:
                                   och * TOWN + (og + 1) * W], tmpt),
                            [(self.sV, 1)])
            prev_mm = self.sT.val
            if kind == "a":
                ob_done = self.sV.val
                self.wait("tensor", self.sV, ob_done)
                self.wait("scalar", self.sV, ob_done)
                self.emit_ln_stats_only(
                    lambda pi: self.F32S[:, pi * 512:(pi + 1) * 512])
                r3 = self.R[:, :].rearrange("p (c t) -> p c t", c=8)
                p6f = PB[6][:, 0:512].rearrange("p (c t) -> p c t", c=2)
                p7f = PB[7][:, 0:512].rearrange("p (c t) -> p c t", c=2)
                tp2 = self.STG2[:, 0:512]
                tp3 = tp2.rearrange("p (c t) -> p c t", c=2)
                for pi in range(4):
                    if pi > 0:
                        self.wait("vector", self.sV, self.sV.val)
                    self.v(lambda e, pi=pi, tp3=tp3: e.tensor_mul(
                        tp3, self.F32S[:, pi * 512:(pi + 1) * 512]
                        .rearrange("p (c t) -> p c t", c=2), p6f), [(self.sV, 1)])
                    self.wait("vector", self.sV, self.sV.val)
                    self.v(lambda e, tp3=tp3: e.tensor_sub(tp3, tp3, p7f),
                           [(self.sV, 1)])
                    self.wait("vector", self.sV, self.sV.val)
                    for half in range(2):
                        ch = pi * 2 + half
                        self.v(lambda e, ch=ch, half=half, tp2=tp2: e.tensor_scalar(
                            tp2[:, half * W:(half + 1) * W],
                            tp2[:, half * W:(half + 1) * W],
                            self.LNG[:, ch:ch + 1],
                            self.LNG[:, 8 + ch:9 + ch], ALU.mult, ALU.add),
                            [(self.sV, 1)] if half == 1 else ())
                    self.wait("vector", self.sV, self.sV.val)
                    self.v(lambda e, pi=pi, og=og, tp3=tp3, r3=r3: e.tensor_add(
                        r3[:, 2 * pi:2 * pi + 2, og * W:(og + 1) * W],
                        r3[:, 2 * pi:2 * pi + 2, og * W:(og + 1) * W],
                        tp3), [(self.sV, 1)])
        self.close_done = self.sV.val

    def emit_ln_stats_only(self, src_pair):
        self.emit_ln(src_pair, None, affine="stats")

    def p5_ffn(self, idx):
        PB, W = self.PB, TGW
        i = self.i
        xlb = lambda ch: self.MISC[:, ch * W:(ch + 1) * W]

        if idx == 0:
            self.crl_d = self.nc.dram_tensor("crl_d", [16, 16384], BF16,
                                             kind="Internal").ap()
            for h in range(4):
                for rb in range(8):
                    r0 = rb * 8
                    self.wait("gpsimd", self.sV, self.sV.val)
                    self.dma("gpsimd",
                             self.STG[0:16, 0:128].rearrange("p (r i) -> p r i", r=8),
                             i["makerT"][h, r0:r0 + 8].rearrange("r j i -> j r i"),
                             self.sDg)
                    self.dma("gpsimd",
                             self.STG2[0:16, 0:512].rearrange("p (r k) -> p r k", r=8),
                             i["upT"][h, r0:r0 + 8].rearrange("r j k -> j r k"),
                             self.sDg)
                    dgv = self.sDg.val
                    self.wait("tensor", self.sDg, dgv)
                    if h + rb > 0:
                        self.wait("tensor", self.sV, self.sV.val)
                    for ri in range(8):
                        self.mm(PB[0][0:16, ri * 64:(ri + 1) * 64],
                                self.STG[0:16, ri * 16:(ri + 1) * 16],
                                self.STG2[0:16, ri * 64:(ri + 1) * 64],
                                start=(ri == 0), stop=(ri == 7),
                                inc=[(self.sT, 1)] if ri == 7 else ())
                    self.wait("vector", self.sT, self.sT.val)
                    if h + rb > 0:
                        self.wait("vector", self.sDc, self.sDc.val)
                    self.v(lambda e: e.tensor_copy(self.Y2[0:16, 0:512],
                                                   PB[0][0:16, 0:512]),
                           [(self.sV, 1)])
                    self.wait("scalar", self.sV, self.sV.val)
                    crl3 = self.crl_d.rearrange("p (c x) -> p c x", x=128)
                    c0_ = r0 * 2 + h // 2
                    dst = crl3[0:16, c0_:c0_ + 15:2,
                               (h % 2) * 64:(h % 2) * 64 + 64]
                    self.on("scalar", lambda e, dst=dst: e.dma_start(
                        out=dst, in_=self.Y2[0:16, 0:512]
                        .rearrange("p (r c) -> p r c", r=8)), [(self.sDc, 16)])
            self.crl_ready = self.sDc.val

        self.wait("tensor", self.sV, self.sV.val)
        self.wait("scalar", self.sV, self.sV.val)
        x2dma = 0
        for og in range(8):
            if og > 0:
                self.wait("tensor", self.sV, self.sV.val)
                self.wait("vector", self.sDs, x2dma)
            self.emit_ln(self.emit_ln_pairs_from_R(og),
                         lambda pi: self.MISC[:, pi * 512:(pi + 1) * 512])
            self.wait("sync", self.sV, self.sV.val)
            for ch in range(8):
                self.dma("sync", self.x2_d[ch * 128:(ch + 1) * 128,
                                           og * W:(og + 1) * W], xlb(ch), self.sDs)
            x2dma = self.sDs.val
            self.wait("sync", self.sDs, x2dma)
        x2_done = self.sDs.val

        self.wait("gpsimd", self.sT, self.sT.val)
        self.wait("gpsimd", self.sDs, x2_done)
        self.wait("gpsimd", self.sV, self.sV.val)
        self.wait("scalar", self.sDc, self.sDc.val)
        xr_slots = [self.XR, self.QSTG[:, 0:2048]]
        xr_sems = [self.sQ0, self.sQ1]
        xr_ld = {}
        x2r = self.x2_d.rearrange("(r i) t -> i r t", i=16)

        def xr_dma(it, guard):
            sp_, g_ = it // 8, it % 8
            slot_ = it % 2
            if guard is not None:
                self.wait("sync", self.sT, guard)
            for half in range(2):
                self.dma("sync",
                         xr_slots[slot_][0:16, half * 1024:(half + 1) * 1024],
                         x2r[0:16, sp_ * 8 + half * 4:sp_ * 8 + (half + 1) * 4,
                             g_ * W:(g_ + 1) * W], xr_sems[slot_])
            xr_ld[it] = xr_sems[slot_].val
        prev_rails = 0
        prev_radd = 0
        prev_gelu = [0, 0]
        for sp in range(8):
            slot = sp % 2
            if sp >= 2:
                self.wait("gpsimd", self.sT, self._ring_used[slot])
            dst = self.KV[:, slot * 16384:(slot + 1) * 16384] \
                .rearrange("p (c o) -> p c o", c=16)
            srcw = self.wp_bf[sp * 2048:(sp + 1) * 2048, :] \
                .rearrange("(c p) o -> p c o", p=128)
            if sp == 0:
                self.wait("gpsimd", self.sG, self.sG.val)
            self.dma("gpsimd", dst, srcw, self.sDg)
            ring_ld = self.sDg.val
            if sp > 0:
                self.wait("gpsimd", self.sT, prev_rails)
            if sp == 0 and idx == 0:
                self.wait("gpsimd", self.sDc, self.crl_ready)
            self.dma("gpsimd", self.CRL[0:16, 0:2048],
                     self.crl_d[0:16, sp * 2048:(sp + 1) * 2048], self.sDg)
            crl_ld = self.sDg.val
            self.wait("gpsimd", self.sDg, crl_ld)
            for g in range(8):
                it = sp * 8 + g
                slot_x = it % 2
                if it == 0:
                    xr_dma(0, self.sT.val)
                    xr_dma(1, None)
                elif it + 1 < 64:
                    xr_dma(it + 1, prev_rails)
                self.wait("tensor", xr_sems[slot_x], xr_ld[it])
                if g == 0:
                    self.wait("tensor", self.sDg, crl_ld)
                gelu_of = {}

                def emit_rails(c16):
                    zb = PB[c16 % 2]
                    if not (sp == 0 and g == 0 and c16 < 2):
                        self.wait("tensor", self.sA, prev_gelu[c16 % 2])
                    self.mm(zb[:, 0:W], self.CRL[0:16, c16 * 128:(c16 + 1) * 128],
                            xr_slots[slot_x][0:16, (c16 // 2) * 256:
                                             (c16 // 2) * 256 + 256],
                            start=True, stop=True, inc=[(self.sT, 1)])
                    self.wait("scalar", self.sT, self.sT.val)
                    hch = self.HCH[:, (c16 % 2) * 256:(c16 % 2) * 256 + 256]
                    self.a(lambda e, zb=zb, hch=hch: e.activation(
                        hch, zb[:, 0:W], AF.Gelu), [(self.sA, 1)])
                    prev_gelu[c16 % 2] = self.sA.val
                    gelu_of[c16] = (self.sA.val, hch)

                def emit_cproj(c16):
                    gv, hch = gelu_of[c16]
                    self.wait("tensor", self.sA, gv)
                    if c16 == 0 and not (sp == 0 and g == 0):
                        self.wait("tensor", self.sV, prev_radd)
                    for ot in range(8):
                        bank = PB[4 + ot // 2]
                        cols = (ot % 2) * 256
                        st_flag = (c16 == 0 and ot % 2 == 0)
                        sp_flag = (c16 == 15 and ot % 2 == 1)
                        self.mm(bank[:, cols:cols + W],
                                self.KV[:, slot * 16384 + c16 * 1024 + ot * 128:
                                        slot * 16384 + c16 * 1024 + (ot + 1) * 128],
                                hch, start=st_flag, stop=sp_flag,
                                inc=[(self.sT, 1)] if (c16 == 15 and ot == 7) else ())

                emit_rails(0)
                for c16 in range(16):
                    if c16 + 1 < 16:
                        emit_rails(c16 + 1)
                    emit_cproj(c16)
                prev_rails = self.sT.val
                self._ring_used = getattr(self, "_ring_used", [0, 0])
                self._ring_used[slot] = self.sT.val
                self.wait("vector", self.sT, self.sT.val)
                r3 = self.R[:, :].rearrange("p (c t) -> p c t", c=8)
                for pi in range(4):
                    bank3 = PB[4 + pi][:, 0:512].rearrange("p (c t) -> p c t", c=2)
                    self.v(lambda e, pi=pi, g=g, bank3=bank3, r3=r3: e.tensor_add(
                        r3[:, 2 * pi:2 * pi + 2, g * W:(g + 1) * W],
                        r3[:, 2 * pi:2 * pi + 2, g * W:(g + 1) * W], bank3),
                        [(self.sV, 1)] if pi == 3 else ())
                prev_radd = self.sV.val

    def p6_attn_b(self):
        PB, W = self.PB, TGW
        i = self.i
        xlb = lambda ch: self.MISC[:, ch * W:(ch + 1) * W]
        wsb = self.WSB

        self.wait("tensor", self.sV, self.sV.val)
        self.wait("scalar", self.sV, self.sV.val)
        c3dma = 0
        for og in range(8):
            if og > 0:
                self.wait("tensor", self.sV, self.sV.val)
                self.wait("vector", self.sDs, c3dma)
            self.emit_ln(self.emit_ln_pairs_from_R(og),
                         lambda pi: self.MISC[:, pi * 512:(pi + 1) * 512])
            self.wait("sync", self.sV, self.sV.val)
            for ch in range(8):
                self.dma("sync", self.c3_d[ch * 128:(ch + 1) * 128,
                                           og * W:(og + 1) * W], xlb(ch), self.sDs)
            c3dma = self.sDs.val
            self.wait("sync", self.sDs, c3dma)
        c3_done = self.sDs.val
        self.wait("gpsimd", self.sDs, c3_done)
        self.on("gpsimd", lambda e: e.collective_compute(
            "AllGather", ALU.bypass,
            replica_groups=[[0, 1, 2, 3], [4, 5, 6, 7]],
            ins=[self.c3_d[:, :].opt()], outs=[self.xg3_d[:, :].opt()]),
            [(self.sCC, 1)])
        xg3_ready = self.sCC.val

        casts = []
        for ch in range(8):
            casts += [(i["bWqT"][ch * 128:(ch + 1) * 128, :], self.wq_off + ch * 256, 256),
                      (i["bWkT"][ch * 128:(ch + 1) * 128, :], self.wk_off + ch * 256, 256),
                      (i["bWvT"][ch * 128:(ch + 1) * 128, :], self.wv_off + ch * 256, 256)]
        self.wait("gpsimd", self.sT, self.sT.val)
        self.emit_casts(casts, self.WSB)
        wb_done = self.wcast_g
        self.dma("sync", self.BBS[:, 0:2], i["bb_q"][:, :], self.sDs)
        self.dma("sync", self.BBS[:, 2:4], i["bb_k"][:, :], self.sDs)
        self.dma("sync", self.BBS[:, 4:12], i["bb_o"][:, :], self.sDs)
        self.dma("sync", self.STG[0:1, 0:1], i["bs_q"][:, :], self.sDs)
        self.dma("sync", self.STG[0:1, 1:2], i["bs_k"][:, :], self.sDs)
        self.dma("sync", self.STG[0:1, 2:3], i["bs_v"][:, :], self.sDs)
        self.dma("sync", self.STG[0:1, 3:4], i["bs_o"][:, :], self.sDs)
        self.dma("sync", self.STG[0:1, 4:260], i["bb_v"][:, :], self.sDs)
        sc_ld = self.sDs.val
        self.wait("sync", self.sDs, sc_ld)
        self.wait("tensor", self.sDs, sc_ld)
        self.wait("tensor", self.sV, self.sV.val)
        for k in range(4):
            self.mm(PB[0][:, k:k + 1], self.ONE1[:, :], self.STG[0:1, k:k + 1],
                    start=(k == 0), stop=(k == 3))
        self.mm(PB[1][:, 0:256], self.ONE1[:, :], self.STG[0:1, 4:260],
                inc=[(self.sT, 1)])
        self.wait("vector", self.sT, self.sT.val)
        self.v(lambda e: e.tensor_copy(self.BBS[:, 12:16], PB[0][:, 0:4]))
        self.v(lambda e: e.tensor_copy(self.BVB[:, 0:256], PB[1][:, 0:256]),
               [(self.sV, 1)])
        sc_done = self.sV.val

        self.v(lambda e: e.memset(
            self.KV[:, self.VOFF + 64:self.VOFF + 16640:65], 1.0), [(self.sV, 1)])

        self.wait("sync", self.sCC, xg3_ready)
        self.wait("tensor", self.sV, sc_done)
        self.wait("tensor", self.sG, wb_done)
        proj_done = {}
        v_evac_done = 0
        xlb2 = lambda s, ch: self.MISC[:, s * 2048 + ch * W:s * 2048 + (ch + 1) * W]
        xsem = [self.sQ0, self.sQ1]
        xg_ld = {}

        def xg_dma(tg_):
            s = tg_ % 2
            rblk, ccol = (tg_ // 8) * 1024, (tg_ % 8) * W
            if tg_ >= 2:
                self.wait("sync", self.sT, proj_done[tg_ - 2])
            for ch in range(8):
                self.dma("sync", xlb2(s, ch),
                         self.xg3_d[rblk + ch * 128:rblk + (ch + 1) * 128,
                                    ccol:ccol + W], xsem[s])
            xg_ld[tg_] = xsem[s].val
        xg_dma(0)
        xg_dma(1)
        for tg in range(NTG):
            s = tg % 2
            self.wait("tensor", xsem[s], xg_ld[tg])
            if tg > 0:
                self.wait("tensor", self.sV, v_evac_done)
            for ch in range(8):
                for p in range(2):
                    self.mm(PB[0 + p][:, 0:W],
                            wsb[:, self.wq_off + ch * 256 + p * 128:
                                self.wq_off + ch * 256 + (p + 1) * 128],
                            xlb2(s, ch), start=(ch == 0), stop=(ch == 7))
                    self.mm(PB[2 + p][:, 0:W],
                            wsb[:, self.wk_off + ch * 256 + p * 128:
                                self.wk_off + ch * 256 + (p + 1) * 128],
                            xlb2(s, ch), start=(ch == 0), stop=(ch == 7))
                for sub in range(2):
                    self.mm(PB[6 + sub][:, 0:256],
                            xlb2(s, ch)[:, sub * 128:(sub + 1) * 128],
                            wsb[:, self.wv_off + ch * 256:self.wv_off + (ch + 1) * 256],
                            start=(ch == 0), stop=(ch == 7),
                            inc=[(self.sT, 1)] if (ch == 7 and sub == 1) else ())
            proj_done[tg] = self.sT.val
            if tg + 2 < NTG:
                xg_dma(tg + 2)
            self.wait("vector", self.sT, proj_done[tg])
            self.wait("vector", self.sDs, self.sDs.val)
            for p in range(2):
                self.v(lambda e, p=p, tg=tg: e.tensor_scalar(
                    self.KV[:, p * T + tg * W:p * T + (tg + 1) * W],
                    PB[2 + p][:, 0:W], self.BBS[:, 13:14],
                    self.BBS[:, 2 + p:3 + p], ALU.mult, ALU.add))
                self.v(lambda e, p=p: e.tensor_scalar(
                    self.QSTG[:, p * W:(p + 1) * W], PB[0 + p][:, 0:W],
                    self.BBS[:, 12:13], self.BBS[:, 0 + p:1 + p],
                    ALU.mult, ALU.add),
                    [(self.sV, 1)] if p == 1 else ())
            qk_ev = self.sV.val
            self.wait("sync", self.sV, qk_ev)
            for p in range(2):
                self.dma("sync", self.Qa_d[p][:, tg * W:(tg + 1) * W],
                         self.QSTG[:, p * W:(p + 1) * W], self.sDs)
            self.q_dma_done = self.sDs.val
            for sub in range(2):
                j = tg * 2 + sub
                dst = self.KV[:, self.VOFF + j * 260:self.VOFF + (j + 1) * 260]
                dst3 = dst.rearrange("p (h c) -> p h c", h=4)[:, :, 0:64]
                src3 = PB[6 + sub][:, 0:256].rearrange("p (h c) -> p h c", h=4)
                bvb3 = self.BVB[:, 0:256].rearrange("p (h c) -> p h c", h=4)
                self.v(lambda e, d=dst3, s=src3, b=bvb3: e.scalar_tensor_tensor(
                    d, s, self.BBS[:, 14:15], b, ALU.mult, ALU.add),
                    [(self.sV, 1)] if sub == 1 else ())
            v_evac_done = self.sV.val

        self.wait("tensor", self.sV, self.sV.val)
        self.wait("sync", self.sDs, self.sDs.val)
        self.wait("vector", self.sDs, self.q_dma_done)
        self.v(lambda e: e.memset(self.QSTG[:, 0:4096], 0.0), [(self.sV, 1)])
        qz = self.sV.val
        SBK = [0, 1, 3, 5]
        YBK = [2, 4, 6, 7]
        hmg = [(h, mg) for h in range(4) for mg in range(4)]
        self.yg_head_done = {}
        qsem = [self.sQ0, self.sQ1]
        qs_guard = [0, 0]

        def qs_dma(idx2):
            h, mg = hmg[idx2]
            slot = idx2 % 2
            qrow = (h % 2) * 64
            self.wait("sync", self.sT, qs_guard[slot])
            self.dma("sync",
                     self.QSTG[qrow:qrow + 64, slot * 2048:(slot + 1) * 2048],
                     self.Qa_d[h // 2][qrow:qrow + 64, mg * 2048:(mg + 1) * 2048],
                     qsem[slot])
            return qsem[slot].val
        self.wait("sync", self.sV, qz)
        qs_done = {0: qs_dma(0)}
        self.wait("tensor", self.sV, qz)
        fin_v = 0
        scs = lambda k, mq: self.MISC[:, ((k % 2) * 4 + mq) * 512:
                                      ((k % 2) * 4 + mq) * 512 + 512]
        for idx2, (h, mg) in enumerate(hmg):
            slot = idx2 % 2
            qrow = (h % 2) * 64
            kpo = (h // 2) * T
            if idx2 + 1 < len(hmg):
                qs_done[idx2 + 1] = qs_dma(idx2 + 1)
            if mg == 0 and h > 0:
                qrow_old = ((h - 1) % 2) * 64
                self.wait("vector", self.sT, self.sT.val)
                self.v(lambda e, r=qrow_old: e.memset(
                    self.QSTG[r:r + 64, 0:4096], 0.0), [(self.sV, 1)])
                fin_v = self.sV.val
            QSL = self.QSTG[:, slot * 2048:(slot + 1) * 2048]
            self.wait("tensor", qsem[slot], qs_done[idx2])
            if fin_v:
                self.wait("tensor", self.sV, fin_v)
            evA = {}
            evB = {}
            for k in range(NT):
                if k > 0:
                    self.wait("tensor", self.sA, evA[k - 1])
                for mq in range(2):
                    self.mm(PB[SBK[mq]][:, 0:512],
                            self.KV[:, kpo + k * 128:kpo + (k + 1) * 128],
                            QSL[:, mq * 512:(mq + 1) * 512],
                            start=True, stop=True,
                            inc=[(self.sT, 1)] if mq == 1 else ())
                qka = self.sT.val
                if k > 0:
                    self.wait("tensor", self.sA, evB[k - 1])
                for mq in range(2, 4):
                    self.mm(PB[SBK[mq]][:, 0:512],
                            self.KV[:, kpo + k * 128:kpo + (k + 1) * 128],
                            QSL[:, mq * 512:(mq + 1) * 512],
                            start=True, stop=True,
                            inc=[(self.sT, 1)] if mq == 3 else ())
                qkb = self.sT.val
                if k > 0:
                    kp = k - 1
                    for mq in range(4):
                        self.mm(PB[YBK[mq]][0:65, 0:512],
                                self.KV[:, self.VOFF + kp * 260 + h * 65:
                                        self.VOFF + kp * 260 + h * 65 + 65],
                                scs(kp, mq), start=(kp == 0), stop=False)
                self.wait("scalar", self.sT, qka)
                for mq in range(2):
                    self.a(lambda e, b=PB[SBK[mq]], s=scs(k, mq):
                           e.activation(s, b[:, 0:512], AF.Exp, scale=0.125),
                           [(self.sA, 1)] if mq == 1 else ())
                evA[k] = self.sA.val
                self.wait("scalar", self.sT, qkb)
                for mq in range(2, 4):
                    self.a(lambda e, b=PB[SBK[mq]], s=scs(k, mq):
                           e.activation(s, b[:, 0:512], AF.Exp, scale=0.125),
                           [(self.sA, 1)] if mq == 3 else ())
                evB[k] = self.sA.val
            self.wait("tensor", self.sA, evB[NT - 1])
            for mq in range(4):
                self.mm(PB[YBK[mq]][0:65, 0:512],
                        self.KV[:, self.VOFF + (NT - 1) * 260 + h * 65:
                                self.VOFF + (NT - 1) * 260 + h * 65 + 65],
                        scs(NT - 1, mq), start=False, stop=True,
                        inc=[(self.sT, 1)] if mq == 3 else ())
            av_all = self.sT.val
            qs_guard[slot] = av_all
            ybuf = [self.Y2[0:64, 0:512], self.Y2[0:64, 512:1024],
                    self.HCH[0:64, 0:512], self.WPC[0:64, 0:512]]
            self.wait("vector", self.sT, av_all)
            self.wait("vector", self.sDc, self.sDc.val)
            for mq in range(4):
                self.v(lambda e, mq=mq: e.reciprocal(
                    self.F32S[0:1, mq * 512:(mq + 1) * 512],
                    PB[YBK[mq]][64:65, 0:512]),
                    [(self.sV, 1)] if mq == 3 else ())
            rec_done = self.sV.val
            self.wait("tensor", self.sV, rec_done)
            for mq in range(4):
                self.mm(PB[SBK[mq]][0:64, 0:512], self.ONE1[0:1, 0:64],
                        self.F32S[0:1, mq * 512:(mq + 1) * 512],
                        inc=[(self.sT, 1)] if mq == 3 else ())
            bc_done = self.sT.val
            self.wait("vector", self.sT, bc_done)
            cp_val = {}
            for mq in range(4):
                self.v(lambda e, mq=mq: e.tensor_copy(
                    self.F32S[0:64, 2048 + mq * 512:2048 + (mq + 1) * 512],
                    PB[SBK[mq]][0:64, 0:512]), [(self.sV, 1)])
                cp_val[mq] = self.sV.val
            for mq in range(4):
                self.wait("vector", self.sV, cp_val[mq])
                self.v(lambda e, mq=mq: e.tensor_mul(
                    ybuf[mq], PB[YBK[mq]][0:64, 0:512],
                    self.F32S[0:64, 2048 + mq * 512:2048 + (mq + 1) * 512]),
                    [(self.sV, 1)])
                mv = self.sV.val
                self.wait("scalar", self.sV, mv)
                self.dma("scalar",
                         self.Ya_d[h * 64:(h + 1) * 64,
                                   mg * 2048 + mq * 512:mg * 2048 + (mq + 1) * 512],
                         ybuf[mq], self.sDc)
            fin_v = self.sV.val
            self.y_dma_done = self.sDc.val
            if mg == 3:
                self.wait("gpsimd", self.sDc, self.y_dma_done)
                self.on("gpsimd", lambda e, h=h: e.collective_compute(
                    "AllGather", ALU.bypass,
                    replica_groups=[[0, 1, 2, 3], [4, 5, 6, 7]],
                    ins=[self.Ya_d[h * 64:(h + 1) * 64, :].opt()],
                    outs=[self.Yg_d[h * 256:(h + 1) * 256, :].opt()]),
                    [(self.sCC, 1)])
                self.yg_head_done[h] = self.sCC.val
        self.yg_ready = self.sCC.val
        if "p6" in self.dbg:
            self.wait("sync", self.sDc, self.y_dma_done)
            self.tap_dram("Yb", self.Ya_d, (256, T), BF16)
        self.p4_close_attn("b")

    def p7_out(self):
        self.wait("sync", self.sV, self.sV.val)
        self.wait("sync", self.sT, self.sT.val)
        self.wait("sync", self.sA, self.sA.val)
        for ch in range(8):
            self.dma("sync", self.out_d[ch * 128:(ch + 1) * 128, :],
                     self.R[:, ch * TOWN:(ch + 1) * TOWN], self.sDs)
        self.wait("sync", self.sDs, self.sDs.val)


def build_nc(phases=99, dbg=()):
    nc = bass.Bass("TRN2", target_bir_lowering=False, debug=False)
    mk = MK(nc, phases=phases, dbg=dbg)
    mk.build()
    return nc, mk





def kernel(**inputs):
    consts = make_consts()
    in_maps = []
    for c in range(8):
        m = host_prep(inputs, c)
        m.update(consts)
        in_maps.append(m)
    nc, _ = build_nc(phases=8)
    from concourse.bass_utils import run_bass_kernel_spmd
    res = run_bass_kernel_spmd(nc, in_maps, core_ids=list(range(8)))
    out = np.empty((2, T, D), dtype=np.float32)
    for c in range(8):
        b, q = c // 4, c % 4
        out[b, q * TOWN:(q + 1) * TOWN, :] = np.asarray(res.results[c]["out"]).T
    return out


# version 09
# speedup vs baseline: 1.1524x; 1.1524x over previous
import math
import numpy as np
from contextlib import ExitStack

import concourse.bass as bass
import concourse.mybir as mybir

F32 = mybir.dt.float32
BF16 = mybir.dt.bfloat16
AF = mybir.ActivationFunctionType
ALU = mybir.AluOpType

T = 8192
D = 1024
NT = 64
NG = 16
TOWN = 2048
TGW = 256
NTG = T // TGW
S = 64
LV = 12
LN8 = math.log(0.125)


def host_prep(inp, core):
    b, q = core // 4, core % 4
    hs = slice(4 * q * S, (4 * q + 4) * S)
    hl = slice(4 * q * LV, (4 * q + 4) * LV)
    hh = slice(4 * q, 4 * q + 4)
    C = np.ascontiguousarray
    A = lambda k: np.asarray(inp[k], dtype=np.float32)
    d = {}
    d["xT"] = C(A("L0_attn_a_x")[b].T)
    d["aWqT"] = C(A("L0_attn_a_Wq")[hs].T)
    d["aWkT"] = C(A("L0_attn_a_Wk")[hs].T)
    d["aWvT"] = C(A("L0_attn_a_Wv")[hs].T)
    d["aWdlT"] = C(A("L0_attn_a_Wdl")[hl].T)
    d["aWdtT"] = C(A("L0_attn_a_Wdt")[hh].T)
    hl_, r_, d_ = np.meshgrid(np.arange(4), np.arange(4), np.arange(64),
                              indexing="ij")
    dperm = ((r_ * 4 + hl_) * 64 + d_).ravel()
    d["aWoutT"] = C(A("L0_attn_a_Wout").T[dperm])
    d["aA_log"] = C(A("L0_attn_a_A_log")[hh].reshape(4, 1))
    d["adt_bias"] = C(A("L0_attn_a_dt_bias")[hh].reshape(4, 1))
    d["aLp"] = C(A("L0_attn_a_Lp")[hh].reshape(48, 1))
    d["aln_g"] = C(A("L0_attn_a_ln_g").reshape(8, 128).T)
    d["aln_b"] = C(A("L0_attn_a_ln_b").reshape(8, 128).T)
    for nm in ("q", "k", "v"):
        d[f"bW{nm}T"] = C(A(f"L0_attn_b_w_{nm}")[hs].T)
    d["bWoT"] = C(A("L0_attn_b_w_o").T[dperm])
    for nm in ("q", "k", "v", "o"):
        d[f"bs_{nm}"] = C(A(f"L0_attn_b_s_{nm}").reshape(1, 1))
    d["bb_q"] = C(A("L0_attn_b_b_q")[hs].reshape(2, 128).T)
    d["bb_k"] = C(A("L0_attn_b_b_k")[hs].reshape(2, 128).T)
    d["bb_v"] = C(A("L0_attn_b_b_v")[hs].reshape(1, 256))
    d["bb_o"] = C(A("L0_attn_b_b_o").reshape(8, 128).T)
    d["makerT"] = C(np.transpose(A("L0_ffn_rail_maker"), (0, 1, 3, 2)))
    d["upT"] = C(np.transpose(A("L0_ffn_rail_up_proj"), (0, 1, 3, 2)))
    h_, r_, k_ = np.meshgrid(np.arange(4), np.arange(64), np.arange(64), indexing="ij")
    old = h_ * 4096 + r_ * 64 + k_
    new = (r_ * 2 + h_ // 2) * 128 + (h_ % 2) * 64 + k_
    perm = np.empty(16384, dtype=np.int64)
    perm[new.ravel()] = old.ravel()
    d["wpT"] = C(A("L0_ffn_w_cproj")[:, perm].T)
    return d


def make_consts():
    d = {}
    d["ident"] = np.eye(128, dtype=np.float32)
    d["ones1"] = np.ones((1, 128), dtype=np.float32)
    d["onescol"] = np.ones((128, 1), dtype=np.float32)
    t = np.arange(128)
    xor = t[:, None] ^ t[None, :]
    lv = np.zeros((128, 128), dtype=np.int64)
    for k in range(LV - 1):
        lv += (xor >= (1 << k))
    causal = (t[None, :] <= t[:, None])
    mt = np.zeros((128, 8 * 128), dtype=np.float32)
    for l in range(8):
        mt[:, l * 128:(l + 1) * 128] = ((lv == l) & causal).astype(np.float32)
    import ml_dtypes
    d["maskT"] = mt.astype(ml_dtypes.bfloat16)
    return d


INPUT_SPECS = [
    ("xT", [D, T]), ("aWqT", [D, 256]), ("aWkT", [D, 256]), ("aWvT", [D, 256]),
    ("aWdlT", [D, 48]), ("aWdtT", [D, 4]), ("aWoutT", [D, D]),
    ("aA_log", [4, 1]), ("adt_bias", [4, 1]), ("aLp", [48, 1]),
    ("aln_g", [128, 8]), ("aln_b", [128, 8]),
    ("bWqT", [D, 256]), ("bWkT", [D, 256]), ("bWvT", [D, 256]), ("bWoT", [D, D]),
    ("bs_q", [1, 1]), ("bs_k", [1, 1]), ("bs_v", [1, 1]), ("bs_o", [1, 1]),
    ("bb_q", [128, 2]), ("bb_k", [128, 2]), ("bb_v", [1, 256]), ("bb_o", [128, 8]),
    ("makerT", [4, 64, 16, 16]), ("upT", [4, 64, 16, 64]), ("wpT", [16384, D]),
    ("ident", [128, 128]), ("ones1", [1, 128]), ("onescol", [128, 1]),
    ("maskT", [128, 8 * 128]),
]


class Sem:
    def __init__(self, h, name):
        self.h, self.name, self.val = h, name, 0


class MK:
    ENGS = ("sync", "tensor", "vector", "scalar", "gpsimd")

    def __init__(self, nc, phases=99, dbg=()):
        self.nc = nc
        self.phases = phases
        self.dbg = set(dbg)
        self.cmds = {e: [] for e in self.ENGS}

    def on(self, eng, fn, incs=()):
        for sem, n in incs:
            sem.val += n
        vals = [(sem.h, n) for sem, n in incs]

        def cmd(e, fn=fn, vals=vals):
            inst = fn(e)
            for h, n in vals:
                inst = inst.then_inc(h, n)
        self.cmds[eng].append(cmd)

    def wait(self, eng, sem, val=None):
        v = sem.val if val is None else val
        h = sem.h
        self.cmds[eng].append(lambda e, h=h, v=v: e.wait_ge(h, v))

    def dma(self, eng, out, in_, sem):
        self.on(eng, lambda e, out=out, in_=in_: e.dma_start(out=out, in_=in_),
                [(sem, 16)])

    def mm(self, out, lhsT, rhs, start=True, stop=True, inc=()):
        self.on("tensor", lambda e, o=out, l=lhsT, r=rhs, s1=start, s2=stop:
                e.matmul(o, l, r, start=s1, stop=s2), inc)

    def tr(self, out, in_, n, inc=()):
        idn = self.ident_ap[0:n, 0:n]
        self.on("tensor", lambda e, o=out, i=in_, d=idn: e.transpose(o, i, d), inc)

    def v(self, fn, inc=()):
        self.on("vector", fn, inc)

    def a(self, fn, inc=()):
        self.on("scalar", fn, inc)

    def g(self, fn, inc=()):
        self.on("gpsimd", fn, inc)

    def qsem(self, eng):
        return {"sync": self.sDs, "gpsimd": self.sDg, "scalar": self.sDc}[eng]

    def build(self):
        nc = self.nc
        st = self.stack = ExitStack()
        st.__enter__()
        dram = lambda name, shape, dt, kind="Internal": nc.dram_tensor(
            name, list(shape), dt, kind=kind).ap()
        self.i = {n: dram(n, s, BF16 if n == "maskT" else F32,
                          kind="ExternalInput") for n, s in INPUT_SPECS}
        self.out_d = dram("out", [D, TOWN], F32, kind="ExternalOutput")
        self.Qa_d = dram("Qa_d", [2, 128, T], BF16)
        self.Ya_d = dram("Ya_d", [256, T], BF16)
        self.Yg_d = dram("Yg_d", [4 * 256, T], BF16)
        self.x2_d = dram("x2_d", [D, TOWN], BF16)
        self.c3_d = dram("c3_d", [D, TOWN], BF16)
        self.xg3_d = dram("xg3_d", [4 * D, TOWN], BF16)
        self.wp_bf = dram("wp_bf", [16384, D], BF16)
        self.Yo_d = dram("Yo_d", [4 * 256, TOWN], BF16)

        sb = lambda name, shape, dt: st.enter_context(
            nc.sbuf_tensor(name, list(shape), dt))
        self.R = sb("R", [128, 8 * TOWN], F32)
        self.KV = sb("KV", [128, 33024], BF16)
        self.VOFF = 16384
        self.MISC = sb("MISC", [128, 5376], BF16)
        self.CRL = self.MISC[:, 0:2048]
        self.XR = self.MISC[:, 2048:4096]
        self.SCS0, self.QF11, self.QF10 = 2048, 3584, 4096
        self.QN8, self.QN9, self.SCS2 = 4608, 4736, 4864
        self.F32S = sb("F32S", [128, 5456], F32)
        self.DD0, self.FF0, self.MF = 0, 512, 1024
        self.CSC2, self.RSC2, self.MF2 = 1152, 1216, 1232
        self.CGT, self.LAMT, self.CSC, self.RSC = 2048, 2304, 5376, 5440
        self.WSB = sb("WSB", [128, 6560], BF16)
        self.wq_off, self.wk_off, self.wv_off = 0, 2048, 4096
        self.wdl_off, self.wdt_off = 6144, 6528
        self.QSTG = sb("QSTG", [128, 4096], BF16)
        self.T48 = sb("T48", [48, TGW], F32)
        self.T48B = sb("T48B", [48, TGW], F32)
        self.T4 = sb("T4", [4, TGW], F32)
        self.T4B = sb("T4B", [4, TGW], F32)
        self.ONES4 = sb("ONES4", [4, TGW], F32)
        self.CARRY = sb("CARRY", [4, 1], F32)
        self.PP = sb("PP", [48, 1], F32)
        self.AA = sb("AA", [4, 2], F32)
        self.CONSTB = sb("CONSTB", [128, 1024], BF16)
        self.IDENT = sb("IDENT", [128, 128], F32)
        self.ONE1 = sb("ONE1", [1, 128], F32)
        self.ONEC = sb("ONEC", [128, 1], F32)
        self.ONET = sb("ONET", [128, 128], F32)
        self.LNG = sb("LNG", [128, 16], F32)
        self.BBS = sb("BBS", [128, 16], F32)
        self.BVB = sb("BVB", [128, 256], F32)
        self.STG = sb("STG", [128, 512], F32)
        self.STG2 = sb("STG2", [128, 512], F32)
        self.WPC = sb("WPC", [128, 512], BF16)
        self.Y2 = sb("Y2", [128, 1024], BF16)
        self.HCH = sb("HCH", [128, 2 * 256], BF16)
        self.RB = sb("RB", [128, 1024], F32)
        self.SCL = sb("SCL", [128, 4], F32)
        self.mu_r = self.T48B[0:1, 0:TGW]
        self.rstd_r = self.T4B[0:1, 0:TGW]
        self.nu_r = self.BVB[0:1, 0:TGW]
        self.tmp_r = self.STG[0:1, 0:TGW]

        self.ident_ap = self.IDENT[:, :]
        self.PB = [st.enter_context(nc.psum_tensor(f"pb{i}", [128, 512], F32))
                   for i in range(8)]
        for nm in ("sDs", "sDg", "sDc", "sT", "sV", "sA", "sG", "sCC",
                   "sQ0", "sQ1"):
            setattr(self, nm, Sem(st.enter_context(nc.semaphore(nm)), nm))

        self.build_program()

        with nc.Block() as block:
            for eng in self.ENGS:
                cmds = self.cmds[eng]

                def emit(e, cmds=cmds):
                    for c in cmds:
                        c(e)
                getattr(block, eng)(emit)
        st.__exit__(None, None, None)
        return nc

    def tap_dram(self, name, src, shape, dtype):
        t = self.nc.dram_tensor("dbg_" + name, list(shape), dtype,
                                kind="ExternalOutput").ap()
        p, f = shape
        for r0 in range(0, p, 128):
            rp = min(128, p - r0)
            for c0 in range(0, f, 512):
                w = min(512, f - c0)
                buf = self.Y2[0:rp, 0:w] if dtype == BF16 else self.STG[0:rp, 0:w]
                self.dma("sync", buf, src[r0:r0 + rp, c0:c0 + w], self.sDs)
                self.wait("sync", self.sDs)
                self.dma("sync", t[r0:r0 + rp, c0:c0 + w], buf, self.sDs)
                self.wait("sync", self.sDs)

    def tap_sbuf(self, name, src, shape, dtype):
        t = self.nc.dram_tensor("dbg_" + name, list(shape), dtype,
                                kind="ExternalOutput").ap()
        self.dma("sync", t[0:shape[0], 0:shape[1]], src, self.sDs)
        self.wait("sync", self.sDs)

    def build_program(self):
        self.p0_consts_weights()
        self.p1_xln1_proj()
        if self.phases >= 2:
            self.p2_attn_a()
        if self.phases >= 4:
            self.p4_close_attn(kind="a")
            if "p4" in self.dbg:
                self.wait("sync", self.sV, self.sV.val)
                self.tap_sbuf("R1", self.R[:, :], (128, 8 * TOWN), F32)
        if self.phases >= 5:
            self.p5_ffn(0)
            if "p5" in self.dbg:
                self.wait("sync", self.sV, self.sV.val)
                self.tap_sbuf("R2", self.R[:, :], (128, 8 * TOWN), F32)
        if self.phases >= 6:
            self.p6_attn_b()
        if self.phases >= 7:
            self.p5_ffn(1)
        if self.phases >= 8:
            self.p7_out()

    def p0_consts_weights(self):
        i = self.i

        sDs_h = self.sDs.h

        def r_init(e):
            q = e.partition_id() % 4
            self._q_sync = q
            inst = None
            for ch in range(8):
                inst = e.dma_start(
                    out=self.R[:, ch * TOWN:(ch + 1) * TOWN],
                    in_=i["xT"][ch * 128:(ch + 1) * 128, bass.ts(q, TOWN)])
                inst.then_inc(sDs_h, 16)
            return inst
        self.sDs.val += 16 * 8
        self.on("sync", r_init)
        self.r_init_done = self.sDs.val

        for dst, src in [(self.CONSTB[:, 0:1024], i["maskT"][:, :]),
                         (self.IDENT[:, :], i["ident"][:, :]),
                         (self.ONE1[:, :], i["ones1"][:, :]),
                         (self.ONEC[:, :], i["onescol"][:, :]),
                         (self.PP[:, :], i["aLp"][:, :]),
                         (self.AA[0:4, 1:2], i["adt_bias"][:, :]),
                         (self.STG[0:4, 0:1], i["aA_log"][:, :]),
                         (self.LNG[:, 0:8], i["aln_g"][:, :]),
                         (self.LNG[:, 8:16], i["aln_b"][:, :])]:
            self.dma("sync", dst, src, self.sDs)
        const_done = self.const_done = self.sDs.val
        self.wait("sync", self.sDs, const_done)
        self.wait("vector", self.sDs, const_done)
        self.v(lambda e: e.memset(self.ONET[:, :], 1.0))
        self.v(lambda e: e.memset(self.ONES4[:, :], 1.0))
        self.v(lambda e: e.memset(self.CARRY[:, :], 0.0), [(self.sV, 1)])
        self.wait("scalar", self.sDs, const_done)
        self.a(lambda e: e.activation(self.AA[0:4, 0:1], self.STG[0:4, 0:1], AF.Exp),
               [(self.sA, 1)])
        self.wait("vector", self.sA)
        self.v(lambda e: e.tensor_scalar(self.AA[0:4, 0:1], self.AA[0:4, 0:1],
                                         -1.0, None, ALU.mult), [(self.sV, 1)])
        self.consts_ready = self.sV.val

        casts = []
        for ch in range(8):
            casts += [(i["aWqT"][ch * 128:(ch + 1) * 128, :], self.wq_off + ch * 256, 256),
                      (i["aWkT"][ch * 128:(ch + 1) * 128, :], self.wk_off + ch * 256, 256),
                      (i["aWvT"][ch * 128:(ch + 1) * 128, :], self.wv_off + ch * 256, 256),
                      (i["aWdlT"][ch * 128:(ch + 1) * 128, :], self.wdl_off + ch * 48, 48),
                      (i["aWdtT"][ch * 128:(ch + 1) * 128, :], self.wdt_off + ch * 4, 4)]
        self.emit_casts(casts, self.WSB)
        self.w_cast_done = self.wcast_g

    def emit_casts(self, casts, dstbuf):
        pos, group, flushes = 0, [], []
        for src, off, w in casts:
            if pos + w > 512:
                flushes.append(group)
                group, pos = [], 0
            group.append((src, off, w, pos))
            pos += w
        if group:
            flushes.append(group)
        for gi, grp in enumerate(flushes):
            if gi >= 1:
                self.wait("gpsimd", self.sG, self.sG.val)
            for src, off, w, p0 in grp:
                self.dma("gpsimd", self.STG2[:, p0:p0 + w], src, self.sDg)
            self.wait("gpsimd", self.sDg, self.sDg.val)
            for k, (src, off, w, p0) in enumerate(grp):
                self.g(lambda e, off=off, w=w, p0=p0, d=dstbuf:
                       e.tensor_copy(d[:, off:off + w], self.STG2[:, p0:p0 + w]),
                       [(self.sG, 1)])
            self.wcast_g = self.sG.val

    def emit_wp_prep(self):
        it = self._wp_it
        self._wp_it += 1
        i = self.i
        for half in range(2):
            stg = self.STG2[:, 0:512]
            wpc = self.WPC[:, 0:512]
            if self._wp_guard[0] is not None:
                self.wait("gpsimd", self.sG, self._wp_guard[0])
            elif getattr(self, "wcast_g", 0):
                self.wait("gpsimd", self.sG, self.wcast_g)
            if self._wp_outg[0] is not None:
                self.wait("gpsimd", self.sDg, self._wp_outg[0])
            self.dma("gpsimd", stg,
                     i["wpT"][it * 128:(it + 1) * 128, half * 512:(half + 1) * 512],
                     self.sDg)
            self.wait("gpsimd", self.sDg, self.sDg.val)
            self.g(lambda e, wpc=wpc, stg=stg: e.tensor_copy(wpc, stg),
                   [(self.sG, 1)])
            self._wp_guard[0] = self.sG.val
            self.wait("gpsimd", self.sG, self.sG.val)
            self.dma("gpsimd",
                     self.wp_bf[it * 128:(it + 1) * 128, half * 512:(half + 1) * 512],
                     wpc, self.sDg)
            self._wp_outg[0] = self.sDg.val

    def emit_ln(self, src_pair, dst_pair, affine=None):
        PB, W = self.PB, TGW
        mu, rstd, nu, tmp = self.mu_r, self.rstd_r, self.nu_r, self.tmp_r
        p6f = PB[6][:, 0:2 * W].rearrange("p (c t) -> p c t", c=2)
        p7f = PB[7][:, 0:2 * W].rearrange("p (c t) -> p c t", c=2)
        self.wait("scalar", self.sV, self.sV.val)
        mmdone = []
        for pi in range(4):
            rb = self.RB[:, (pi % 2) * 2 * W:(pi % 2) * 2 * W + 2 * W]
            rb3 = rb.rearrange("p (c t) -> p c t", c=2)
            if pi >= 2:
                self.wait("scalar", self.sT, mmdone[pi - 2])
            self.a(lambda e, s=src_pair(pi), rb3=rb3: e.activation(
                rb3, s, AF.Square), [(self.sA, 1)])
            self.wait("tensor", self.sA, self.sA.val)
            self.mm(PB[6][0:1, 0:2 * W], self.ONEC[:, :], src_pair(pi),
                    start=(pi == 0), stop=(pi == 3))
            self.mm(PB[7][0:1, 0:2 * W], self.ONEC[:, :], rb3,
                    start=(pi == 0), stop=(pi == 3), inc=[(self.sT, 1)])
            mmdone.append(self.sT.val)
        self.wait("vector", self.sT, self.sT.val)
        self.v(lambda e: e.tensor_scalar(mu, PB[6][0:1, 0:W], 1.0 / D, None,
                                         ALU.mult))
        self.v(lambda e: e.tensor_scalar(tmp, PB[7][0:1, 0:W], 1.0 / D, None,
                                         ALU.mult), [(self.sV, 1)])
        self.wait("vector", self.sV, self.sV.val)
        self.v(lambda e: e.scalar_tensor_tensor(mu, PB[6][0:1, W:2 * W], 1.0 / D,
                                                mu, ALU.mult, ALU.add))
        self.v(lambda e: e.scalar_tensor_tensor(tmp, PB[7][0:1, W:2 * W], 1.0 / D,
                                                tmp, ALU.mult, ALU.add),
               [(self.sV, 1)])
        self.wait("scalar", self.sV, self.sV.val)
        self.a(lambda e: e.activation(self.RB[0:1, 0:W], mu, AF.Square), [(self.sA, 1)])
        self.wait("vector", self.sA, self.sA.val)
        self.wait("vector", self.sV, self.sV.val)
        self.v(lambda e: e.scalar_tensor_tensor(tmp, tmp, 1e-5, self.RB[0:1, 0:W],
                                                ALU.add, ALU.subtract), [(self.sV, 1)])
        self.wait("scalar", self.sV, self.sV.val)
        self.a(lambda e: e.activation(tmp, tmp, AF.Ln), [(self.sA, 1)])
        self.wait("scalar", self.sA, self.sA.val)
        self.a(lambda e: e.activation(rstd, tmp, AF.Exp, scale=-0.5), [(self.sA, 1)])
        self.wait("vector", self.sA, self.sA.val)
        self.v(lambda e: e.tensor_mul(nu, mu, rstd), [(self.sV, 1)])
        self.wait("tensor", self.sV, self.sV.val)
        self.mm(PB[6][:, 0:W], self.ONE1[:, :], rstd)
        self.mm(PB[6][:, W:2 * W], self.ONE1[:, :], rstd)
        self.mm(PB[7][:, 0:W], self.ONE1[:, :], nu)
        self.mm(PB[7][:, W:2 * W], self.ONE1[:, :], nu, inc=[(self.sT, 1)])
        self.wait("vector", self.sT, self.sT.val)
        if affine == "stats":
            return
        tmpt = self.RB[:, 0:2 * W]
        tmpt3 = tmpt.rearrange("p (c t) -> p c t", c=2)
        for pi in range(4):
            if pi > 0:
                self.wait("vector", self.sV, self.sV.val)
            self.v(lambda e, s=src_pair(pi), t=tmpt3: e.tensor_mul(t, s, p6f),
                   [(self.sV, 1)])
            self.wait("vector", self.sV, self.sV.val)
            self.v(lambda e, t=tmpt3, d=dst_pair(pi): e.tensor_sub(d, t, p7f),
                   [(self.sV, 1)])

    def emit_ln_pairs_from_R(self, og):
        W = TGW
        r3 = self.R[:, :].rearrange("p (c t) -> p c t", c=8)
        return lambda pi: r3[:, 2 * pi:2 * pi + 2, og * W:(og + 1) * W]

    def emit_softplus(self, x, tmp):
        self.wait("vector", self.sV, self.sV.val)
        self.v(lambda e: e.tensor_scalar(tmp, x, -1.0, None, ALU.mult), [(self.sV, 1)])
        self.wait("vector", self.sV, self.sV.val)
        self.v(lambda e: e.tensor_max(tmp, x, tmp), [(self.sV, 1)])
        self.wait("scalar", self.sV, self.sV.val)
        self.a(lambda e: e.activation(tmp, tmp, AF.Exp, scale=-1.0), [(self.sA, 1)])
        self.wait("scalar", self.sA, self.sA.val)
        self.a(lambda e: e.activation(tmp, tmp, AF.Ln, bias=1.0), [(self.sA, 1)])
        self.wait("vector", self.sA, self.sA.val)
        self.v(lambda e: e.tensor_scalar(x, x, 0.0, None, ALU.max), [(self.sV, 1)])
        self.wait("vector", self.sV, self.sV.val)
        self.v(lambda e: e.tensor_add(x, x, tmp), [(self.sV, 1)])

    def p1_xln1_proj(self):
        i, PB = self.i, self.PB
        W = TGW
        xt = lambda ch: self.F32S[:, ch * W:(ch + 1) * W]
        xlb = lambda ch: self.MISC[:, ch * W:(ch + 1) * W]
        wsb = self.WSB
        self._wp_it, self._wp_guard, self._wp_outg = 0, [None, None], [None, None]
        self.wait("tensor", self.sG, self.w_cast_done)
        self.v(lambda e: e.memset(
            self.KV[:, self.VOFF + 64:self.VOFF + 16640:65], 1.0), [(self.sV, 1)])
        self.wait("tensor", self.sV, self.consts_ready)
        proj_done = 0
        v_evac_done = 0
        lamcg_done = 0
        for tg in range(NTG):
            for ch in range(8):
                if tg > 0 and ch == 0:
                    self.wait("sync", self.sV, self.xln_consumed)
                self.dma("sync", xt(ch), i["xT"][ch * 128:(ch + 1) * 128,
                                                 tg * W:(tg + 1) * W], self.sDs)
            self.wait("scalar", self.sDs, self.sDs.val)
            if tg > 0:
                self.wait("tensor", self.sV, v_evac_done)
                self.wait("vector", self.sT, proj_done)
            self.emit_ln(lambda pi: self.F32S[:, pi * 512:(pi + 1) * 512],
                         lambda pi: self.MISC[:, pi * 512:(pi + 1) * 512])
            self.xln_consumed = self.sV.val
            xln_done = self.sV.val

            self.wait("tensor", self.sV, xln_done)
            if tg > 0:
                self.wait("tensor", self.sV, self.qk_evac_done)
                self.wait("tensor", self.sV, lamcg_done)
            for ch in range(8):
                for p in range(2):
                    self.mm(PB[0 + p][:, 0:W],
                            wsb[:, self.wq_off + ch * 256 + p * 128:
                                self.wq_off + ch * 256 + (p + 1) * 128],
                            xlb(ch), start=(ch == 0), stop=(ch == 7))
                    self.mm(PB[2 + p][:, 0:W],
                            wsb[:, self.wk_off + ch * 256 + p * 128:
                                self.wk_off + ch * 256 + (p + 1) * 128],
                            xlb(ch), start=(ch == 0), stop=(ch == 7))
                self.mm(PB[4][0:48, 0:W],
                        wsb[:, self.wdl_off + ch * 48:self.wdl_off + (ch + 1) * 48],
                        xlb(ch), start=(ch == 0), stop=(ch == 7))
                self.mm(PB[5][0:4, 0:W],
                        wsb[:, self.wdt_off + ch * 4:self.wdt_off + (ch + 1) * 4],
                        xlb(ch), start=(ch == 0), stop=(ch == 7))
                for sub in range(2):
                    self.mm(PB[6 + sub][:, 0:256],
                            xlb(ch)[:, sub * 128:(sub + 1) * 128],
                            wsb[:, self.wv_off + ch * 256:self.wv_off + (ch + 1) * 256],
                            start=(ch == 0), stop=(ch == 7),
                            inc=[(self.sT, 1)] if (ch == 7 and sub == 1) else ())
            proj_done = self.sT.val

            self.wait("vector", self.sT, proj_done)
            for p in range(2):
                self.v(lambda e, p=p, tg=tg: e.tensor_copy(
                    self.KV[:, p * T + tg * W:p * T + (tg + 1) * W],
                    PB[2 + p][:, 0:W]))
                self.v(lambda e, p=p: e.tensor_copy(
                    self.QSTG[:, p * W:(p + 1) * W], PB[0 + p][:, 0:W]),
                    [(self.sV, 1)] if p == 1 else ())
            self.qk_evac_done = self.sV.val
            self.wait("sync", self.sV, self.qk_evac_done)
            for p in range(2):
                self.dma("sync", self.Qa_d[p][:, tg * W:(tg + 1) * W],
                         self.QSTG[:, p * W:(p + 1) * W], self.sDs)
            self.q_dma_done = self.sDs.val
            for sub in range(2):
                j = tg * 2 + sub
                dst = self.KV[:, self.VOFF + j * 260:self.VOFF + (j + 1) * 260]
                dst3 = dst.rearrange("p (h c) -> p h c", h=4)[:, :, 0:64]
                src3 = PB[6 + sub][:, 0:256] \
                    .rearrange("p (h c) -> p h c", h=4)
                self.v(lambda e, d=dst3, s=src3: e.tensor_copy(d, s),
                       [(self.sV, 1)] if sub == 1 else ())
            v_evac_done = self.sV.val

            self.v(lambda e: e.tensor_scalar(self.T48[:, :], PB[4][0:48, 0:W],
                                             self.PP[0:48, 0:1], None, ALU.mult))
            self.v(lambda e: e.tensor_scalar(self.T4[:, :], PB[5][0:4, 0:W],
                                             self.AA[0:4, 1:2], None, ALU.add),
                   [(self.sV, 1)])
            self.emit_softplus(self.T48[:, :], self.T48B[:, :])
            self.emit_softplus(self.T4[:, :], self.T4B[:, :])
            self.wait("vector", self.sV, self.sV.val)
            self.v(lambda e: e.tensor_scalar(self.T4[:, :], self.T4[:, :],
                                             self.AA[0:4, 0:1], None, ALU.mult),
                   [(self.sV, 1)])
            self.wait("vector", self.sV, self.sV.val)
            self.v(lambda e: e.tensor_tensor_scan(
                self.T4B[:, :], self.ONES4[:, :], self.T4[:, :],
                self.CARRY[0:4, 0:1], ALU.mult, ALU.add), [(self.sV, 1)])
            self.wait("vector", self.sV, self.sV.val)
            self.v(lambda e: e.tensor_copy(self.CARRY[0:4, 0:1],
                                           self.T4B[0:4, W - 1:W]), [(self.sV, 1)])
            cg_done = self.sV.val

            self.wait("tensor", self.sV, cg_done)
            for sub in range(2):
                j = tg * 2 + sub
                self.tr(PB[4][:, 0:48], self.T48[:, sub * 128:(sub + 1) * 128], 48,
                        inc=[(self.sT, 1)])
                t_lam = self.sT.val
                self.tr(PB[5][:, 0:4], self.T4B[:, sub * 128:(sub + 1) * 128], 4,
                        inc=[(self.sT, 1)])
                t_cg = self.sT.val
                self.wait("vector", self.sT, t_lam)
                dlam = self.F32S[:, self.LAMT + j:self.LAMT + j + 48 * 64:64]
                self.v(lambda e, d=dlam: e.tensor_copy(d, PB[4][:, 0:48]))
                self.wait("vector", self.sT, t_cg)
                dcg = self.F32S[:, self.CGT + j:self.CGT + j + 4 * 64:64]
                self.v(lambda e, d=dcg: e.tensor_copy(d, PB[5][:, 0:4]),
                       [(self.sV, 1)])
                if sub == 0:
                    self.wait("tensor", self.sV, self.sV.val)
            lamcg_done = self.sV.val

            for _ in range(4):
                if self._wp_it < 128:
                    self.emit_wp_prep()

        self.p1_done = dict(sV=self.sV.val, sT=self.sT.val, sDs=self.sDs.val)
        if "p1" in self.dbg:
            self.wait("sync", self.sV, self.sV.val)
            self.tap_sbuf("K0", self.KV[:, 0:T], (128, T), BF16)
            self.tap_sbuf("K1", self.KV[:, T:2 * T], (128, T), BF16)
            self.tap_sbuf("Vtm", self.KV[:, self.VOFF:self.VOFF + 64 * 260],
                          (128, 64 * 260), BF16)
            self.tap_sbuf("cgT", self.F32S[:, self.CGT:self.CGT + 256], (128, 256), F32)
            self.tap_sbuf("LamT", self.F32S[:, self.LAMT:self.LAMT + 3072],
                          (128, 3072), F32)
            self.tap_dram("Qa0", self.Qa_d[0], (128, T), BF16)
            self.tap_dram("Qa1", self.Qa_d[1], (128, T), BF16)

    def p2_attn_a(self):
        PB = self.PB
        while self._wp_it < 128:
            self.emit_wp_prep()

        hm = [(h, m) for h in range(4) for m in range(NG)]
        self.yg_head_done = {}
        self.v(lambda e: e.memset(self.MISC[:, 3584:4864], 0.0), [(self.sV, 1)])
        self.wait("gpsimd", self.sV, self.p1_done["sV"])
        self.wait("gpsimd", self.sDs, self.const_done)
        self.wait("sync", self.sDs, self.sDs.val)
        self._qs_guard = [None, None]
        qsem = [self.sQ0, self.sQ1]

        def qs_dma(idx):
            h, m = hm[idx]
            slot = idx % 2
            if self._qs_guard[slot] is not None:
                tval, vval = self._qs_guard[slot]
                self.wait("sync", self.sT, tval)
                self.wait("sync", self.sV, vval)
            self.dma("sync", self.QSTG[:, slot * 512:(slot + 1) * 512],
                     self.Qa_d[h // 2][:, m * 512:(m + 1) * 512], qsem[slot])
            return qsem[slot].val
        qs_done = {0: qs_dma(0)}

        y_evac_done = 0
        self.v(lambda e: e.memset(self.QSTG[:, 1024:2048], 0.0), [(self.sV, 1)])
        qf_sets = [
            (self.MISC[:, self.QF11:self.QF11 + 512],
             self.MISC[:, self.QF10:self.QF10 + 512]),
            (self.QSTG[:, 1024:1536], self.QSTG[:, 1536:2048]),
        ]
        CSCS = [self.CSC, self.CSC2]
        RSCS = [self.RSC, self.RSC2]
        MFS = [self.MF, self.MF2]
        prep_qf_done = {}

        def prep(idx):
            h, m = hm[idx]
            slot = idx % 2
            qrow = (h % 2) * 64
            QS = self.QSTG[qrow:qrow + 64, slot * 512:(slot + 1) * 512]
            bsel = idx % 2
            csc, rsc, mf = CSCS[bsel], RSCS[bsel], MFS[bsel]
            qf11_ap, qf10_ap = qf_sets[bsel]
            cgcols = lambda j0, n: self.F32S[:, self.CGT + h * 64 + j0:
                                             self.CGT + h * 64 + j0 + n]
            lamcols = lambda l, j0, n: self.F32S[:, self.LAMT + (h * 12 + l) * 64
                                                 + j0:
                                                 self.LAMT + (h * 12 + l) * 64
                                                 + j0 + n]
            if m in (0, 1) and h > 0:
                qrow_old = ((h - 1) % 2) * 64
                self.wait("vector", self.sT, self.sT.val)
                if bsel == 0:
                    self.v(lambda e, r=qrow_old: e.memset(
                        self.MISC[r:r + 64, 3584:4608], 0.0), [(self.sV, 1)])
                else:
                    self.v(lambda e, r=qrow_old: e.memset(
                        self.QSTG[r:r + 64, 1024:2048], 0.0), [(self.sV, 1)])
            self.wait("vector", qsem[slot], qs_done[idx])
            self.wait("tensor", self.sV, self.sV.val)
            self.wait("tensor", self.sA, self.sA.val)
            self.mm(PB[6][:, 0:1], self.ONE1[:, :],
                    self.F32S[0:1, self.CGT + h * 64 + 4 * m:
                              self.CGT + h * 64 + 4 * m + 1], inc=[(self.sT, 1)])
            self.wait("vector", self.sT, self.sT.val)
            self.v(lambda e: e.tensor_copy(self.SCL[:, 0:1], PB[6][:, 0:1]))
            self.v(lambda e: e.tensor_scalar(self.SCL[:, 1:2], PB[6][:, 0:1],
                                             -1.0, None, ALU.mult),
                   [(self.sV, 1)])
            scl_done = self.sV.val
            self.wait("scalar", self.sV, scl_done)
            if m > 0:
                self.a(lambda e, n=4 * m, src_=cgcols(0, 4 * m), csc=csc:
                       e.activation(
                    self.F32S[:, csc:csc + n], src_,
                    AF.Exp, scale=-1.0, bias=self.SCL[:, 0:1]), [(self.sA, 1)])
            self.a(lambda e, src_=cgcols(4 * m, 4), rsc=rsc: e.activation(
                self.F32S[:, rsc:rsc + 4], src_,
                AF.Exp, bias=self.SCL[:, 1:2]), [(self.sA, 1)])
            self.wait("vector", self.sA, self.sA.val)
            if m > 0:
                self.v(lambda e, src_=lamcols(11, 4 * m, 4), rsc=rsc:
                       e.scalar_tensor_tensor(
                    self.F32S[:, rsc + 4:rsc + 8], src_,
                    0.125, self.F32S[:, rsc:rsc + 4], ALU.mult, ALU.mult),
                    [(self.sV, 1)])
            if m % 2 == 1:
                self.v(lambda e, src_=lamcols(10, 4 * m, 4), rsc=rsc:
                       e.scalar_tensor_tensor(
                    self.F32S[:, rsc + 8:rsc + 12], src_,
                    0.125, self.F32S[:, rsc:rsc + 4], ALU.mult, ALU.mult),
                    [(self.sV, 1)])
            rs_done = self.sV.val
            if m > 0:
                self.wait("scalar", self.sV, rs_done)
                rounds = [(qf11_ap, rsc + 4)]
                if m % 2 == 1:
                    rounds.append((qf10_ap, rsc + 8))
                bct = self.F32S[:, mf:mf + 128]
                bank_guard = {6: scl_done, 7: scl_done}
                rno = 0
                for qf, sc0 in rounds:
                    for jp in range(2):
                        if rno > 0:
                            self.wait("scalar", self.sT, self.sT.val)
                        for jj in range(2):
                            self.a(lambda e, jj=jj, bct=bct,
                                   c=self.F32S[:, sc0 + 2 * jp + jj:
                                               sc0 + 2 * jp + jj + 1]:
                                   e.activation(bct[:, jj * 64:(jj + 1) * 64],
                                                self.ONET[:, 0:64], AF.Copy,
                                                scale=c),
                                   [(self.sA, 1)] if jj == 1 else ())
                        self.wait("tensor", self.sA, self.sA.val)
                        bno = 6 + rno % 2
                        bank = PB[bno]
                        self.wait("tensor", self.sV, bank_guard[bno])
                        self.tr(bank[:, 0:128], bct[:, 0:128], 128,
                                inc=[(self.sT, 1)])
                        self.wait("vector", self.sT, self.sT.val)
                        for jj in range(2):
                            j = 2 * jp + jj
                            self.v(lambda e, j=j, jj=jj, qf=qf, bank=bank,
                                   QS=QS, qrow=qrow: e.tensor_mul(
                                qf[qrow:qrow + 64, j * 128:(j + 1) * 128],
                                QS[0:64, j * 128:(j + 1) * 128],
                                bank[jj * 64:(jj + 1) * 64, 0:128]),
                                [(self.sV, 1)] if jj == 1 else ())
                        bank_guard[bno] = self.sV.val
                        rno += 1
            prep_qf_done[idx] = self.sV.val

        prep(0)
        for idx, (h, m) in enumerate(hm):
            slot = idx % 2
            if idx + 1 < len(hm):
                qs_done[idx + 1] = qs_dma(idx + 1)
                prep(idx + 1)
            qrow = (h % 2) * 64
            QS = self.QSTG[qrow:qrow + 64, slot * 512:(slot + 1) * 512]
            kro = (h % 2) * 64
            kpo = (h // 2) * T
            csc, rsc, mf = CSCS[idx % 2], RSCS[idx % 2], MFS[idx % 2]
            qf11_mov, qf10_mov = qf_sets[idx % 2]
            if m == 0 and h > 0:
                qrow_old = ((h - 1) % 2) * 64
                self.wait("vector", self.sT, self.sT.val)
                self.v(lambda e, r=qrow_old: e.memset(
                    self.MISC[r:r + 64, 4608:4864], 0.0), [(self.sV, 1)])
            cgc = lambda j: self.F32S[:, self.CGT + h * 64 + j:
                                      self.CGT + h * 64 + j + 1]
            cgcols = lambda j0, n: self.F32S[:, self.CGT + h * 64 + j0:
                                             self.CGT + h * 64 + j0 + n]
            lamc = lambda l, j: self.F32S[:, self.LAMT + (h * 12 + l) * 64 + j:
                                          self.LAMT + (h * 12 + l) * 64 + j + 1]

            dd = self.F32S[:, self.DD0:self.DD0 + 512]
            ff = self.F32S[:, self.FF0:self.FF0 + 512]
            if getattr(self, "_prev_av_all", None):
                self.wait("vector", self.sT, self._prev_av_all)
            _val_iq = [None] * 4
            _mask_ops = [(l, iq) for l in range(8) for iq in range(4)]

            def emit_mask_ops(n):
                for _ in range(min(n, len(_mask_ops))):
                    l, iq = _mask_ops.pop(0)
                    a_ = 4 * m + iq
                    tgt = ff[:, iq * 128:(iq + 1) * 128]
                    msk = self.CONSTB[:, l * 128:(l + 1) * 128]
                    lc = lamc(l, a_)
                    if _val_iq[iq] is not None:
                        self.wait("vector", self.sV, _val_iq[iq])
                    if l == 0:
                        self.v(lambda e, t=tgt, m_=msk, lc=lc: e.tensor_scalar(
                            t, m_, lc, None, ALU.mult), [(self.sV, 1)])
                    else:
                        self.v(lambda e, t=tgt, m_=msk, lc=lc:
                               e.scalar_tensor_tensor(
                            t, m_, lc, t, ALU.mult, ALU.add), [(self.sV, 1)])
                    _val_iq[iq] = self.sV.val

            first_av = [True]
            av_cnt = [0]
            n_far = 4 * m
            n_near = {0: 0, 1: 1, 2: 2, 3: 3}
            total_av = n_far + sum(n_near[i] for i in range(4)) + 4

            evac_val = {}
            qk_val = {}
            def scs_slot(k):
                i = k % 8
                c = i * 512 if i < 7 else self.SCS2
                return self.MISC[:, c:c + 512]
            av_done_at = {}

            def emit_av(rhs, j, width, start, stop, col0):
                vap = self.KV[:, self.VOFF + j * 260 + h * 65:
                              self.VOFF + j * 260 + h * 65 + 64]
                self.mm(PB[2][0:64, col0:col0 + width], vap, rhs,
                        start=start, stop=stop, inc=[(self.sT, 1)])

            if y_evac_done:
                self.wait("tensor", self.sV, y_evac_done)
            if m > 0:
                self.wait("tensor", self.sV, prep_qf_done[idx])
            SBANKS = [0, 1, 3, 5]
            nb = (n_far + 3) // 4
            ev_v = {}
            ev_a = {}

            def qk_of(k):
                lvl = 10 if (k // 4) == (m ^ 1) else 11
                qmov = qf11_mov if lvl == 11 else qf10_mov
                return (PB[SBANKS[k % 4]],
                        self.KV[:, kpo + k * 128:kpo + (k + 1) * 128],
                        qmov)
            for kb in range(nb):
                ks = list(range(kb * 4, min(kb * 4 + 4, n_far)))
                ks0, ks1 = ks[0:2], ks[2:4]
                if kb >= 1:
                    self.wait("tensor", self.sV, ev_v[kb - 1])
                for k in ks0:
                    bank, st, mv = qk_of(k)
                    self.mm(bank[:, 0:512], st, mv, start=True, stop=True,
                            inc=[(self.sT, 1)] if k == ks0[-1] else ())
                qk0 = self.sT.val
                if kb >= 1:
                    self.wait("tensor", self.sA, ev_a[kb - 1])
                for k in ks1:
                    bank, st, mv = qk_of(k)
                    self.mm(bank[:, 0:512], st, mv, start=True, stop=True,
                            inc=[(self.sT, 1)] if k == ks1[-1] else ())
                qk1 = self.sT.val
                if kb >= 1:
                    for pk in range((kb - 1) * 4, kb * 4):
                        emit_av(scs_slot(pk), pk, 512, first_av[0], False, 0)
                        first_av[0] = False
                self.wait("vector", self.sT, qk0)
                for k in ks0:
                    cs = self.F32S[:, csc + k:csc + k + 1]
                    self.v(lambda e, b=PB[SBANKS[k % 4]], s=scs_slot(k), cs=cs:
                           e.tensor_scalar(s, b[:, 0:512], cs, None, ALU.mult),
                           [(self.sV, 1)])
                ev_v[kb] = self.sV.val
                if ks1:
                    self.wait("scalar", self.sT, qk1)
                    for k in ks1:
                        cs = self.F32S[:, csc + k:csc + k + 1]
                        self.a(lambda e, b=PB[SBANKS[k % 4]], s=scs_slot(k), cs=cs:
                               e.activation(s, b[:, 0:512], AF.Copy, scale=cs),
                               [(self.sA, 1)])
                ev_a[kb] = self.sA.val
                emit_mask_ops(6)
            if nb >= 1:
                self.wait("tensor", self.sV, ev_v[nb - 1])
                self.wait("tensor", self.sA, ev_a[nb - 1])
                for pk in range((nb - 1) * 4, n_far):
                    emit_av(scs_slot(pk), pk, 512, first_av[0], False, 0)
                    first_av[0] = False

            for iq in range(4):
                a = 4 * m + iq
                col0 = iq * 128
                near = []
                if iq == 1:
                    near = [(a - 1, 8)]
                elif iq == 2:
                    near = [(a - 2, 9), (a - 1, 9)]
                elif iq == 3:
                    near = [(a - 3, 9), (a - 2, 9), (a - 1, 8)]
                if near:
                    self.wait("tensor", self.sV, self.sV.val)
                    self.mm(PB[6][:, 1:2], self.ONE1[:, :],
                            self.F32S[0:1, self.CGT + h * 64 + a:
                                      self.CGT + h * 64 + a + 1], inc=[(self.sT, 1)])
                    self.wait("vector", self.sT, self.sT.val)
                    self.v(lambda e: e.tensor_copy(self.SCL[:, 2:3], PB[6][:, 1:2]))
                    self.v(lambda e: e.tensor_scalar(
                        self.SCL[:, 3:4], PB[6][:, 1:2], -1.0, None, ALU.mult),
                        [(self.sV, 1)])
                    self.wait("scalar", self.sV, self.sV.val)
                    self.a(lambda e, src_=cgc(a), rsc=rsc: e.activation(
                        self.F32S[:, rsc + 12:rsc + 13], src_,
                        AF.Exp, bias=self.SCL[:, 3:4]), [(self.sA, 1)])
                    nn = len(near)
                    self.a(lambda e, nn=nn, src_=cgcols(a - nn, nn), csc=csc:
                           e.activation(
                        self.F32S[:, csc + 60:csc + 60 + nn], src_,
                        AF.Exp, scale=-1.0, bias=self.SCL[:, 2:3]), [(self.sA, 1)])
                    self.wait("vector", self.sA, self.sA.val)
                    lvls = sorted(set(l for _, l in near))
                    for li, l in enumerate(lvls):
                        self.v(lambda e, li=li, src_=lamc(l, a), rsc=rsc:
                               e.scalar_tensor_tensor(
                            self.F32S[:, rsc + 13 + li:rsc + 14 + li],
                            src_, 0.125, self.F32S[:, rsc + 12:rsc + 13],
                            ALU.mult, ALU.mult), [(self.sV, 1)])
                    self.wait("scalar", self.sV, self.sV.val)
                    bct = self.F32S[:, mf:mf + 128]
                    for li, l in enumerate(lvls):
                        c0 = li * 64
                        self.a(lambda e, c0=c0, li=li, bct=bct, rsc=rsc:
                               e.activation(
                            bct[:, c0:c0 + 64], self.ONET[:, 0:64],
                            AF.Copy, scale=self.F32S[:, rsc + 13 + li:rsc + 14 + li]),
                            [(self.sA, 1)] if li == len(lvls) - 1 else ())
                    self.wait("tensor", self.sA, self.sA.val)
                    self.wait("tensor", self.sV, self.sV.val)
                    for li, l in enumerate(lvls):
                        self.tr(PB[6 + li][0:64, 0:128],
                                bct[:, li * 64:li * 64 + 64], 128,
                                inc=[(self.sT, 1)] if li == len(lvls) - 1 else ())
                    self.wait("vector", self.sT, self.sT.val)
                    for li, l in enumerate(lvls):
                        qreg = self.QN8 if l == 8 else self.QN9
                        self.v(lambda e, qreg=qreg, li=li, iq=iq, QS=QS, qrow=qrow:
                               e.tensor_mul(
                            self.MISC[qrow:qrow + 64, qreg:qreg + 128],
                            QS[0:64, iq * 128:(iq + 1) * 128],
                            PB[6 + li][0:64, 0:128]),
                            [(self.sV, 1)] if li == len(lvls) - 1 else ())
                    nq_done_a, nq_done_v = self.sA.val, self.sV.val
                    self.wait("tensor", self.sA, nq_done_a)
                    self.wait("tensor", self.sV, nq_done_v)
                    for ki, (kk, l) in enumerate(near):
                        bank = PB[kk % 2]
                        qreg = self.QN8 if l == 8 else self.QN9
                        self.mm(bank[:, 0:128],
                                self.KV[:, kpo + kk * 128:kpo + (kk + 1) * 128],
                                self.MISC[:, qreg:qreg + 128],
                                start=True, stop=True, inc=[(self.sT, 1)])
                        myqk = self.sT.val
                        self.wait("vector", self.sT, myqk)
                        cs = self.F32S[:, csc + 60 + ki:csc + 61 + ki]
                        sl = scs_slot(ki)
                        self.v(lambda e, b=bank, s=sl, cs=cs: e.tensor_scalar(
                            s[:, 0:128], b[:, 0:128], cs, None, ALU.mult),
                            [(self.sV, 1)])
                        self.wait("tensor", self.sV, self.sV.val)
                        emit_av(sl[:, 0:128], kk, 128, first_av[0], False, col0)
                        first_av[0] = False

            for iq in range(4):
                a = 4 * m + iq
                self.mm(PB[4][:, iq * 128:(iq + 1) * 128],
                        self.KV[kro:kro + 64, kpo + a * 128:kpo + (a + 1) * 128],
                        QS[0:64, iq * 128:(iq + 1) * 128],
                        start=(iq == 0), stop=(iq == 3),
                        inc=[(self.sT, 1)] if iq == 3 else ())
            qk_diag = self.sT.val
            self.wait("scalar", self.sT, self.sT.val)
            for iq in range(4):
                a = 4 * m + iq
                self.a(lambda e, iq=iq, sc=cgc(a): e.activation(
                    dd[:, iq * 128:(iq + 1) * 128], self.ONET[:, 0:128], AF.Copy,
                    scale=sc), [(self.sA, 1)] if iq == 3 else ())
            dd_done = self.sA.val
            self.wait("tensor", self.sA, dd_done)
            for iq in range(4):
                a = 4 * m + iq
                bank = PB[6 + iq % 2]
                self.wait("tensor", self.sV, self.sV.val)
                self.tr(bank[:, 0:128], dd[:, iq * 128:(iq + 1) * 128], 128,
                        inc=[(self.sT, 1)])
                self.wait("vector", self.sT, self.sT.val)
                self.v(lambda e, iq=iq, bank=bank, sc=cgc(a): e.tensor_scalar(
                    dd[:, iq * 128:(iq + 1) * 128], bank[:, 0:128], sc,
                    0.0, ALU.subtract, ALU.max), [(self.sV, 1)])
            self.wait("scalar", self.sV, self.sV.val)
            self.a(lambda e: e.activation(dd[:, 0:512], dd[:, 0:512], AF.Exp,
                                          scale=-1.0), [(self.sA, 1)])
            e2_done = self.sA.val
            emit_mask_ops(32)
            lam_acc_v, lam_acc_g = self.sV.val, self.sG.val
            self.wait("vector", self.sA, e2_done)
            self.wait("vector", self.sG, lam_acc_g)
            self.wait("vector", self.sV, lam_acc_v)
            self.v(lambda e: e.scalar_tensor_tensor(
                ff[:, 0:512], dd[:, 0:512], 0.125, ff[:, 0:512],
                ALU.mult, ALU.mult), [(self.sV, 1)])
            f_done = self.sV.val
            self._prev_f_done = f_done
            self.wait("vector", self.sT, qk_diag)
            sl0 = self.MISC[:, self.SCS0:self.SCS0 + 512]
            self.v(lambda e, sl0=sl0: e.tensor_copy(sl0, PB[4][:, 0:512]),
                   [(self.sV, 1)])
            scd_done = self.sV.val
            self.wait("tensor", self.sV, f_done)
            self.wait("vector", self.sV, scd_done)
            wd = self.MISC[:, self.SCS2:self.SCS2 + 512]
            for iq in range(4):
                bank = PB[6 + iq % 2]
                if iq >= 2:
                    self.wait("tensor", self.sV, self.sV.val)
                self.tr(bank[:, 0:128], ff[:, iq * 128:(iq + 1) * 128], 128,
                        inc=[(self.sT, 1)])
                self.wait("vector", self.sT, self.sT.val)
                self.v(lambda e, iq=iq, bank=bank, sl0=sl0, wd=wd: e.tensor_mul(
                    wd[:, iq * 128:(iq + 1) * 128], sl0[:, iq * 128:(iq + 1) * 128],
                    bank[:, 0:128]), [(self.sV, 1)])
                self.wait("tensor", self.sV, self.sV.val)
                emit_av(wd[:, iq * 128:(iq + 1) * 128], 4 * m + iq, 128,
                        first_av[0], iq == 3, iq * 128)
                first_av[0] = False
            av_all = self.sT.val
            self._prev_av_all = av_all
            if "p2d" in self.dbg and (h, m) == (1, 15):
                self.wait("sync", self.sT, av_all)
                self.wait("sync", self.sV, self.sV.val)
                self.tap_sbuf("dd", dd, (128, 512), F32)
                self.tap_sbuf("ff", ff, (128, 512), F32)
                self.tap_sbuf("sl0", sl0, (128, 512), BF16)
                self.tap_sbuf("wd", wd, (128, 512), BF16)
                self.tap_sbuf("qsd", self.QSTG[:, slot * 512:(slot + 1) * 512],
                              (128, 512), BF16)
                self.tap_sbuf("qf11", self.MISC[:, self.QF11:self.QF11 + 512],
                              (128, 512), BF16)
                self.tap_sbuf("qf10", self.MISC[:, self.QF10:self.QF10 + 512],
                              (128, 512), BF16)
                self.tap_sbuf("rsc", self.F32S[:, self.RSC:self.RSC + 12],
                              (128, 12), F32)
                self.tap_sbuf("csc", self.F32S[:, self.CSC:self.CSC + 60],
                              (128, 60), F32)
                self.wait("gpsimd", self.sDs, self.sDs.val)
                self.wait("vector", self.sDs, self.sDs.val)
                self.wait("scalar", self.sDs, self.sDs.val)
                self.wait("tensor", self.sDs, self.sDs.val)
            self.wait("vector", self.sT, av_all)
            if y_evac_done:
                self.wait("vector", self.sDc, self.y_dma_done)
            self.v(lambda e: e.tensor_copy(self.Y2[0:64, 0:512], PB[2][0:64, 0:512]),
                   [(self.sV, 1)])
            y_evac_done = self.sV.val
            self.wait("scalar", self.sV, y_evac_done)
            self.dma("scalar", self.Ya_d[h * 64:(h + 1) * 64, m * 512:(m + 1) * 512],
                     self.Y2[0:64, 0:512], self.sDc)
            self.y_dma_done = self.sDc.val
            self._qs_guard[slot] = (self.sT.val, self.sV.val)
            if m == NG - 1:
                self.wait("gpsimd", self.sDc, self.y_dma_done)
                self.on("gpsimd", lambda e, h=h: e.collective_compute(
                    "AllGather", ALU.bypass,
                    replica_groups=[[0, 1, 2, 3], [4, 5, 6, 7]],
                    ins=[self.Ya_d[h * 64:(h + 1) * 64, :].opt()],
                    outs=[self.Yg_d[h * 256:(h + 1) * 256, :].opt()]),
                    [(self.sCC, 1)])
                self.yg_head_done[h] = self.sCC.val
        self.yg_ready = self.sCC.val
        self.p2_y_evac_done = y_evac_done
        if "p2" in self.dbg:
            self.wait("sync", self.sDc, self.y_dma_done)
            self.tap_dram("Ya", self.Ya_d, (256, T), BF16)

    def sEng(self, eng):
        return {"vector": self.sV, "scalar": self.sA, "tensor": self.sT,
                "gpsimd": self.sG}[eng]

    def p3_collect_y(self, src, dst):
        self.wait("gpsimd", self.sDc, self.y_dma_done)
        self.on("gpsimd", lambda e, s=src, d=dst: e.collective_compute(
            "AllGather", ALU.bypass,
            replica_groups=[[0, 1, 2, 3], [4, 5, 6, 7]],
            ins=[s[:, :].opt()], outs=[d[:, :].opt()]), [(self.sCC, 1)])
        self.yg_ready = self.sCC.val

    def emit_wcache(self, wdram):
        casts = []
        for ch in range(8):
            for ot in range(4):
                casts.append((wdram[ch * 128:(ch + 1) * 128,
                                    ot * 256:(ot + 1) * 256],
                              (ch * 8 + ot * 2) * 128, 256))
        self.emit_casts(casts, self.KV)

    def p4_close_attn(self, kind):
        PB, W = self.PB, TGW
        xlb = lambda ch: self.MISC[:, ch * W:(ch + 1) * W]
        yst = lambda ch: self.MISC[:, 2048 + ch * W:2048 + (ch + 1) * W]
        obuf = lambda ch: self.F32S[:, ch * W:(ch + 1) * W]
        wt = lambda ch, ot: self.KV[:, (ch * 8 + ot) * 128:(ch * 8 + ot) * 128 + 128]
        tmpt = self.STG2[:, 0:W]

        self.wait("gpsimd", self.sV, self.sV.val)
        self.wait("gpsimd", self.sT, self.sT.val)
        self.emit_wcache(self.i["aWoutT"] if kind == "a" else self.i["bWoT"])
        wc_done = self.wcast_g

        qslot = {"v": None}

        yo_done = {}
        for hh in range(4):
            self.wait("sync", self.sCC, self.yg_head_done[hh])
            self.on("sync", lambda e, hh=hh: e.dma_start(
                out=self.Yo_d[hh * 256:(hh + 1) * 256, :],
                in_=self.Yg_d[hh * 256:(hh + 1) * 256,
                              bass.ts(self._q_sync, TOWN)]), [(self.sDs, 16)])
            yo_done[hh] = self.sDs.val
        self.wait("sync", self.sDs, self.sDs.val)

        def dyn_yg(e, ch, og):
            return e.dma_start(out=yst(ch),
                               in_=self.Yo_d[ch * 128:(ch + 1) * 128,
                                             og * W:(og + 1) * W])
        self.wait("tensor", self.sG, wc_done)
        prev_mm = 0
        for og in range(8):
            if kind == "a":
                if og > 0:
                    self.wait("tensor", self.sV, self.sV.val)
                    self.wait("vector", self.sT, prev_mm)
                self.emit_ln(self.emit_ln_pairs_from_R(og),
                             lambda pi: self.MISC[:, pi * 512:(pi + 1) * 512])
            if og > 0:
                self.wait("sync", self.sT, prev_mm)
            for ch in range(8):
                self.on("sync", lambda e, ch=ch, og=og: dyn_yg(e, ch, og),
                        [(self.sDs, 16)])
            yg_ld = self.sDs.val
            self.wait("sync", self.sDs, yg_ld)
            self.wait("tensor", self.sDs, yg_ld)
            if og > 0:
                self.wait("tensor", self.sV, self.sV.val)
            for oh in range(2):
                if oh == 1:
                    self.wait("tensor", self.sV, self.sV.val)
                for ch in range(8):
                    for ot in range(4):
                        self.mm(PB[ot][:, 0:W], wt(ch, oh * 4 + ot), yst(ch),
                                start=(ch == 0), stop=(ch == 7),
                                inc=[(self.sT, 1)] if (ch == 7 and ot == 3) else ())
                mmv = self.sT.val
                self.wait("vector", self.sT, mmv)
                if oh == 0:
                    self.wait("vector", self.sA, self.sA.val)
                    self.wait("vector", self.sV, self.sV.val)
                for ot in range(4):
                    och = oh * 4 + ot
                    if kind == "a":
                        self.v(lambda e, ot=ot, och=och: e.tensor_add(
                            obuf(och), PB[ot][:, 0:W], xlb(och)),
                            [(self.sV, 1)])
                    else:
                        self.wait("vector", self.sV, self.sV.val)
                        self.v(lambda e, ot=ot, och=och: e.tensor_scalar(
                            tmpt, PB[ot][:, 0:W], self.BBS[:, 15:16],
                            self.BBS[:, 4 + och:5 + och], ALU.mult, ALU.add),
                            [(self.sV, 1)])
                        self.wait("vector", self.sV, self.sV.val)
                        self.v(lambda e, och=och, og=og: e.tensor_add(
                            self.R[:, och * TOWN + og * W:
                                   och * TOWN + (og + 1) * W],
                            self.R[:, och * TOWN + og * W:
                                   och * TOWN + (og + 1) * W], tmpt),
                            [(self.sV, 1)])
            prev_mm = self.sT.val
            if kind == "a":
                ob_done = self.sV.val
                self.wait("tensor", self.sV, ob_done)
                self.wait("scalar", self.sV, ob_done)
                self.emit_ln_stats_only(
                    lambda pi: self.F32S[:, pi * 512:(pi + 1) * 512])
                r3 = self.R[:, :].rearrange("p (c t) -> p c t", c=8)
                p6f = PB[6][:, 0:512].rearrange("p (c t) -> p c t", c=2)
                p7f = PB[7][:, 0:512].rearrange("p (c t) -> p c t", c=2)
                tp2 = self.STG2[:, 0:512]
                tp3 = tp2.rearrange("p (c t) -> p c t", c=2)
                for pi in range(4):
                    if pi > 0:
                        self.wait("vector", self.sV, self.sV.val)
                    self.v(lambda e, pi=pi, tp3=tp3: e.tensor_mul(
                        tp3, self.F32S[:, pi * 512:(pi + 1) * 512]
                        .rearrange("p (c t) -> p c t", c=2), p6f), [(self.sV, 1)])
                    self.wait("vector", self.sV, self.sV.val)
                    self.v(lambda e, tp3=tp3: e.tensor_sub(tp3, tp3, p7f),
                           [(self.sV, 1)])
                    self.wait("vector", self.sV, self.sV.val)
                    for half in range(2):
                        ch = pi * 2 + half
                        self.v(lambda e, ch=ch, half=half, tp2=tp2: e.tensor_scalar(
                            tp2[:, half * W:(half + 1) * W],
                            tp2[:, half * W:(half + 1) * W],
                            self.LNG[:, ch:ch + 1],
                            self.LNG[:, 8 + ch:9 + ch], ALU.mult, ALU.add),
                            [(self.sV, 1)] if half == 1 else ())
                    self.wait("vector", self.sV, self.sV.val)
                    self.v(lambda e, pi=pi, og=og, tp3=tp3, r3=r3: e.tensor_add(
                        r3[:, 2 * pi:2 * pi + 2, og * W:(og + 1) * W],
                        r3[:, 2 * pi:2 * pi + 2, og * W:(og + 1) * W],
                        tp3), [(self.sV, 1)])
        self.close_done = self.sV.val

    def emit_ln_stats_only(self, src_pair):
        self.emit_ln(src_pair, None, affine="stats")

    def p5_ffn(self, idx):
        PB, W = self.PB, TGW
        i = self.i
        xlb = lambda ch: self.MISC[:, ch * W:(ch + 1) * W]

        if idx == 0:
            self.crl_d = self.nc.dram_tensor("crl_d", [16, 16384], BF16,
                                             kind="Internal").ap()
            for h in range(4):
                for rb in range(8):
                    r0 = rb * 8
                    self.wait("gpsimd", self.sV, self.sV.val)
                    self.dma("gpsimd",
                             self.STG[0:16, 0:128].rearrange("p (r i) -> p r i", r=8),
                             i["makerT"][h, r0:r0 + 8].rearrange("r j i -> j r i"),
                             self.sDg)
                    self.dma("gpsimd",
                             self.STG2[0:16, 0:512].rearrange("p (r k) -> p r k", r=8),
                             i["upT"][h, r0:r0 + 8].rearrange("r j k -> j r k"),
                             self.sDg)
                    dgv = self.sDg.val
                    self.wait("tensor", self.sDg, dgv)
                    if h + rb > 0:
                        self.wait("tensor", self.sV, self.sV.val)
                    for ri in range(8):
                        self.mm(PB[0][0:16, ri * 64:(ri + 1) * 64],
                                self.STG[0:16, ri * 16:(ri + 1) * 16],
                                self.STG2[0:16, ri * 64:(ri + 1) * 64],
                                start=(ri == 0), stop=(ri == 7),
                                inc=[(self.sT, 1)] if ri == 7 else ())
                    self.wait("vector", self.sT, self.sT.val)
                    if h + rb > 0:
                        self.wait("vector", self.sDc, self.sDc.val)
                    self.v(lambda e: e.tensor_copy(self.Y2[0:16, 0:512],
                                                   PB[0][0:16, 0:512]),
                           [(self.sV, 1)])
                    self.wait("scalar", self.sV, self.sV.val)
                    crl3 = self.crl_d.rearrange("p (c x) -> p c x", x=128)
                    c0_ = r0 * 2 + h // 2
                    dst = crl3[0:16, c0_:c0_ + 15:2,
                               (h % 2) * 64:(h % 2) * 64 + 64]
                    self.on("scalar", lambda e, dst=dst: e.dma_start(
                        out=dst, in_=self.Y2[0:16, 0:512]
                        .rearrange("p (r c) -> p r c", r=8)), [(self.sDc, 16)])
            self.crl_ready = self.sDc.val

        self.wait("tensor", self.sV, self.sV.val)
        self.wait("scalar", self.sV, self.sV.val)
        x2dma = 0
        for og in range(8):
            if og > 0:
                self.wait("tensor", self.sV, self.sV.val)
                self.wait("vector", self.sDs, x2dma)
            self.emit_ln(self.emit_ln_pairs_from_R(og),
                         lambda pi: self.MISC[:, pi * 512:(pi + 1) * 512])
            self.wait("sync", self.sV, self.sV.val)
            for ch in range(8):
                self.dma("sync", self.x2_d[ch * 128:(ch + 1) * 128,
                                           og * W:(og + 1) * W], xlb(ch), self.sDs)
            x2dma = self.sDs.val
            self.wait("sync", self.sDs, x2dma)
        x2_done = self.sDs.val

        self.wait("gpsimd", self.sT, self.sT.val)
        self.wait("gpsimd", self.sDs, x2_done)
        self.wait("gpsimd", self.sV, self.sV.val)
        self.wait("scalar", self.sDc, self.sDc.val)
        xr_slots = [self.XR, self.QSTG[:, 0:2048]]
        xr_sems = [self.sQ0, self.sQ1]
        xr_ld = {}
        x2r = self.x2_d.rearrange("(r i) t -> i r t", i=16)

        def xr_dma(it, guard):
            sp_, g_ = it // 8, it % 8
            slot_ = it % 2
            if guard is not None:
                self.wait("sync", self.sT, guard)
            for half in range(2):
                self.dma("sync",
                         xr_slots[slot_][0:16, half * 1024:(half + 1) * 1024],
                         x2r[0:16, sp_ * 8 + half * 4:sp_ * 8 + (half + 1) * 4,
                             g_ * W:(g_ + 1) * W], xr_sems[slot_])
            xr_ld[it] = xr_sems[slot_].val
        prev_rails = 0
        prev_radd = 0
        prev_gelu = [0, 0]
        for sp in range(8):
            slot = sp % 2
            if sp >= 2:
                self.wait("gpsimd", self.sT, self._ring_used[slot])
            dst = self.KV[:, slot * 16384:(slot + 1) * 16384] \
                .rearrange("p (c o) -> p c o", c=16)
            srcw = self.wp_bf[sp * 2048:(sp + 1) * 2048, :] \
                .rearrange("(c p) o -> p c o", p=128)
            if sp == 0:
                self.wait("gpsimd", self.sG, self.sG.val)
            self.dma("gpsimd", dst, srcw, self.sDg)
            ring_ld = self.sDg.val
            if sp > 0:
                self.wait("gpsimd", self.sT, prev_rails)
            if sp == 0 and idx == 0:
                self.wait("gpsimd", self.sDc, self.crl_ready)
            self.dma("gpsimd", self.CRL[0:16, 0:2048],
                     self.crl_d[0:16, sp * 2048:(sp + 1) * 2048], self.sDg)
            crl_ld = self.sDg.val
            self.wait("gpsimd", self.sDg, crl_ld)
            for g in range(8):
                it = sp * 8 + g
                slot_x = it % 2
                if it == 0:
                    xr_dma(0, self.sT.val)
                    xr_dma(1, None)
                elif it + 1 < 64:
                    xr_dma(it + 1, prev_rails)
                self.wait("tensor", xr_sems[slot_x], xr_ld[it])
                if g == 0:
                    self.wait("tensor", self.sDg, crl_ld)
                gelu_of = {}

                def emit_rails(c16):
                    zb = PB[c16 % 2]
                    if not (sp == 0 and g == 0 and c16 < 2):
                        self.wait("tensor", self.sA, prev_gelu[c16 % 2])
                    self.mm(zb[:, 0:W], self.CRL[0:16, c16 * 128:(c16 + 1) * 128],
                            xr_slots[slot_x][0:16, (c16 // 2) * 256:
                                             (c16 // 2) * 256 + 256],
                            start=True, stop=True, inc=[(self.sT, 1)])
                    self.wait("scalar", self.sT, self.sT.val)
                    hch = self.HCH[:, (c16 % 2) * 256:(c16 % 2) * 256 + 256]
                    self.a(lambda e, zb=zb, hch=hch: e.activation(
                        hch, zb[:, 0:W], AF.Gelu), [(self.sA, 1)])
                    prev_gelu[c16 % 2] = self.sA.val
                    gelu_of[c16] = (self.sA.val, hch)

                def emit_cproj(c16):
                    gv, hch = gelu_of[c16]
                    self.wait("tensor", self.sA, gv)
                    if c16 == 0 and not (sp == 0 and g == 0):
                        self.wait("tensor", self.sV, prev_radd)
                    for ot in range(8):
                        bank = PB[4 + ot // 2]
                        cols = (ot % 2) * 256
                        st_flag = (c16 == 0 and ot % 2 == 0)
                        sp_flag = (c16 == 15 and ot % 2 == 1)
                        self.mm(bank[:, cols:cols + W],
                                self.KV[:, slot * 16384 + c16 * 1024 + ot * 128:
                                        slot * 16384 + c16 * 1024 + (ot + 1) * 128],
                                hch, start=st_flag, stop=sp_flag,
                                inc=[(self.sT, 1)] if (c16 == 15 and ot == 7) else ())

                emit_rails(0)
                for c16 in range(16):
                    if c16 + 1 < 16:
                        emit_rails(c16 + 1)
                    emit_cproj(c16)
                prev_rails = self.sT.val
                self._ring_used = getattr(self, "_ring_used", [0, 0])
                self._ring_used[slot] = self.sT.val
                self.wait("vector", self.sT, self.sT.val)
                r3 = self.R[:, :].rearrange("p (c t) -> p c t", c=8)
                for pi in range(4):
                    bank3 = PB[4 + pi][:, 0:512].rearrange("p (c t) -> p c t", c=2)
                    self.v(lambda e, pi=pi, g=g, bank3=bank3, r3=r3: e.tensor_add(
                        r3[:, 2 * pi:2 * pi + 2, g * W:(g + 1) * W],
                        r3[:, 2 * pi:2 * pi + 2, g * W:(g + 1) * W], bank3),
                        [(self.sV, 1)] if pi == 3 else ())
                prev_radd = self.sV.val

    def p6_attn_b(self):
        PB, W = self.PB, TGW
        i = self.i
        xlb = lambda ch: self.MISC[:, ch * W:(ch + 1) * W]
        wsb = self.WSB

        self.wait("tensor", self.sV, self.sV.val)
        self.wait("scalar", self.sV, self.sV.val)
        c3dma = 0
        for og in range(8):
            if og > 0:
                self.wait("tensor", self.sV, self.sV.val)
                self.wait("vector", self.sDs, c3dma)
            self.emit_ln(self.emit_ln_pairs_from_R(og),
                         lambda pi: self.MISC[:, pi * 512:(pi + 1) * 512])
            self.wait("sync", self.sV, self.sV.val)
            for ch in range(8):
                self.dma("sync", self.c3_d[ch * 128:(ch + 1) * 128,
                                           og * W:(og + 1) * W], xlb(ch), self.sDs)
            c3dma = self.sDs.val
            self.wait("sync", self.sDs, c3dma)
        c3_done = self.sDs.val
        self.wait("gpsimd", self.sDs, c3_done)
        self.on("gpsimd", lambda e: e.collective_compute(
            "AllGather", ALU.bypass,
            replica_groups=[[0, 1, 2, 3], [4, 5, 6, 7]],
            ins=[self.c3_d[:, :].opt()], outs=[self.xg3_d[:, :].opt()]),
            [(self.sCC, 1)])
        xg3_ready = self.sCC.val

        casts = []
        for ch in range(8):
            casts += [(i["bWqT"][ch * 128:(ch + 1) * 128, :], self.wq_off + ch * 256, 256),
                      (i["bWkT"][ch * 128:(ch + 1) * 128, :], self.wk_off + ch * 256, 256),
                      (i["bWvT"][ch * 128:(ch + 1) * 128, :], self.wv_off + ch * 256, 256)]
        self.wait("gpsimd", self.sT, self.sT.val)
        self.emit_casts(casts, self.WSB)
        wb_done = self.wcast_g
        self.dma("sync", self.BBS[:, 0:2], i["bb_q"][:, :], self.sDs)
        self.dma("sync", self.BBS[:, 2:4], i["bb_k"][:, :], self.sDs)
        self.dma("sync", self.BBS[:, 4:12], i["bb_o"][:, :], self.sDs)
        self.dma("sync", self.STG[0:1, 0:1], i["bs_q"][:, :], self.sDs)
        self.dma("sync", self.STG[0:1, 1:2], i["bs_k"][:, :], self.sDs)
        self.dma("sync", self.STG[0:1, 2:3], i["bs_v"][:, :], self.sDs)
        self.dma("sync", self.STG[0:1, 3:4], i["bs_o"][:, :], self.sDs)
        self.dma("sync", self.STG[0:1, 4:260], i["bb_v"][:, :], self.sDs)
        sc_ld = self.sDs.val
        self.wait("sync", self.sDs, sc_ld)
        self.wait("tensor", self.sDs, sc_ld)
        self.wait("tensor", self.sV, self.sV.val)
        for k in range(4):
            self.mm(PB[0][:, k:k + 1], self.ONE1[:, :], self.STG[0:1, k:k + 1],
                    start=(k == 0), stop=(k == 3))
        self.mm(PB[1][:, 0:256], self.ONE1[:, :], self.STG[0:1, 4:260],
                inc=[(self.sT, 1)])
        self.wait("vector", self.sT, self.sT.val)
        self.v(lambda e: e.tensor_copy(self.BBS[:, 12:16], PB[0][:, 0:4]))
        self.v(lambda e: e.tensor_copy(self.BVB[:, 0:256], PB[1][:, 0:256]),
               [(self.sV, 1)])
        sc_done = self.sV.val

        self.v(lambda e: e.memset(
            self.KV[:, self.VOFF + 64:self.VOFF + 16640:65], 1.0), [(self.sV, 1)])

        self.wait("sync", self.sCC, xg3_ready)
        self.wait("tensor", self.sV, sc_done)
        self.wait("tensor", self.sG, wb_done)
        proj_done = {}
        v_evac_done = 0
        xlb2 = lambda s, ch: self.MISC[:, s * 2048 + ch * W:s * 2048 + (ch + 1) * W]
        xsem = [self.sQ0, self.sQ1]
        xg_ld = {}

        def xg_dma(tg_):
            s = tg_ % 2
            rblk, ccol = (tg_ // 8) * 1024, (tg_ % 8) * W
            if tg_ >= 2:
                self.wait("sync", self.sT, proj_done[tg_ - 2])
            for ch in range(8):
                self.dma("sync", xlb2(s, ch),
                         self.xg3_d[rblk + ch * 128:rblk + (ch + 1) * 128,
                                    ccol:ccol + W], xsem[s])
            xg_ld[tg_] = xsem[s].val
        xg_dma(0)
        xg_dma(1)
        for tg in range(NTG):
            s = tg % 2
            self.wait("tensor", xsem[s], xg_ld[tg])
            if tg > 0:
                self.wait("tensor", self.sV, v_evac_done)
            for ch in range(8):
                for p in range(2):
                    self.mm(PB[0 + p][:, 0:W],
                            wsb[:, self.wq_off + ch * 256 + p * 128:
                                self.wq_off + ch * 256 + (p + 1) * 128],
                            xlb2(s, ch), start=(ch == 0), stop=(ch == 7))
                    self.mm(PB[2 + p][:, 0:W],
                            wsb[:, self.wk_off + ch * 256 + p * 128:
                                self.wk_off + ch * 256 + (p + 1) * 128],
                            xlb2(s, ch), start=(ch == 0), stop=(ch == 7))
                for sub in range(2):
                    self.mm(PB[6 + sub][:, 0:256],
                            xlb2(s, ch)[:, sub * 128:(sub + 1) * 128],
                            wsb[:, self.wv_off + ch * 256:self.wv_off + (ch + 1) * 256],
                            start=(ch == 0), stop=(ch == 7),
                            inc=[(self.sT, 1)] if (ch == 7 and sub == 1) else ())
            proj_done[tg] = self.sT.val
            if tg + 2 < NTG:
                xg_dma(tg + 2)
            self.wait("vector", self.sT, proj_done[tg])
            self.wait("vector", self.sDs, self.sDs.val)
            for p in range(2):
                self.v(lambda e, p=p, tg=tg: e.tensor_scalar(
                    self.KV[:, p * T + tg * W:p * T + (tg + 1) * W],
                    PB[2 + p][:, 0:W], self.BBS[:, 13:14],
                    self.BBS[:, 2 + p:3 + p], ALU.mult, ALU.add))
                self.v(lambda e, p=p: e.tensor_scalar(
                    self.QSTG[:, p * W:(p + 1) * W], PB[0 + p][:, 0:W],
                    self.BBS[:, 12:13], self.BBS[:, 0 + p:1 + p],
                    ALU.mult, ALU.add),
                    [(self.sV, 1)] if p == 1 else ())
            qk_ev = self.sV.val
            self.wait("sync", self.sV, qk_ev)
            for p in range(2):
                self.dma("sync", self.Qa_d[p][:, tg * W:(tg + 1) * W],
                         self.QSTG[:, p * W:(p + 1) * W], self.sDs)
            self.q_dma_done = self.sDs.val
            for sub in range(2):
                j = tg * 2 + sub
                dst = self.KV[:, self.VOFF + j * 260:self.VOFF + (j + 1) * 260]
                dst3 = dst.rearrange("p (h c) -> p h c", h=4)[:, :, 0:64]
                src3 = PB[6 + sub][:, 0:256].rearrange("p (h c) -> p h c", h=4)
                bvb3 = self.BVB[:, 0:256].rearrange("p (h c) -> p h c", h=4)
                self.v(lambda e, d=dst3, s=src3, b=bvb3: e.scalar_tensor_tensor(
                    d, s, self.BBS[:, 14:15], b, ALU.mult, ALU.add),
                    [(self.sV, 1)] if sub == 1 else ())
            v_evac_done = self.sV.val

        self.wait("tensor", self.sV, self.sV.val)
        self.wait("sync", self.sDs, self.sDs.val)
        self.wait("vector", self.sDs, self.q_dma_done)
        self.v(lambda e: e.memset(self.QSTG[:, 0:4096], 0.0), [(self.sV, 1)])
        qz = self.sV.val
        SBK = [0, 1, 3, 5]
        YBK = [2, 4, 6, 7]
        hmg = [(h, mg) for h in range(4) for mg in range(4)]
        self.yg_head_done = {}
        qsem = [self.sQ0, self.sQ1]
        qs_guard = [0, 0]

        def qs_dma(idx2):
            h, mg = hmg[idx2]
            slot = idx2 % 2
            qrow = (h % 2) * 64
            self.wait("sync", self.sT, qs_guard[slot])
            self.dma("sync",
                     self.QSTG[qrow:qrow + 64, slot * 2048:(slot + 1) * 2048],
                     self.Qa_d[h // 2][qrow:qrow + 64, mg * 2048:(mg + 1) * 2048],
                     qsem[slot])
            return qsem[slot].val
        self.wait("sync", self.sV, qz)
        qs_done = {0: qs_dma(0)}
        self.wait("tensor", self.sV, qz)
        fin_v = 0
        scs = lambda k, mq: self.MISC[:, ((k % 2) * 4 + mq) * 512:
                                      ((k % 2) * 4 + mq) * 512 + 512]
        for idx2, (h, mg) in enumerate(hmg):
            slot = idx2 % 2
            qrow = (h % 2) * 64
            kpo = (h // 2) * T
            if idx2 + 1 < len(hmg):
                qs_done[idx2 + 1] = qs_dma(idx2 + 1)
            if mg == 0 and h > 0:
                qrow_old = ((h - 1) % 2) * 64
                self.wait("vector", self.sT, self.sT.val)
                self.v(lambda e, r=qrow_old: e.memset(
                    self.QSTG[r:r + 64, 0:4096], 0.0), [(self.sV, 1)])
                fin_v = self.sV.val
            QSL = self.QSTG[:, slot * 2048:(slot + 1) * 2048]
            self.wait("tensor", qsem[slot], qs_done[idx2])
            if fin_v:
                self.wait("tensor", self.sV, fin_v)
            evA = {}
            evB = {}
            for k in range(NT):
                if k > 0:
                    self.wait("tensor", self.sA, evA[k - 1])
                for mq in range(2):
                    self.mm(PB[SBK[mq]][:, 0:512],
                            self.KV[:, kpo + k * 128:kpo + (k + 1) * 128],
                            QSL[:, mq * 512:(mq + 1) * 512],
                            start=True, stop=True,
                            inc=[(self.sT, 1)] if mq == 1 else ())
                qka = self.sT.val
                if k > 0:
                    self.wait("tensor", self.sA, evB[k - 1])
                for mq in range(2, 4):
                    self.mm(PB[SBK[mq]][:, 0:512],
                            self.KV[:, kpo + k * 128:kpo + (k + 1) * 128],
                            QSL[:, mq * 512:(mq + 1) * 512],
                            start=True, stop=True,
                            inc=[(self.sT, 1)] if mq == 3 else ())
                qkb = self.sT.val
                if k > 0:
                    kp = k - 1
                    for mq in range(4):
                        self.mm(PB[YBK[mq]][0:65, 0:512],
                                self.KV[:, self.VOFF + kp * 260 + h * 65:
                                        self.VOFF + kp * 260 + h * 65 + 65],
                                scs(kp, mq), start=(kp == 0), stop=False)
                self.wait("scalar", self.sT, qka)
                for mq in range(2):
                    self.a(lambda e, b=PB[SBK[mq]], s=scs(k, mq):
                           e.activation(s, b[:, 0:512], AF.Exp, scale=0.125),
                           [(self.sA, 1)] if mq == 1 else ())
                evA[k] = self.sA.val
                self.wait("scalar", self.sT, qkb)
                for mq in range(2, 4):
                    self.a(lambda e, b=PB[SBK[mq]], s=scs(k, mq):
                           e.activation(s, b[:, 0:512], AF.Exp, scale=0.125),
                           [(self.sA, 1)] if mq == 3 else ())
                evB[k] = self.sA.val
            self.wait("tensor", self.sA, evB[NT - 1])
            for mq in range(4):
                self.mm(PB[YBK[mq]][0:65, 0:512],
                        self.KV[:, self.VOFF + (NT - 1) * 260 + h * 65:
                                self.VOFF + (NT - 1) * 260 + h * 65 + 65],
                        scs(NT - 1, mq), start=False, stop=True,
                        inc=[(self.sT, 1)] if mq == 3 else ())
            av_all = self.sT.val
            qs_guard[slot] = av_all
            ybuf = [self.Y2[0:64, 0:512], self.Y2[0:64, 512:1024],
                    self.HCH[0:64, 0:512], self.WPC[0:64, 0:512]]
            self.wait("vector", self.sT, av_all)
            self.wait("vector", self.sDc, self.sDc.val)
            for mq in range(4):
                self.v(lambda e, mq=mq: e.reciprocal(
                    self.F32S[0:1, mq * 512:(mq + 1) * 512],
                    PB[YBK[mq]][64:65, 0:512]),
                    [(self.sV, 1)] if mq == 3 else ())
            rec_done = self.sV.val
            self.wait("tensor", self.sV, rec_done)
            for mq in range(4):
                self.mm(PB[SBK[mq]][0:64, 0:512], self.ONE1[0:1, 0:64],
                        self.F32S[0:1, mq * 512:(mq + 1) * 512],
                        inc=[(self.sT, 1)] if mq == 3 else ())
            bc_done = self.sT.val
            self.wait("vector", self.sT, bc_done)
            cp_val = {}
            for mq in range(4):
                self.v(lambda e, mq=mq: e.tensor_copy(
                    self.F32S[0:64, 2048 + mq * 512:2048 + (mq + 1) * 512],
                    PB[SBK[mq]][0:64, 0:512]), [(self.sV, 1)])
                cp_val[mq] = self.sV.val
            for mq in range(4):
                self.wait("vector", self.sV, cp_val[mq])
                self.v(lambda e, mq=mq: e.tensor_mul(
                    ybuf[mq], PB[YBK[mq]][0:64, 0:512],
                    self.F32S[0:64, 2048 + mq * 512:2048 + (mq + 1) * 512]),
                    [(self.sV, 1)])
                mv = self.sV.val
                self.wait("scalar", self.sV, mv)
                self.dma("scalar",
                         self.Ya_d[h * 64:(h + 1) * 64,
                                   mg * 2048 + mq * 512:mg * 2048 + (mq + 1) * 512],
                         ybuf[mq], self.sDc)
            fin_v = self.sV.val
            self.y_dma_done = self.sDc.val
            if mg == 3:
                self.wait("gpsimd", self.sDc, self.y_dma_done)
                self.on("gpsimd", lambda e, h=h: e.collective_compute(
                    "AllGather", ALU.bypass,
                    replica_groups=[[0, 1, 2, 3], [4, 5, 6, 7]],
                    ins=[self.Ya_d[h * 64:(h + 1) * 64, :].opt()],
                    outs=[self.Yg_d[h * 256:(h + 1) * 256, :].opt()]),
                    [(self.sCC, 1)])
                self.yg_head_done[h] = self.sCC.val
        self.yg_ready = self.sCC.val
        if "p6" in self.dbg:
            self.wait("sync", self.sDc, self.y_dma_done)
            self.tap_dram("Yb", self.Ya_d, (256, T), BF16)
        self.p4_close_attn("b")

    def p7_out(self):
        self.wait("sync", self.sV, self.sV.val)
        self.wait("sync", self.sT, self.sT.val)
        self.wait("sync", self.sA, self.sA.val)
        for ch in range(8):
            self.dma("sync", self.out_d[ch * 128:(ch + 1) * 128, :],
                     self.R[:, ch * TOWN:(ch + 1) * TOWN], self.sDs)
        self.wait("sync", self.sDs, self.sDs.val)


def build_nc(phases=99, dbg=()):
    nc = bass.Bass("TRN2", target_bir_lowering=False, debug=False)
    mk = MK(nc, phases=phases, dbg=dbg)
    mk.build()
    return nc, mk





def kernel(**inputs):
    consts = make_consts()
    in_maps = []
    for c in range(8):
        m = host_prep(inputs, c)
        m.update(consts)
        in_maps.append(m)
    nc, _ = build_nc(phases=8)
    from concourse.bass_utils import run_bass_kernel_spmd
    res = run_bass_kernel_spmd(nc, in_maps, core_ids=list(range(8)))
    out = np.empty((2, T, D), dtype=np.float32)
    for c in range(8):
        b, q = c // 4, c % 4
        out[b, q * TOWN:(q + 1) * TOWN, :] = np.asarray(res.results[c]["out"]).T
    return out


# version 11 (final)
# speedup vs baseline: 1.1728x; 1.0177x over previous
import math
import numpy as np
from contextlib import ExitStack

import concourse.bass as bass
import concourse.mybir as mybir

F32 = mybir.dt.float32
BF16 = mybir.dt.bfloat16
AF = mybir.ActivationFunctionType
ALU = mybir.AluOpType

T = 8192
D = 1024
NT = 64
NG = 16
TOWN = 2048
TGW = 256
NTG = T // TGW
S = 64
LV = 12
LN8 = math.log(0.125)


def host_prep(inp, core):
    b, q = core // 4, core % 4
    hs = slice(4 * q * S, (4 * q + 4) * S)
    hl = slice(4 * q * LV, (4 * q + 4) * LV)
    hh = slice(4 * q, 4 * q + 4)
    C = np.ascontiguousarray
    A = lambda k: np.asarray(inp[k], dtype=np.float32)
    d = {}
    d["xT"] = C(A("L0_attn_a_x")[b].T)
    d["aWqT"] = C(A("L0_attn_a_Wq")[hs].T)
    d["aWkT"] = C(A("L0_attn_a_Wk")[hs].T)
    d["aWvT"] = C(A("L0_attn_a_Wv")[hs].T)
    d["aWdlT"] = C(A("L0_attn_a_Wdl")[hl].T)
    d["aWdtT"] = C(A("L0_attn_a_Wdt")[hh].T)
    hl_, r_, d_ = np.meshgrid(np.arange(4), np.arange(4), np.arange(64),
                              indexing="ij")
    dperm = ((r_ * 4 + hl_) * 64 + d_).ravel()
    d["aWoutT"] = C(A("L0_attn_a_Wout").T[dperm])
    d["aA_log"] = C(A("L0_attn_a_A_log")[hh].reshape(4, 1))
    d["adt_bias"] = C(A("L0_attn_a_dt_bias")[hh].reshape(4, 1))
    d["aLp"] = C(A("L0_attn_a_Lp")[hh].reshape(48, 1))
    d["aln_g"] = C(A("L0_attn_a_ln_g").reshape(8, 128).T)
    d["aln_b"] = C(A("L0_attn_a_ln_b").reshape(8, 128).T)
    for nm in ("q", "k", "v"):
        d[f"bW{nm}T"] = C(A(f"L0_attn_b_w_{nm}")[hs].T)
    d["bWoT"] = C(A("L0_attn_b_w_o").T[dperm])
    for nm in ("q", "k", "v", "o"):
        d[f"bs_{nm}"] = C(A(f"L0_attn_b_s_{nm}").reshape(1, 1))
    d["bb_q"] = C(A("L0_attn_b_b_q")[hs].reshape(2, 128).T)
    d["bb_k"] = C(A("L0_attn_b_b_k")[hs].reshape(2, 128).T)
    d["bb_v"] = C(A("L0_attn_b_b_v")[hs].reshape(1, 256))
    d["bb_o"] = C(A("L0_attn_b_b_o").reshape(8, 128).T)
    d["makerT"] = C(np.transpose(A("L0_ffn_rail_maker"), (0, 1, 3, 2)))
    d["upT"] = C(np.transpose(A("L0_ffn_rail_up_proj"), (0, 1, 3, 2)))
    h_, r_, k_ = np.meshgrid(np.arange(4), np.arange(64), np.arange(64), indexing="ij")
    old = h_ * 4096 + r_ * 64 + k_
    new = (r_ * 2 + h_ // 2) * 128 + (h_ % 2) * 64 + k_
    perm = np.empty(16384, dtype=np.int64)
    perm[new.ravel()] = old.ravel()
    d["wpT"] = C(A("L0_ffn_w_cproj")[:, perm].T)
    return d


def make_consts():
    d = {}
    d["ident"] = np.eye(128, dtype=np.float32)
    d["ones1"] = np.ones((1, 128), dtype=np.float32)
    d["onescol"] = np.ones((128, 1), dtype=np.float32)
    t = np.arange(128)
    xor = t[:, None] ^ t[None, :]
    lv = np.zeros((128, 128), dtype=np.int64)
    for k in range(LV - 1):
        lv += (xor >= (1 << k))
    causal = (t[None, :] <= t[:, None])
    mt = np.zeros((128, 8 * 128), dtype=np.float32)
    for l in range(8):
        mt[:, l * 128:(l + 1) * 128] = ((lv == l) & causal).astype(np.float32)
    import ml_dtypes
    d["maskT"] = mt.astype(ml_dtypes.bfloat16)
    return d


INPUT_SPECS = [
    ("xT", [D, T]), ("aWqT", [D, 256]), ("aWkT", [D, 256]), ("aWvT", [D, 256]),
    ("aWdlT", [D, 48]), ("aWdtT", [D, 4]), ("aWoutT", [D, D]),
    ("aA_log", [4, 1]), ("adt_bias", [4, 1]), ("aLp", [48, 1]),
    ("aln_g", [128, 8]), ("aln_b", [128, 8]),
    ("bWqT", [D, 256]), ("bWkT", [D, 256]), ("bWvT", [D, 256]), ("bWoT", [D, D]),
    ("bs_q", [1, 1]), ("bs_k", [1, 1]), ("bs_v", [1, 1]), ("bs_o", [1, 1]),
    ("bb_q", [128, 2]), ("bb_k", [128, 2]), ("bb_v", [1, 256]), ("bb_o", [128, 8]),
    ("makerT", [4, 64, 16, 16]), ("upT", [4, 64, 16, 64]), ("wpT", [16384, D]),
    ("ident", [128, 128]), ("ones1", [1, 128]), ("onescol", [128, 1]),
    ("maskT", [128, 8 * 128]),
]


class Sem:
    def __init__(self, h, name):
        self.h, self.name, self.val = h, name, 0


class MK:
    ENGS = ("sync", "tensor", "vector", "scalar", "gpsimd")

    def __init__(self, nc, phases=99, dbg=()):
        self.nc = nc
        self.phases = phases
        self.dbg = set(dbg)
        self.cmds = {e: [] for e in self.ENGS}

    def on(self, eng, fn, incs=()):
        for sem, n in incs:
            sem.val += n
        vals = [(sem.h, n) for sem, n in incs]

        def cmd(e, fn=fn, vals=vals):
            inst = fn(e)
            for h, n in vals:
                inst = inst.then_inc(h, n)
        self.cmds[eng].append(cmd)

    def wait(self, eng, sem, val=None):
        v = sem.val if val is None else val
        h = sem.h
        self.cmds[eng].append(lambda e, h=h, v=v: e.wait_ge(h, v))

    def dma(self, eng, out, in_, sem):
        self.on(eng, lambda e, out=out, in_=in_: e.dma_start(out=out, in_=in_),
                [(sem, 16)])

    def mm(self, out, lhsT, rhs, start=True, stop=True, inc=()):
        self.on("tensor", lambda e, o=out, l=lhsT, r=rhs, s1=start, s2=stop:
                e.matmul(o, l, r, start=s1, stop=s2), inc)

    def tr(self, out, in_, n, inc=()):
        idn = self.ident_ap[0:n, 0:n]
        self.on("tensor", lambda e, o=out, i=in_, d=idn: e.transpose(o, i, d), inc)

    def v(self, fn, inc=()):
        self.on("vector", fn, inc)

    def a(self, fn, inc=()):
        self.on("scalar", fn, inc)

    def g(self, fn, inc=()):
        self.on("gpsimd", fn, inc)

    def qsem(self, eng):
        return {"sync": self.sDs, "gpsimd": self.sDg, "scalar": self.sDc}[eng]

    def build(self):
        nc = self.nc
        st = self.stack = ExitStack()
        st.__enter__()
        dram = lambda name, shape, dt, kind="Internal": nc.dram_tensor(
            name, list(shape), dt, kind=kind).ap()
        self.i = {n: dram(n, s, BF16 if n == "maskT" else F32,
                          kind="ExternalInput") for n, s in INPUT_SPECS}
        self.out_d = dram("out", [D, TOWN], F32, kind="ExternalOutput")
        self.Qa_d = dram("Qa_d", [2, 128, T], BF16)
        self.Ya_d = dram("Ya_d", [256, T], BF16)
        self.Yg_d = dram("Yg_d", [4 * 256, T], BF16)
        self.x2_d = dram("x2_d", [D, TOWN], BF16)
        self.c3_d = dram("c3_d", [D, TOWN], BF16)
        self.xg3_d = dram("xg3_d", [4 * D, TOWN], BF16)
        self.wp_bf = dram("wp_bf", [16384, D], BF16)
        self.Yo_d = dram("Yo_d", [4 * 256, TOWN], BF16)

        sb = lambda name, shape, dt: st.enter_context(
            nc.sbuf_tensor(name, list(shape), dt))
        self.R = sb("R", [128, 8 * TOWN], F32)
        self.KV = sb("KV", [128, 33024], BF16)
        self.VOFF = 16384
        self.MISC = sb("MISC", [128, 5376], BF16)
        self.CRL = self.MISC[:, 0:2048]
        self.XR = self.MISC[:, 2048:4096]
        self.SCS0, self.QF11, self.QF10 = 2048, 3584, 4096
        self.QN8, self.QN9, self.SCS2 = 4608, 4736, 4864
        self.F32S = sb("F32S", [128, 5456], F32)
        self.DD0, self.FF0, self.MF = 0, 512, 1024
        self.CSC2, self.RSC2, self.MF2 = 1152, 1216, 1232
        self.TSC = 1408
        self.CGT, self.LAMT, self.CSC, self.RSC = 2048, 2304, 5376, 5440
        self.WSB = sb("WSB", [128, 6560], BF16)
        self.wq_off, self.wk_off, self.wv_off = 0, 2048, 4096
        self.wdl_off, self.wdt_off = 6144, 6528
        self.QSTG = sb("QSTG", [128, 4096], BF16)
        self.T48 = sb("T48", [48, TGW], F32)
        self.T48B = sb("T48B", [48, TGW], F32)
        self.T4 = sb("T4", [4, TGW], F32)
        self.T4B = sb("T4B", [4, TGW], F32)
        self.ONES4 = sb("ONES4", [4, TGW], F32)
        self.CARRY = sb("CARRY", [4, 1], F32)
        self.PP = sb("PP", [48, 1], F32)
        self.AA = sb("AA", [4, 2], F32)
        self.CONSTB = sb("CONSTB", [128, 1024], BF16)
        self.IDENT = sb("IDENT", [128, 128], F32)
        self.ONE1 = sb("ONE1", [1, 128], F32)
        self.ONEC = sb("ONEC", [128, 1], F32)
        self.ONET = sb("ONET", [128, 128], F32)
        self.LNG = sb("LNG", [128, 16], F32)
        self.BBS = sb("BBS", [128, 16], F32)
        self.BVB = sb("BVB", [128, 256], F32)
        self.STG = sb("STG", [128, 512], F32)
        self.STG2 = sb("STG2", [128, 512], F32)
        self.WPC = sb("WPC", [128, 512], BF16)
        self.Y2 = sb("Y2", [128, 1024], BF16)
        self.HCH = sb("HCH", [128, 2 * 256], BF16)
        self.RB = sb("RB", [128, 1024], F32)
        self.SCL = sb("SCL", [128, 4], F32)
        self.mu_r = self.T48B[0:1, 0:TGW]
        self.rstd_r = self.T4B[0:1, 0:TGW]
        self.nu_r = self.BVB[0:1, 0:TGW]
        self.tmp_r = self.STG[0:1, 0:TGW]

        self.ident_ap = self.IDENT[:, :]
        self.PB = [st.enter_context(nc.psum_tensor(f"pb{i}", [128, 512], F32))
                   for i in range(8)]
        for nm in ("sDs", "sDg", "sDc", "sT", "sV", "sA", "sG", "sCC",
                   "sQ0", "sQ1"):
            setattr(self, nm, Sem(st.enter_context(nc.semaphore(nm)), nm))

        self.build_program()

        with nc.Block() as block:
            for eng in self.ENGS:
                cmds = self.cmds[eng]

                def emit(e, cmds=cmds):
                    for c in cmds:
                        c(e)
                getattr(block, eng)(emit)
        st.__exit__(None, None, None)
        return nc

    def tap_dram(self, name, src, shape, dtype):
        t = self.nc.dram_tensor("dbg_" + name, list(shape), dtype,
                                kind="ExternalOutput").ap()
        p, f = shape
        for r0 in range(0, p, 128):
            rp = min(128, p - r0)
            for c0 in range(0, f, 512):
                w = min(512, f - c0)
                buf = self.Y2[0:rp, 0:w] if dtype == BF16 else self.STG[0:rp, 0:w]
                self.dma("sync", buf, src[r0:r0 + rp, c0:c0 + w], self.sDs)
                self.wait("sync", self.sDs)
                self.dma("sync", t[r0:r0 + rp, c0:c0 + w], buf, self.sDs)
                self.wait("sync", self.sDs)

    def tap_sbuf(self, name, src, shape, dtype):
        t = self.nc.dram_tensor("dbg_" + name, list(shape), dtype,
                                kind="ExternalOutput").ap()
        self.dma("sync", t[0:shape[0], 0:shape[1]], src, self.sDs)
        self.wait("sync", self.sDs)

    def build_program(self):
        self.p0_consts_weights()
        self.p1_xln1_proj()
        if self.phases >= 2:
            self.p2_attn_a()
        if self.phases >= 4:
            self.p4_close_attn(kind="a")
            if "p4" in self.dbg:
                self.wait("sync", self.sV, self.sV.val)
                self.tap_sbuf("R1", self.R[:, :], (128, 8 * TOWN), F32)
        if self.phases >= 5:
            self.p5_ffn(0)
            if "p5" in self.dbg:
                self.wait("sync", self.sV, self.sV.val)
                self.tap_sbuf("R2", self.R[:, :], (128, 8 * TOWN), F32)
        if self.phases >= 6:
            self.p6_attn_b()
        if self.phases >= 7:
            self.p5_ffn(1)
        if self.phases >= 8:
            self.p7_out()

    def p0_consts_weights(self):
        i = self.i

        sDs_h = self.sDs.h

        def r_init(e):
            q = e.partition_id() % 4
            self._q_sync = q
            inst = None
            for ch in range(8):
                inst = e.dma_start(
                    out=self.R[:, ch * TOWN:(ch + 1) * TOWN],
                    in_=i["xT"][ch * 128:(ch + 1) * 128, bass.ts(q, TOWN)])
                inst.then_inc(sDs_h, 16)
            return inst
        self.sDs.val += 16 * 8
        self.on("sync", r_init)
        self.r_init_done = self.sDs.val

        for dst, src in [(self.CONSTB[:, 0:1024], i["maskT"][:, :]),
                         (self.IDENT[:, :], i["ident"][:, :]),
                         (self.ONE1[:, :], i["ones1"][:, :]),
                         (self.ONEC[:, :], i["onescol"][:, :]),
                         (self.PP[:, :], i["aLp"][:, :]),
                         (self.AA[0:4, 1:2], i["adt_bias"][:, :]),
                         (self.STG[0:4, 0:1], i["aA_log"][:, :]),
                         (self.LNG[:, 0:8], i["aln_g"][:, :]),
                         (self.LNG[:, 8:16], i["aln_b"][:, :])]:
            self.dma("sync", dst, src, self.sDs)
        const_done = self.const_done = self.sDs.val
        self.wait("sync", self.sDs, const_done)
        self.wait("vector", self.sDs, const_done)
        self.v(lambda e: e.memset(self.ONET[:, :], 1.0))
        self.v(lambda e: e.memset(self.ONES4[:, :], 1.0))
        self.v(lambda e: e.memset(self.CARRY[:, :], 0.0), [(self.sV, 1)])
        self.wait("scalar", self.sDs, const_done)
        self.a(lambda e: e.activation(self.AA[0:4, 0:1], self.STG[0:4, 0:1], AF.Exp),
               [(self.sA, 1)])
        self.wait("vector", self.sA)
        self.v(lambda e: e.tensor_scalar(self.AA[0:4, 0:1], self.AA[0:4, 0:1],
                                         -1.0, None, ALU.mult), [(self.sV, 1)])
        self.consts_ready = self.sV.val

        casts = []
        for ch in range(8):
            casts += [(i["aWqT"][ch * 128:(ch + 1) * 128, :], self.wq_off + ch * 256, 256),
                      (i["aWkT"][ch * 128:(ch + 1) * 128, :], self.wk_off + ch * 256, 256),
                      (i["aWvT"][ch * 128:(ch + 1) * 128, :], self.wv_off + ch * 256, 256),
                      (i["aWdlT"][ch * 128:(ch + 1) * 128, :], self.wdl_off + ch * 48, 48),
                      (i["aWdtT"][ch * 128:(ch + 1) * 128, :], self.wdt_off + ch * 4, 4)]
        self.emit_casts(casts, self.WSB)
        self.w_cast_done = self.wcast_g

    def emit_casts(self, casts, dstbuf):
        pos, group, flushes = 0, [], []
        for src, off, w in casts:
            if pos + w > 512:
                flushes.append(group)
                group, pos = [], 0
            group.append((src, off, w, pos))
            pos += w
        if group:
            flushes.append(group)
        for gi, grp in enumerate(flushes):
            if gi >= 1:
                self.wait("gpsimd", self.sG, self.sG.val)
            for src, off, w, p0 in grp:
                self.dma("gpsimd", self.STG2[:, p0:p0 + w], src, self.sDg)
            self.wait("gpsimd", self.sDg, self.sDg.val)
            for k, (src, off, w, p0) in enumerate(grp):
                self.g(lambda e, off=off, w=w, p0=p0, d=dstbuf:
                       e.tensor_copy(d[:, off:off + w], self.STG2[:, p0:p0 + w]),
                       [(self.sG, 1)])
            self.wcast_g = self.sG.val

    def emit_wp_prep(self):
        it = self._wp_it
        self._wp_it += 1
        i = self.i
        for half in range(2):
            stg = self.STG2[:, 0:512]
            wpc = self.WPC[:, 0:512]
            if self._wp_guard[0] is not None:
                self.wait("gpsimd", self.sG, self._wp_guard[0])
            elif getattr(self, "wcast_g", 0):
                self.wait("gpsimd", self.sG, self.wcast_g)
            if self._wp_outg[0] is not None:
                self.wait("gpsimd", self.sDg, self._wp_outg[0])
            self.dma("gpsimd", stg,
                     i["wpT"][it * 128:(it + 1) * 128, half * 512:(half + 1) * 512],
                     self.sDg)
            self.wait("gpsimd", self.sDg, self.sDg.val)
            self.g(lambda e, wpc=wpc, stg=stg: e.tensor_copy(wpc, stg),
                   [(self.sG, 1)])
            self._wp_guard[0] = self.sG.val
            self.wait("gpsimd", self.sG, self.sG.val)
            self.dma("gpsimd",
                     self.wp_bf[it * 128:(it + 1) * 128, half * 512:(half + 1) * 512],
                     wpc, self.sDg)
            self._wp_outg[0] = self.sDg.val

    def emit_ln(self, src_pair, dst_pair, affine=None):
        PB, W = self.PB, TGW
        mu, rstd, nu, tmp = self.mu_r, self.rstd_r, self.nu_r, self.tmp_r
        p6f = PB[6][:, 0:2 * W].rearrange("p (c t) -> p c t", c=2)
        p7f = PB[7][:, 0:2 * W].rearrange("p (c t) -> p c t", c=2)
        self.wait("scalar", self.sV, self.sV.val)
        mmdone = []
        for pi in range(4):
            rb = self.RB[:, (pi % 2) * 2 * W:(pi % 2) * 2 * W + 2 * W]
            rb3 = rb.rearrange("p (c t) -> p c t", c=2)
            if pi >= 2:
                self.wait("scalar", self.sT, mmdone[pi - 2])
            self.a(lambda e, s=src_pair(pi), rb3=rb3: e.activation(
                rb3, s, AF.Square), [(self.sA, 1)])
            self.wait("tensor", self.sA, self.sA.val)
            self.mm(PB[6][0:1, 0:2 * W], self.ONEC[:, :], src_pair(pi),
                    start=(pi == 0), stop=(pi == 3))
            self.mm(PB[7][0:1, 0:2 * W], self.ONEC[:, :], rb3,
                    start=(pi == 0), stop=(pi == 3), inc=[(self.sT, 1)])
            mmdone.append(self.sT.val)
        self.wait("vector", self.sT, self.sT.val)
        self.v(lambda e: e.tensor_scalar(mu, PB[6][0:1, 0:W], 1.0 / D, None,
                                         ALU.mult))
        self.v(lambda e: e.tensor_scalar(tmp, PB[7][0:1, 0:W], 1.0 / D, None,
                                         ALU.mult), [(self.sV, 1)])
        self.wait("vector", self.sV, self.sV.val)
        self.v(lambda e: e.scalar_tensor_tensor(mu, PB[6][0:1, W:2 * W], 1.0 / D,
                                                mu, ALU.mult, ALU.add))
        self.v(lambda e: e.scalar_tensor_tensor(tmp, PB[7][0:1, W:2 * W], 1.0 / D,
                                                tmp, ALU.mult, ALU.add),
               [(self.sV, 1)])
        self.wait("scalar", self.sV, self.sV.val)
        self.a(lambda e: e.activation(self.RB[0:1, 0:W], mu, AF.Square), [(self.sA, 1)])
        self.wait("vector", self.sA, self.sA.val)
        self.wait("vector", self.sV, self.sV.val)
        self.v(lambda e: e.scalar_tensor_tensor(tmp, tmp, 1e-5, self.RB[0:1, 0:W],
                                                ALU.add, ALU.subtract), [(self.sV, 1)])
        self.wait("scalar", self.sV, self.sV.val)
        self.a(lambda e: e.activation(tmp, tmp, AF.Ln), [(self.sA, 1)])
        self.wait("scalar", self.sA, self.sA.val)
        self.a(lambda e: e.activation(rstd, tmp, AF.Exp, scale=-0.5), [(self.sA, 1)])
        self.wait("vector", self.sA, self.sA.val)
        self.v(lambda e: e.tensor_mul(nu, mu, rstd), [(self.sV, 1)])
        self.wait("tensor", self.sV, self.sV.val)
        self.mm(PB[6][:, 0:W], self.ONE1[:, :], rstd)
        self.mm(PB[6][:, W:2 * W], self.ONE1[:, :], rstd)
        self.mm(PB[7][:, 0:W], self.ONE1[:, :], nu)
        self.mm(PB[7][:, W:2 * W], self.ONE1[:, :], nu, inc=[(self.sT, 1)])
        self.wait("vector", self.sT, self.sT.val)
        if affine == "stats":
            return
        tmpt = self.RB[:, 0:2 * W]
        tmpt3 = tmpt.rearrange("p (c t) -> p c t", c=2)
        for pi in range(4):
            if pi > 0:
                self.wait("vector", self.sV, self.sV.val)
            self.v(lambda e, s=src_pair(pi), t=tmpt3: e.tensor_mul(t, s, p6f),
                   [(self.sV, 1)])
            self.wait("vector", self.sV, self.sV.val)
            self.v(lambda e, t=tmpt3, d=dst_pair(pi): e.tensor_sub(d, t, p7f),
                   [(self.sV, 1)])

    def emit_ln_pairs_from_R(self, og):
        W = TGW
        r3 = self.R[:, :].rearrange("p (c t) -> p c t", c=8)
        return lambda pi: r3[:, 2 * pi:2 * pi + 2, og * W:(og + 1) * W]

    def emit_softplus(self, x, tmp):
        self.wait("vector", self.sV, self.sV.val)
        self.v(lambda e: e.tensor_scalar(tmp, x, -1.0, None, ALU.mult), [(self.sV, 1)])
        self.wait("vector", self.sV, self.sV.val)
        self.v(lambda e: e.tensor_max(tmp, x, tmp), [(self.sV, 1)])
        self.wait("scalar", self.sV, self.sV.val)
        self.a(lambda e: e.activation(tmp, tmp, AF.Exp, scale=-1.0), [(self.sA, 1)])
        self.wait("scalar", self.sA, self.sA.val)
        self.a(lambda e: e.activation(tmp, tmp, AF.Ln, bias=1.0), [(self.sA, 1)])
        self.wait("vector", self.sA, self.sA.val)
        self.v(lambda e: e.tensor_scalar(x, x, 0.0, None, ALU.max), [(self.sV, 1)])
        self.wait("vector", self.sV, self.sV.val)
        self.v(lambda e: e.tensor_add(x, x, tmp), [(self.sV, 1)])

    def p1_xln1_proj(self):
        i, PB = self.i, self.PB
        W = TGW
        xt = lambda ch: self.F32S[:, ch * W:(ch + 1) * W]
        xlb = lambda ch: self.MISC[:, ch * W:(ch + 1) * W]
        wsb = self.WSB
        self._wp_it, self._wp_guard, self._wp_outg = 0, [None, None], [None, None]
        self.wait("tensor", self.sG, self.w_cast_done)
        self.v(lambda e: e.memset(
            self.KV[:, self.VOFF + 64:self.VOFF + 16640:65], 1.0), [(self.sV, 1)])
        self.wait("tensor", self.sV, self.consts_ready)
        proj_done = 0
        v_evac_done = 0
        lamcg_done = 0
        for tg in range(NTG):
            for ch in range(8):
                if tg > 0 and ch == 0:
                    self.wait("sync", self.sV, self.xln_consumed)
                self.dma("sync", xt(ch), i["xT"][ch * 128:(ch + 1) * 128,
                                                 tg * W:(tg + 1) * W], self.sDs)
            self.wait("scalar", self.sDs, self.sDs.val)
            if tg > 0:
                self.wait("tensor", self.sV, v_evac_done)
                self.wait("vector", self.sT, proj_done)
            self.emit_ln(lambda pi: self.F32S[:, pi * 512:(pi + 1) * 512],
                         lambda pi: self.MISC[:, pi * 512:(pi + 1) * 512])
            self.xln_consumed = self.sV.val
            xln_done = self.sV.val

            self.wait("tensor", self.sV, xln_done)
            if tg > 0:
                self.wait("tensor", self.sV, self.qk_evac_done)
                self.wait("tensor", self.sV, lamcg_done)
            for ch in range(8):
                for p in range(2):
                    self.mm(PB[0 + p][:, 0:W],
                            wsb[:, self.wq_off + ch * 256 + p * 128:
                                self.wq_off + ch * 256 + (p + 1) * 128],
                            xlb(ch), start=(ch == 0), stop=(ch == 7))
                    self.mm(PB[2 + p][:, 0:W],
                            wsb[:, self.wk_off + ch * 256 + p * 128:
                                self.wk_off + ch * 256 + (p + 1) * 128],
                            xlb(ch), start=(ch == 0), stop=(ch == 7))
                self.mm(PB[4][0:48, 0:W],
                        wsb[:, self.wdl_off + ch * 48:self.wdl_off + (ch + 1) * 48],
                        xlb(ch), start=(ch == 0), stop=(ch == 7))
                self.mm(PB[5][0:4, 0:W],
                        wsb[:, self.wdt_off + ch * 4:self.wdt_off + (ch + 1) * 4],
                        xlb(ch), start=(ch == 0), stop=(ch == 7))
                for sub in range(2):
                    self.mm(PB[6 + sub][:, 0:256],
                            xlb(ch)[:, sub * 128:(sub + 1) * 128],
                            wsb[:, self.wv_off + ch * 256:self.wv_off + (ch + 1) * 256],
                            start=(ch == 0), stop=(ch == 7),
                            inc=[(self.sT, 1)] if (ch == 7 and sub == 1) else ())
            proj_done = self.sT.val

            self.wait("vector", self.sT, proj_done)
            for p in range(2):
                self.v(lambda e, p=p, tg=tg: e.tensor_copy(
                    self.KV[:, p * T + tg * W:p * T + (tg + 1) * W],
                    PB[2 + p][:, 0:W]))
                self.v(lambda e, p=p: e.tensor_copy(
                    self.QSTG[:, p * W:(p + 1) * W], PB[0 + p][:, 0:W]),
                    [(self.sV, 1)] if p == 1 else ())
            self.qk_evac_done = self.sV.val
            self.wait("sync", self.sV, self.qk_evac_done)
            for p in range(2):
                self.dma("sync", self.Qa_d[p][:, tg * W:(tg + 1) * W],
                         self.QSTG[:, p * W:(p + 1) * W], self.sDs)
            self.q_dma_done = self.sDs.val
            for sub in range(2):
                j = tg * 2 + sub
                dst = self.KV[:, self.VOFF + j * 260:self.VOFF + (j + 1) * 260]
                dst3 = dst.rearrange("p (h c) -> p h c", h=4)[:, :, 0:64]
                src3 = PB[6 + sub][:, 0:256] \
                    .rearrange("p (h c) -> p h c", h=4)
                self.v(lambda e, d=dst3, s=src3: e.tensor_copy(d, s),
                       [(self.sV, 1)] if sub == 1 else ())
            v_evac_done = self.sV.val

            self.v(lambda e: e.tensor_scalar(self.T48[:, :], PB[4][0:48, 0:W],
                                             self.PP[0:48, 0:1], None, ALU.mult))
            self.v(lambda e: e.tensor_scalar(self.T4[:, :], PB[5][0:4, 0:W],
                                             self.AA[0:4, 1:2], None, ALU.add),
                   [(self.sV, 1)])
            self.emit_softplus(self.T48[:, :], self.T48B[:, :])
            self.emit_softplus(self.T4[:, :], self.T4B[:, :])
            self.wait("vector", self.sV, self.sV.val)
            self.v(lambda e: e.tensor_scalar(self.T4[:, :], self.T4[:, :],
                                             self.AA[0:4, 0:1], None, ALU.mult),
                   [(self.sV, 1)])
            self.wait("vector", self.sV, self.sV.val)
            self.v(lambda e: e.tensor_tensor_scan(
                self.T4B[:, :], self.ONES4[:, :], self.T4[:, :],
                self.CARRY[0:4, 0:1], ALU.mult, ALU.add), [(self.sV, 1)])
            self.wait("vector", self.sV, self.sV.val)
            self.v(lambda e: e.tensor_copy(self.CARRY[0:4, 0:1],
                                           self.T4B[0:4, W - 1:W]), [(self.sV, 1)])
            cg_done = self.sV.val

            self.wait("tensor", self.sV, cg_done)
            for sub in range(2):
                j = tg * 2 + sub
                self.tr(PB[4][:, 0:48], self.T48[:, sub * 128:(sub + 1) * 128], 48,
                        inc=[(self.sT, 1)])
                t_lam = self.sT.val
                self.tr(PB[5][:, 0:4], self.T4B[:, sub * 128:(sub + 1) * 128], 4,
                        inc=[(self.sT, 1)])
                t_cg = self.sT.val
                self.wait("vector", self.sT, t_lam)
                dlam = self.F32S[:, self.LAMT + j:self.LAMT + j + 48 * 64:64]
                self.v(lambda e, d=dlam: e.tensor_copy(d, PB[4][:, 0:48]))
                self.wait("vector", self.sT, t_cg)
                dcg = self.F32S[:, self.CGT + j:self.CGT + j + 4 * 64:64]
                self.v(lambda e, d=dcg: e.tensor_copy(d, PB[5][:, 0:4]),
                       [(self.sV, 1)])
                if sub == 0:
                    self.wait("tensor", self.sV, self.sV.val)
            lamcg_done = self.sV.val

            for _ in range(4):
                if self._wp_it < 128:
                    self.emit_wp_prep()

        self.p1_done = dict(sV=self.sV.val, sT=self.sT.val, sDs=self.sDs.val)
        if "p1" in self.dbg:
            self.wait("sync", self.sV, self.sV.val)
            self.tap_sbuf("K0", self.KV[:, 0:T], (128, T), BF16)
            self.tap_sbuf("K1", self.KV[:, T:2 * T], (128, T), BF16)
            self.tap_sbuf("Vtm", self.KV[:, self.VOFF:self.VOFF + 64 * 260],
                          (128, 64 * 260), BF16)
            self.tap_sbuf("cgT", self.F32S[:, self.CGT:self.CGT + 256], (128, 256), F32)
            self.tap_sbuf("LamT", self.F32S[:, self.LAMT:self.LAMT + 3072],
                          (128, 3072), F32)
            self.tap_dram("Qa0", self.Qa_d[0], (128, T), BF16)
            self.tap_dram("Qa1", self.Qa_d[1], (128, T), BF16)

    def p2_attn_a(self):
        PB = self.PB
        while self._wp_it < 128:
            self.emit_wp_prep()

        hm = [(h, m) for h in range(4) for m in range(NG)]
        self.yg_head_done = {}
        self.v(lambda e: e.memset(self.MISC[:, 3584:4864], 0.0), [(self.sV, 1)])
        self.wait("gpsimd", self.sV, self.p1_done["sV"])
        self.wait("gpsimd", self.sDs, self.const_done)
        self.wait("sync", self.sDs, self.sDs.val)
        self._qs_guard = [None, None]
        qsem = [self.sQ0, self.sQ1]

        def qs_dma(idx):
            h, m = hm[idx]
            slot = idx % 2
            if self._qs_guard[slot] is not None:
                tval, vval = self._qs_guard[slot]
                self.wait("sync", self.sT, tval)
                self.wait("sync", self.sV, vval)
            self.dma("sync", self.QSTG[:, slot * 512:(slot + 1) * 512],
                     self.Qa_d[h // 2][:, m * 512:(m + 1) * 512], qsem[slot])
            return qsem[slot].val
        qs_done = {0: qs_dma(0)}

        y_evac_done = 0
        self.v(lambda e: e.memset(self.QSTG[:, 1024:2048], 0.0), [(self.sV, 1)])
        qf_sets = [
            (self.MISC[:, self.QF11:self.QF11 + 512],
             self.MISC[:, self.QF10:self.QF10 + 512]),
            (self.QSTG[:, 1024:1536], self.QSTG[:, 1536:2048]),
        ]
        CSCS = [self.CSC, self.CSC2]
        RSCS = [self.RSC, self.RSC2]
        MFS = [self.MF, self.MF2]
        prep_qf_done = {}

        def prep(idx):
            h, m = hm[idx]
            slot = idx % 2
            qrow = (h % 2) * 64
            QS = self.QSTG[qrow:qrow + 64, slot * 512:(slot + 1) * 512]
            bsel = idx % 2
            csc, rsc, mf = CSCS[bsel], RSCS[bsel], MFS[bsel]
            qf11_ap, qf10_ap = qf_sets[bsel]
            cgcols = lambda j0, n: self.F32S[:, self.CGT + h * 64 + j0:
                                             self.CGT + h * 64 + j0 + n]
            lamcols = lambda l, j0, n: self.F32S[:, self.LAMT + (h * 12 + l) * 64
                                                 + j0:
                                                 self.LAMT + (h * 12 + l) * 64
                                                 + j0 + n]
            if m in (0, 1) and h > 0:
                qrow_old = ((h - 1) % 2) * 64
                self.wait("vector", self.sT, self.sT.val)
                if bsel == 0:
                    self.v(lambda e, r=qrow_old: e.memset(
                        self.MISC[r:r + 64, 3584:4608], 0.0), [(self.sV, 1)])
                else:
                    self.v(lambda e, r=qrow_old: e.memset(
                        self.QSTG[r:r + 64, 1024:2048], 0.0), [(self.sV, 1)])
            self.wait("vector", qsem[slot], qs_done[idx])
            self.wait("tensor", self.sV, self.sV.val)
            self.wait("tensor", self.sA, self.sA.val)
            self.mm(PB[6][:, 0:1], self.ONE1[:, :],
                    self.F32S[0:1, self.CGT + h * 64 + 4 * m:
                              self.CGT + h * 64 + 4 * m + 1], inc=[(self.sT, 1)])
            self.wait("vector", self.sT, self.sT.val)
            self.v(lambda e: e.tensor_copy(self.SCL[:, 0:1], PB[6][:, 0:1]))
            self.v(lambda e: e.tensor_scalar(self.SCL[:, 1:2], PB[6][:, 0:1],
                                             -1.0, None, ALU.mult),
                   [(self.sV, 1)])
            scl_done = self.sV.val
            self.wait("scalar", self.sV, scl_done)
            if m > 0:
                self.a(lambda e, n=4 * m, src_=cgcols(0, 4 * m), csc=csc:
                       e.activation(
                    self.F32S[:, csc:csc + n], src_,
                    AF.Exp, scale=-1.0, bias=self.SCL[:, 0:1]), [(self.sA, 1)])
            self.a(lambda e, src_=cgcols(4 * m, 4), rsc=rsc: e.activation(
                self.F32S[:, rsc:rsc + 4], src_,
                AF.Exp, bias=self.SCL[:, 1:2]), [(self.sA, 1)])
            self.wait("vector", self.sA, self.sA.val)
            if m > 0:
                self.v(lambda e, src_=lamcols(11, 4 * m, 4), rsc=rsc:
                       e.scalar_tensor_tensor(
                    self.F32S[:, rsc + 4:rsc + 8], src_,
                    0.125, self.F32S[:, rsc:rsc + 4], ALU.mult, ALU.mult),
                    [(self.sV, 1)])
            if m % 2 == 1:
                self.v(lambda e, src_=lamcols(10, 4 * m, 4), rsc=rsc:
                       e.scalar_tensor_tensor(
                    self.F32S[:, rsc + 8:rsc + 12], src_,
                    0.125, self.F32S[:, rsc:rsc + 4], ALU.mult, ALU.mult),
                    [(self.sV, 1)])
            rs_done = self.sV.val
            if m > 0:
                self.wait("scalar", self.sV, rs_done)
                rounds = [(qf11_ap, rsc + 4)]
                if m % 2 == 1:
                    rounds.append((qf10_ap, rsc + 8))
                bct = self.F32S[:, mf:mf + 128]
                bank_guard = {6: scl_done, 7: scl_done}
                rno = 0
                for qf, sc0 in rounds:
                    for jp in range(2):
                        if rno > 0:
                            self.wait("scalar", self.sT, self.sT.val)
                        for jj in range(2):
                            self.a(lambda e, jj=jj, bct=bct,
                                   c=self.F32S[:, sc0 + 2 * jp + jj:
                                               sc0 + 2 * jp + jj + 1]:
                                   e.activation(bct[:, jj * 64:(jj + 1) * 64],
                                                self.ONET[:, 0:64], AF.Copy,
                                                scale=c),
                                   [(self.sA, 1)] if jj == 1 else ())
                        self.wait("tensor", self.sA, self.sA.val)
                        bno = 6 + rno % 2
                        bank = PB[bno]
                        self.wait("tensor", self.sV, bank_guard[bno])
                        self.tr(bank[:, 0:128], bct[:, 0:128], 128,
                                inc=[(self.sT, 1)])
                        self.wait("vector", self.sT, self.sT.val)
                        for jj in range(2):
                            j = 2 * jp + jj
                            self.v(lambda e, j=j, jj=jj, qf=qf, bank=bank,
                                   QS=QS, qrow=qrow: e.tensor_mul(
                                qf[qrow:qrow + 64, j * 128:(j + 1) * 128],
                                QS[0:64, j * 128:(j + 1) * 128],
                                bank[jj * 64:(jj + 1) * 64, 0:128]),
                                [(self.sV, 1)] if jj == 1 else ())
                        bank_guard[bno] = self.sV.val
                        rno += 1
            prep_qf_done[idx] = self.sV.val

        prep(0)
        for idx, (h, m) in enumerate(hm):
            slot = idx % 2
            if idx + 1 < len(hm):
                qs_done[idx + 1] = qs_dma(idx + 1)
                prep(idx + 1)
            qrow = (h % 2) * 64
            QS = self.QSTG[qrow:qrow + 64, slot * 512:(slot + 1) * 512]
            kro = (h % 2) * 64
            kpo = (h // 2) * T
            csc, rsc, mf = CSCS[idx % 2], RSCS[idx % 2], MFS[idx % 2]
            qf11_mov, qf10_mov = qf_sets[idx % 2]
            if m == 0 and h > 0:
                qrow_old = ((h - 1) % 2) * 64
                self.wait("vector", self.sT, self.sT.val)
                self.v(lambda e, r=qrow_old: e.memset(
                    self.MISC[r:r + 64, 4608:4864], 0.0), [(self.sV, 1)])
            cgc = lambda j: self.F32S[:, self.CGT + h * 64 + j:
                                      self.CGT + h * 64 + j + 1]
            cgcols = lambda j0, n: self.F32S[:, self.CGT + h * 64 + j0:
                                             self.CGT + h * 64 + j0 + n]
            lamc = lambda l, j: self.F32S[:, self.LAMT + (h * 12 + l) * 64 + j:
                                          self.LAMT + (h * 12 + l) * 64 + j + 1]

            dd = self.F32S[:, self.DD0:self.DD0 + 512]
            ff = self.F32S[:, self.FF0:self.FF0 + 512]
            if getattr(self, "_prev_av_all", None):
                self.wait("vector", self.sT, self._prev_av_all)
            _val_iq = [None] * 4
            _mask_ops = [(l, iq) for l in range(8) for iq in range(4)]

            def emit_mask_ops(n):
                for _ in range(min(n, len(_mask_ops))):
                    l, iq = _mask_ops.pop(0)
                    a_ = 4 * m + iq
                    tgt = ff[:, iq * 128:(iq + 1) * 128]
                    msk = self.CONSTB[:, l * 128:(l + 1) * 128]
                    lc = lamc(l, a_)
                    if _val_iq[iq] is not None:
                        self.wait("vector", self.sV, _val_iq[iq])
                    if l == 0:
                        self.v(lambda e, t=tgt, m_=msk, lc=lc: e.tensor_scalar(
                            t, m_, lc, None, ALU.mult), [(self.sV, 1)])
                    else:
                        self.v(lambda e, t=tgt, m_=msk, lc=lc:
                               e.scalar_tensor_tensor(
                            t, m_, lc, t, ALU.mult, ALU.add), [(self.sV, 1)])
                    _val_iq[iq] = self.sV.val

            first_av = [True]
            av_cnt = [0]
            n_far = 4 * m
            n_near = {0: 0, 1: 1, 2: 2, 3: 3}
            total_av = n_far + sum(n_near[i] for i in range(4)) + 4

            evac_val = {}
            qk_val = {}
            def scs_slot(k):
                i = k % 8
                c = i * 512 if i < 7 else self.SCS2
                return self.MISC[:, c:c + 512]
            av_done_at = {}

            def emit_av(rhs, j, width, start, stop, col0):
                vap = self.KV[:, self.VOFF + j * 260 + h * 65:
                              self.VOFF + j * 260 + h * 65 + 64]
                self.mm(PB[2][0:64, col0:col0 + width], vap, rhs,
                        start=start, stop=stop, inc=[(self.sT, 1)])

            if y_evac_done:
                self.wait("tensor", self.sV, y_evac_done)
            if m > 0:
                self.wait("tensor", self.sV, prep_qf_done[idx])
            SBANKS = [0, 1, 3, 5]
            nb = (n_far + 3) // 4
            ev_v = {}
            ev_a = {}

            def qk_of(k):
                lvl = 10 if (k // 4) == (m ^ 1) else 11
                qmov = qf11_mov if lvl == 11 else qf10_mov
                return (PB[SBANKS[k % 4]],
                        self.KV[:, kpo + k * 128:kpo + (k + 1) * 128],
                        qmov)
            for kb in range(nb):
                ks = list(range(kb * 4, min(kb * 4 + 4, n_far)))
                ks0, ks1 = ks[0:2], ks[2:4]
                if kb >= 1:
                    self.wait("tensor", self.sV, ev_v[kb - 1])
                for k in ks0:
                    bank, st, mv = qk_of(k)
                    self.mm(bank[:, 0:512], st, mv, start=True, stop=True,
                            inc=[(self.sT, 1)] if k == ks0[-1] else ())
                qk0 = self.sT.val
                if kb >= 1:
                    self.wait("tensor", self.sA, ev_a[kb - 1])
                for k in ks1:
                    bank, st, mv = qk_of(k)
                    self.mm(bank[:, 0:512], st, mv, start=True, stop=True,
                            inc=[(self.sT, 1)] if k == ks1[-1] else ())
                qk1 = self.sT.val
                if kb >= 1:
                    for pk in range((kb - 1) * 4, kb * 4):
                        emit_av(scs_slot(pk), pk, 512, first_av[0], False, 0)
                        first_av[0] = False
                self.wait("vector", self.sT, qk0)
                for k in ks0:
                    cs = self.F32S[:, csc + k:csc + k + 1]
                    self.v(lambda e, b=PB[SBANKS[k % 4]], s=scs_slot(k), cs=cs:
                           e.tensor_scalar(s, b[:, 0:512], cs, None, ALU.mult),
                           [(self.sV, 1)])
                ev_v[kb] = self.sV.val
                if ks1:
                    self.wait("scalar", self.sT, qk1)
                    for k in ks1:
                        cs = self.F32S[:, csc + k:csc + k + 1]
                        self.a(lambda e, b=PB[SBANKS[k % 4]], s=scs_slot(k), cs=cs:
                               e.activation(s, b[:, 0:512], AF.Copy, scale=cs),
                               [(self.sA, 1)])
                ev_a[kb] = self.sA.val
                emit_mask_ops(6)
            if nb >= 1:
                self.wait("tensor", self.sV, ev_v[nb - 1])
                self.wait("tensor", self.sA, ev_a[nb - 1])
                for pk in range((nb - 1) * 4, n_far):
                    emit_av(scs_slot(pk), pk, 512, first_av[0], False, 0)
                    first_av[0] = False

            for iq in range(4):
                a = 4 * m + iq
                col0 = iq * 128
                near = []
                if iq == 1:
                    near = [(a - 1, 8)]
                elif iq == 2:
                    near = [(a - 2, 9), (a - 1, 9)]
                elif iq == 3:
                    near = [(a - 3, 9), (a - 2, 9), (a - 1, 8)]
                if near:
                    self.wait("tensor", self.sV, self.sV.val)
                    self.mm(PB[6][:, 1:2], self.ONE1[:, :],
                            self.F32S[0:1, self.CGT + h * 64 + a:
                                      self.CGT + h * 64 + a + 1], inc=[(self.sT, 1)])
                    self.wait("vector", self.sT, self.sT.val)
                    self.v(lambda e: e.tensor_copy(self.SCL[:, 2:3], PB[6][:, 1:2]))
                    self.v(lambda e: e.tensor_scalar(
                        self.SCL[:, 3:4], PB[6][:, 1:2], -1.0, None, ALU.mult),
                        [(self.sV, 1)])
                    self.wait("scalar", self.sV, self.sV.val)
                    self.a(lambda e, src_=cgc(a), rsc=rsc: e.activation(
                        self.F32S[:, rsc + 12:rsc + 13], src_,
                        AF.Exp, bias=self.SCL[:, 3:4]), [(self.sA, 1)])
                    nn = len(near)
                    self.a(lambda e, nn=nn, src_=cgcols(a - nn, nn), csc=csc:
                           e.activation(
                        self.F32S[:, csc + 60:csc + 60 + nn], src_,
                        AF.Exp, scale=-1.0, bias=self.SCL[:, 2:3]), [(self.sA, 1)])
                    self.wait("vector", self.sA, self.sA.val)
                    lvls = sorted(set(l for _, l in near))
                    for li, l in enumerate(lvls):
                        self.v(lambda e, li=li, src_=lamc(l, a), rsc=rsc:
                               e.scalar_tensor_tensor(
                            self.F32S[:, rsc + 13 + li:rsc + 14 + li],
                            src_, 0.125, self.F32S[:, rsc + 12:rsc + 13],
                            ALU.mult, ALU.mult), [(self.sV, 1)])
                    self.wait("scalar", self.sV, self.sV.val)
                    bct = self.F32S[:, mf:mf + 128]
                    for li, l in enumerate(lvls):
                        c0 = li * 64
                        self.a(lambda e, c0=c0, li=li, bct=bct, rsc=rsc:
                               e.activation(
                            bct[:, c0:c0 + 64], self.ONET[:, 0:64],
                            AF.Copy, scale=self.F32S[:, rsc + 13 + li:rsc + 14 + li]),
                            [(self.sA, 1)] if li == len(lvls) - 1 else ())
                    self.wait("tensor", self.sA, self.sA.val)
                    self.wait("tensor", self.sV, self.sV.val)
                    for li, l in enumerate(lvls):
                        self.tr(PB[6 + li][0:64, 0:128],
                                bct[:, li * 64:li * 64 + 64], 128,
                                inc=[(self.sT, 1)] if li == len(lvls) - 1 else ())
                    self.wait("vector", self.sT, self.sT.val)
                    for li, l in enumerate(lvls):
                        qreg = self.QN8 if l == 8 else self.QN9
                        self.v(lambda e, qreg=qreg, li=li, iq=iq, QS=QS, qrow=qrow:
                               e.tensor_mul(
                            self.MISC[qrow:qrow + 64, qreg:qreg + 128],
                            QS[0:64, iq * 128:(iq + 1) * 128],
                            PB[6 + li][0:64, 0:128]),
                            [(self.sV, 1)] if li == len(lvls) - 1 else ())
                    nq_done_a, nq_done_v = self.sA.val, self.sV.val
                    self.wait("tensor", self.sA, nq_done_a)
                    self.wait("tensor", self.sV, nq_done_v)
                    for ki, (kk, l) in enumerate(near):
                        bank = PB[kk % 2]
                        qreg = self.QN8 if l == 8 else self.QN9
                        self.mm(bank[:, 0:128],
                                self.KV[:, kpo + kk * 128:kpo + (kk + 1) * 128],
                                self.MISC[:, qreg:qreg + 128],
                                start=True, stop=True, inc=[(self.sT, 1)])
                        myqk = self.sT.val
                        self.wait("vector", self.sT, myqk)
                        cs = self.F32S[:, csc + 60 + ki:csc + 61 + ki]
                        sl = scs_slot(ki)
                        self.v(lambda e, b=bank, s=sl, cs=cs: e.tensor_scalar(
                            s[:, 0:128], b[:, 0:128], cs, None, ALU.mult),
                            [(self.sV, 1)])
                        self.wait("tensor", self.sV, self.sV.val)
                        emit_av(sl[:, 0:128], kk, 128, first_av[0], False, col0)
                        first_av[0] = False

            for iq in range(4):
                a = 4 * m + iq
                self.mm(PB[4][:, iq * 128:(iq + 1) * 128],
                        self.KV[kro:kro + 64, kpo + a * 128:kpo + (a + 1) * 128],
                        QS[0:64, iq * 128:(iq + 1) * 128],
                        start=(iq == 0), stop=(iq == 3),
                        inc=[(self.sT, 1)] if iq == 3 else ())
            qk_diag = self.sT.val
            self.wait("scalar", self.sT, self.sT.val)
            for iq in range(4):
                a = 4 * m + iq
                self.a(lambda e, iq=iq, sc=cgc(a): e.activation(
                    dd[:, iq * 128:(iq + 1) * 128], self.ONET[:, 0:128], AF.Copy,
                    scale=sc), [(self.sA, 1)] if iq == 3 else ())
            dd_done = self.sA.val
            TB = [6, 7, 3, 5]
            self.wait("tensor", self.sA, dd_done)
            for iq in range(4):
                self.tr(PB[TB[iq]][:, 0:128], dd[:, iq * 128:(iq + 1) * 128], 128,
                        inc=[(self.sT, 1)] if iq == 3 else ())
            self.wait("vector", self.sT, self.sT.val)
            for iq in range(4):
                a = 4 * m + iq
                self.v(lambda e, iq=iq, bank=PB[TB[iq]], sc=cgc(a):
                       e.tensor_scalar(
                    dd[:, iq * 128:(iq + 1) * 128], bank[:, 0:128], sc,
                    0.0, ALU.subtract, ALU.max), [(self.sV, 1)])
            self.wait("scalar", self.sV, self.sV.val)
            self.a(lambda e: e.activation(dd[:, 0:512], dd[:, 0:512], AF.Exp,
                                          scale=-1.0), [(self.sA, 1)])
            e2_done = self.sA.val
            emit_mask_ops(32)
            lam_acc_v, lam_acc_g = self.sV.val, self.sG.val
            self.wait("vector", self.sA, e2_done)
            self.wait("vector", self.sG, lam_acc_g)
            self.wait("vector", self.sV, lam_acc_v)
            self.v(lambda e: e.scalar_tensor_tensor(
                ff[:, 0:512], dd[:, 0:512], 0.125, ff[:, 0:512],
                ALU.mult, ALU.mult), [(self.sV, 1)])
            f_done = self.sV.val
            self._prev_f_done = f_done
            self.wait("vector", self.sT, qk_diag)
            sl0 = self.MISC[:, self.SCS0:self.SCS0 + 512]
            self.v(lambda e, sl0=sl0: e.tensor_copy(sl0, PB[4][:, 0:512]),
                   [(self.sV, 1)])
            scd_done = self.sV.val
            self.wait("vector", self.sV, scd_done)
            wd = self.MISC[:, self.SCS2:self.SCS2 + 512]
            self.wait("tensor", self.sV, f_done)
            for iq in range(4):
                self.tr(PB[TB[iq]][:, 0:128], ff[:, iq * 128:(iq + 1) * 128], 128,
                        inc=[(self.sT, 1)] if iq == 3 else ())
            self.wait("vector", self.sT, self.sT.val)
            wd_val = {}
            for iq in range(4):
                self.v(lambda e, iq=iq, bank=PB[TB[iq]], sl0=sl0, wd=wd:
                       e.tensor_mul(
                    wd[:, iq * 128:(iq + 1) * 128], sl0[:, iq * 128:(iq + 1) * 128],
                    bank[:, 0:128]), [(self.sV, 1)])
                wd_val[iq] = self.sV.val
            for iq in range(4):
                self.wait("tensor", self.sV, wd_val[iq])
                emit_av(wd[:, iq * 128:(iq + 1) * 128], 4 * m + iq, 128,
                        first_av[0], iq == 3, iq * 128)
                first_av[0] = False
            av_all = self.sT.val
            self._prev_av_all = av_all
            if "p2d" in self.dbg and (h, m) == (1, 15):
                self.wait("sync", self.sT, av_all)
                self.wait("sync", self.sV, self.sV.val)
                self.tap_sbuf("dd", dd, (128, 512), F32)
                self.tap_sbuf("ff", ff, (128, 512), F32)
                self.tap_sbuf("sl0", sl0, (128, 512), BF16)
                self.tap_sbuf("wd", wd, (128, 512), BF16)
                self.tap_sbuf("qsd", self.QSTG[:, slot * 512:(slot + 1) * 512],
                              (128, 512), BF16)
                self.tap_sbuf("qf11", self.MISC[:, self.QF11:self.QF11 + 512],
                              (128, 512), BF16)
                self.tap_sbuf("qf10", self.MISC[:, self.QF10:self.QF10 + 512],
                              (128, 512), BF16)
                self.tap_sbuf("rsc", self.F32S[:, self.RSC:self.RSC + 12],
                              (128, 12), F32)
                self.tap_sbuf("csc", self.F32S[:, self.CSC:self.CSC + 60],
                              (128, 60), F32)
                self.wait("gpsimd", self.sDs, self.sDs.val)
                self.wait("vector", self.sDs, self.sDs.val)
                self.wait("scalar", self.sDs, self.sDs.val)
                self.wait("tensor", self.sDs, self.sDs.val)
            self.wait("vector", self.sT, av_all)
            if y_evac_done:
                self.wait("vector", self.sDc, self.y_dma_done)
            self.v(lambda e: e.tensor_copy(self.Y2[0:64, 0:512], PB[2][0:64, 0:512]),
                   [(self.sV, 1)])
            y_evac_done = self.sV.val
            self.wait("scalar", self.sV, y_evac_done)
            self.dma("scalar", self.Ya_d[h * 64:(h + 1) * 64, m * 512:(m + 1) * 512],
                     self.Y2[0:64, 0:512], self.sDc)
            self.y_dma_done = self.sDc.val
            self._qs_guard[slot] = (self.sT.val, self.sV.val)
            if m == NG - 1:
                self.wait("gpsimd", self.sDc, self.y_dma_done)
                self.on("gpsimd", lambda e, h=h: e.collective_compute(
                    "AllGather", ALU.bypass,
                    replica_groups=[[0, 1, 2, 3], [4, 5, 6, 7]],
                    ins=[self.Ya_d[h * 64:(h + 1) * 64, :].opt()],
                    outs=[self.Yg_d[h * 256:(h + 1) * 256, :].opt()]),
                    [(self.sCC, 1)])
                self.yg_head_done[h] = self.sCC.val
        self.yg_ready = self.sCC.val
        self.p2_y_evac_done = y_evac_done
        if "p2" in self.dbg:
            self.wait("sync", self.sDc, self.y_dma_done)
            self.tap_dram("Ya", self.Ya_d, (256, T), BF16)

    def sEng(self, eng):
        return {"vector": self.sV, "scalar": self.sA, "tensor": self.sT,
                "gpsimd": self.sG}[eng]

    def p3_collect_y(self, src, dst):
        self.wait("gpsimd", self.sDc, self.y_dma_done)
        self.on("gpsimd", lambda e, s=src, d=dst: e.collective_compute(
            "AllGather", ALU.bypass,
            replica_groups=[[0, 1, 2, 3], [4, 5, 6, 7]],
            ins=[s[:, :].opt()], outs=[d[:, :].opt()]), [(self.sCC, 1)])
        self.yg_ready = self.sCC.val

    def emit_wcache(self, wdram):
        casts = []
        for ch in range(8):
            for ot in range(4):
                casts.append((wdram[ch * 128:(ch + 1) * 128,
                                    ot * 256:(ot + 1) * 256],
                              (ch * 8 + ot * 2) * 128, 256))
        self.emit_casts(casts, self.KV)

    def p4_close_attn(self, kind):
        PB, W = self.PB, TGW
        xlb = lambda ch: self.MISC[:, ch * W:(ch + 1) * W]
        yst = lambda ch: self.MISC[:, 2048 + ch * W:2048 + (ch + 1) * W]
        obuf = lambda ch: self.F32S[:, ch * W:(ch + 1) * W]
        wt = lambda ch, ot: self.KV[:, (ch * 8 + ot) * 128:(ch * 8 + ot) * 128 + 128]
        tmpt = self.STG2[:, 0:W]

        self.wait("gpsimd", self.sV, self.sV.val)
        self.wait("gpsimd", self.sT, self.sT.val)
        self.emit_wcache(self.i["aWoutT"] if kind == "a" else self.i["bWoT"])
        wc_done = self.wcast_g

        qslot = {"v": None}

        yo_done = {}
        for hh in range(4):
            self.wait("sync", self.sCC, self.yg_head_done[hh])
            self.on("sync", lambda e, hh=hh: e.dma_start(
                out=self.Yo_d[hh * 256:(hh + 1) * 256, :],
                in_=self.Yg_d[hh * 256:(hh + 1) * 256,
                              bass.ts(self._q_sync, TOWN)]), [(self.sDs, 16)])
            yo_done[hh] = self.sDs.val
        self.wait("sync", self.sDs, self.sDs.val)

        def dyn_yg(e, ch, og):
            return e.dma_start(out=yst(ch),
                               in_=self.Yo_d[ch * 128:(ch + 1) * 128,
                                             og * W:(og + 1) * W])
        self.wait("tensor", self.sG, wc_done)
        prev_mm = 0
        for og in range(8):
            if kind == "a":
                if og > 0:
                    self.wait("tensor", self.sV, self.sV.val)
                    self.wait("vector", self.sT, prev_mm)
                self.emit_ln(self.emit_ln_pairs_from_R(og),
                             lambda pi: self.MISC[:, pi * 512:(pi + 1) * 512])
            if og > 0:
                self.wait("sync", self.sT, prev_mm)
            for ch in range(8):
                self.on("sync", lambda e, ch=ch, og=og: dyn_yg(e, ch, og),
                        [(self.sDs, 16)])
            yg_ld = self.sDs.val
            self.wait("sync", self.sDs, yg_ld)
            self.wait("tensor", self.sDs, yg_ld)
            if og > 0:
                self.wait("tensor", self.sV, self.sV.val)
            for oh in range(2):
                if oh == 1:
                    self.wait("tensor", self.sV, self.sV.val)
                for ch in range(8):
                    for ot in range(4):
                        self.mm(PB[ot][:, 0:W], wt(ch, oh * 4 + ot), yst(ch),
                                start=(ch == 0), stop=(ch == 7),
                                inc=[(self.sT, 1)] if (ch == 7 and ot == 3) else ())
                mmv = self.sT.val
                self.wait("vector", self.sT, mmv)
                if oh == 0:
                    self.wait("vector", self.sA, self.sA.val)
                    self.wait("vector", self.sV, self.sV.val)
                for ot in range(4):
                    och = oh * 4 + ot
                    if kind == "a":
                        self.v(lambda e, ot=ot, och=och: e.tensor_add(
                            obuf(och), PB[ot][:, 0:W], xlb(och)),
                            [(self.sV, 1)])
                    else:
                        self.wait("vector", self.sV, self.sV.val)
                        self.v(lambda e, ot=ot, och=och: e.tensor_scalar(
                            tmpt, PB[ot][:, 0:W], self.BBS[:, 15:16],
                            self.BBS[:, 4 + och:5 + och], ALU.mult, ALU.add),
                            [(self.sV, 1)])
                        self.wait("vector", self.sV, self.sV.val)
                        self.v(lambda e, och=och, og=og: e.tensor_add(
                            self.R[:, och * TOWN + og * W:
                                   och * TOWN + (og + 1) * W],
                            self.R[:, och * TOWN + og * W:
                                   och * TOWN + (og + 1) * W], tmpt),
                            [(self.sV, 1)])
            prev_mm = self.sT.val
            if kind == "a":
                ob_done = self.sV.val
                self.wait("tensor", self.sV, ob_done)
                self.wait("scalar", self.sV, ob_done)
                self.emit_ln_stats_only(
                    lambda pi: self.F32S[:, pi * 512:(pi + 1) * 512])
                r3 = self.R[:, :].rearrange("p (c t) -> p c t", c=8)
                p6f = PB[6][:, 0:512].rearrange("p (c t) -> p c t", c=2)
                p7f = PB[7][:, 0:512].rearrange("p (c t) -> p c t", c=2)
                tp2 = self.STG2[:, 0:512]
                tp3 = tp2.rearrange("p (c t) -> p c t", c=2)
                for pi in range(4):
                    if pi > 0:
                        self.wait("vector", self.sV, self.sV.val)
                    self.v(lambda e, pi=pi, tp3=tp3: e.tensor_mul(
                        tp3, self.F32S[:, pi * 512:(pi + 1) * 512]
                        .rearrange("p (c t) -> p c t", c=2), p6f), [(self.sV, 1)])
                    self.wait("vector", self.sV, self.sV.val)
                    self.v(lambda e, tp3=tp3: e.tensor_sub(tp3, tp3, p7f),
                           [(self.sV, 1)])
                    self.wait("vector", self.sV, self.sV.val)
                    for half in range(2):
                        ch = pi * 2 + half
                        self.v(lambda e, ch=ch, half=half, tp2=tp2: e.tensor_scalar(
                            tp2[:, half * W:(half + 1) * W],
                            tp2[:, half * W:(half + 1) * W],
                            self.LNG[:, ch:ch + 1],
                            self.LNG[:, 8 + ch:9 + ch], ALU.mult, ALU.add),
                            [(self.sV, 1)] if half == 1 else ())
                    self.wait("vector", self.sV, self.sV.val)
                    self.v(lambda e, pi=pi, og=og, tp3=tp3, r3=r3: e.tensor_add(
                        r3[:, 2 * pi:2 * pi + 2, og * W:(og + 1) * W],
                        r3[:, 2 * pi:2 * pi + 2, og * W:(og + 1) * W],
                        tp3), [(self.sV, 1)])
        self.close_done = self.sV.val

    def emit_ln_stats_only(self, src_pair):
        self.emit_ln(src_pair, None, affine="stats")

    def p5_ffn(self, idx):
        PB, W = self.PB, TGW
        i = self.i
        xlb = lambda ch: self.MISC[:, ch * W:(ch + 1) * W]

        if idx == 0:
            self.crl_d = self.nc.dram_tensor("crl_d", [16, 16384], BF16,
                                             kind="Internal").ap()
            for h in range(4):
                for rb in range(8):
                    r0 = rb * 8
                    self.wait("gpsimd", self.sV, self.sV.val)
                    self.dma("gpsimd",
                             self.STG[0:16, 0:128].rearrange("p (r i) -> p r i", r=8),
                             i["makerT"][h, r0:r0 + 8].rearrange("r j i -> j r i"),
                             self.sDg)
                    self.dma("gpsimd",
                             self.STG2[0:16, 0:512].rearrange("p (r k) -> p r k", r=8),
                             i["upT"][h, r0:r0 + 8].rearrange("r j k -> j r k"),
                             self.sDg)
                    dgv = self.sDg.val
                    self.wait("tensor", self.sDg, dgv)
                    if h + rb > 0:
                        self.wait("tensor", self.sV, self.sV.val)
                    for ri in range(8):
                        self.mm(PB[0][0:16, ri * 64:(ri + 1) * 64],
                                self.STG[0:16, ri * 16:(ri + 1) * 16],
                                self.STG2[0:16, ri * 64:(ri + 1) * 64],
                                start=(ri == 0), stop=(ri == 7),
                                inc=[(self.sT, 1)] if ri == 7 else ())
                    self.wait("vector", self.sT, self.sT.val)
                    if h + rb > 0:
                        self.wait("vector", self.sDc, self.sDc.val)
                    self.v(lambda e: e.tensor_copy(self.Y2[0:16, 0:512],
                                                   PB[0][0:16, 0:512]),
                           [(self.sV, 1)])
                    self.wait("scalar", self.sV, self.sV.val)
                    crl3 = self.crl_d.rearrange("p (c x) -> p c x", x=128)
                    c0_ = r0 * 2 + h // 2
                    dst = crl3[0:16, c0_:c0_ + 15:2,
                               (h % 2) * 64:(h % 2) * 64 + 64]
                    self.on("scalar", lambda e, dst=dst: e.dma_start(
                        out=dst, in_=self.Y2[0:16, 0:512]
                        .rearrange("p (r c) -> p r c", r=8)), [(self.sDc, 16)])
            self.crl_ready = self.sDc.val

        self.wait("tensor", self.sV, self.sV.val)
        self.wait("scalar", self.sV, self.sV.val)
        x2dma = 0
        for og in range(8):
            if og > 0:
                self.wait("tensor", self.sV, self.sV.val)
                self.wait("vector", self.sDs, x2dma)
            self.emit_ln(self.emit_ln_pairs_from_R(og),
                         lambda pi: self.MISC[:, pi * 512:(pi + 1) * 512])
            self.wait("sync", self.sV, self.sV.val)
            for ch in range(8):
                self.dma("sync", self.x2_d[ch * 128:(ch + 1) * 128,
                                           og * W:(og + 1) * W], xlb(ch), self.sDs)
            x2dma = self.sDs.val
            self.wait("sync", self.sDs, x2dma)
        x2_done = self.sDs.val

        self.wait("gpsimd", self.sT, self.sT.val)
        self.wait("gpsimd", self.sDs, x2_done)
        self.wait("gpsimd", self.sV, self.sV.val)
        self.wait("scalar", self.sDc, self.sDc.val)
        xr_slots = [self.XR, self.QSTG[:, 0:2048]]
        xr_sems = [self.sQ0, self.sQ1]
        xr_ld = {}
        x2r = self.x2_d.rearrange("(r i) t -> i r t", i=16)

        def xr_dma(it, guard):
            sp_, g_ = it // 8, it % 8
            slot_ = it % 2
            if guard is not None:
                self.wait("sync", self.sT, guard)
            for half in range(2):
                self.dma("sync",
                         xr_slots[slot_][0:16, half * 1024:(half + 1) * 1024],
                         x2r[0:16, sp_ * 8 + half * 4:sp_ * 8 + (half + 1) * 4,
                             g_ * W:(g_ + 1) * W], xr_sems[slot_])
            xr_ld[it] = xr_sems[slot_].val
        prev_rails = 0
        prev_radd = 0
        prev_gelu = [0, 0]
        for sp in range(8):
            slot = sp % 2
            if sp >= 2:
                self.wait("gpsimd", self.sT, self._ring_used[slot])
            dst = self.KV[:, slot * 16384:(slot + 1) * 16384] \
                .rearrange("p (c o) -> p c o", c=16)
            srcw = self.wp_bf[sp * 2048:(sp + 1) * 2048, :] \
                .rearrange("(c p) o -> p c o", p=128)
            if sp == 0:
                self.wait("gpsimd", self.sG, self.sG.val)
            self.dma("gpsimd", dst, srcw, self.sDg)
            ring_ld = self.sDg.val
            if sp > 0:
                self.wait("gpsimd", self.sT, prev_rails)
            if sp == 0 and idx == 0:
                self.wait("gpsimd", self.sDc, self.crl_ready)
            self.dma("gpsimd", self.CRL[0:16, 0:2048],
                     self.crl_d[0:16, sp * 2048:(sp + 1) * 2048], self.sDg)
            crl_ld = self.sDg.val
            self.wait("gpsimd", self.sDg, crl_ld)
            for g in range(8):
                it = sp * 8 + g
                slot_x = it % 2
                if it == 0:
                    xr_dma(0, self.sT.val)
                    xr_dma(1, None)
                elif it + 1 < 64:
                    xr_dma(it + 1, prev_rails)
                self.wait("tensor", xr_sems[slot_x], xr_ld[it])
                if g == 0:
                    self.wait("tensor", self.sDg, crl_ld)
                gelu_of = {}

                def emit_rails(c16):
                    zb = PB[c16 % 2]
                    if not (sp == 0 and g == 0 and c16 < 2):
                        self.wait("tensor", self.sA, prev_gelu[c16 % 2])
                    self.mm(zb[:, 0:W], self.CRL[0:16, c16 * 128:(c16 + 1) * 128],
                            xr_slots[slot_x][0:16, (c16 // 2) * 256:
                                             (c16 // 2) * 256 + 256],
                            start=True, stop=True, inc=[(self.sT, 1)])
                    self.wait("scalar", self.sT, self.sT.val)
                    hch = self.HCH[:, (c16 % 2) * 256:(c16 % 2) * 256 + 256]
                    self.a(lambda e, zb=zb, hch=hch: e.activation(
                        hch, zb[:, 0:W], AF.Gelu), [(self.sA, 1)])
                    prev_gelu[c16 % 2] = self.sA.val
                    gelu_of[c16] = (self.sA.val, hch)

                def emit_cproj(c16):
                    gv, hch = gelu_of[c16]
                    self.wait("tensor", self.sA, gv)
                    if c16 == 0 and not (sp == 0 and g == 0):
                        self.wait("tensor", self.sV, prev_radd)
                    for ot in range(8):
                        bank = PB[4 + ot // 2]
                        cols = (ot % 2) * 256
                        st_flag = (c16 == 0 and ot % 2 == 0)
                        sp_flag = (c16 == 15 and ot % 2 == 1)
                        self.mm(bank[:, cols:cols + W],
                                self.KV[:, slot * 16384 + c16 * 1024 + ot * 128:
                                        slot * 16384 + c16 * 1024 + (ot + 1) * 128],
                                hch, start=st_flag, stop=sp_flag,
                                inc=[(self.sT, 1)] if (c16 == 15 and ot == 7) else ())

                emit_rails(0)
                for c16 in range(16):
                    if c16 + 1 < 16:
                        emit_rails(c16 + 1)
                    emit_cproj(c16)
                prev_rails = self.sT.val
                self._ring_used = getattr(self, "_ring_used", [0, 0])
                self._ring_used[slot] = self.sT.val
                self.wait("vector", self.sT, self.sT.val)
                r3 = self.R[:, :].rearrange("p (c t) -> p c t", c=8)
                for pi in range(4):
                    bank3 = PB[4 + pi][:, 0:512].rearrange("p (c t) -> p c t", c=2)
                    self.v(lambda e, pi=pi, g=g, bank3=bank3, r3=r3: e.tensor_add(
                        r3[:, 2 * pi:2 * pi + 2, g * W:(g + 1) * W],
                        r3[:, 2 * pi:2 * pi + 2, g * W:(g + 1) * W], bank3),
                        [(self.sV, 1)] if pi == 3 else ())
                prev_radd = self.sV.val

    def p6_attn_b(self):
        PB, W = self.PB, TGW
        i = self.i
        xlb = lambda ch: self.MISC[:, ch * W:(ch + 1) * W]
        wsb = self.WSB

        self.wait("tensor", self.sV, self.sV.val)
        self.wait("scalar", self.sV, self.sV.val)
        c3dma = 0
        for og in range(8):
            if og > 0:
                self.wait("tensor", self.sV, self.sV.val)
                self.wait("vector", self.sDs, c3dma)
            self.emit_ln(self.emit_ln_pairs_from_R(og),
                         lambda pi: self.MISC[:, pi * 512:(pi + 1) * 512])
            self.wait("sync", self.sV, self.sV.val)
            for ch in range(8):
                self.dma("sync", self.c3_d[ch * 128:(ch + 1) * 128,
                                           og * W:(og + 1) * W], xlb(ch), self.sDs)
            c3dma = self.sDs.val
            self.wait("sync", self.sDs, c3dma)
        c3_done = self.sDs.val
        self.wait("gpsimd", self.sDs, c3_done)
        self.on("gpsimd", lambda e: e.collective_compute(
            "AllGather", ALU.bypass,
            replica_groups=[[0, 1, 2, 3], [4, 5, 6, 7]],
            ins=[self.c3_d[:, :].opt()], outs=[self.xg3_d[:, :].opt()]),
            [(self.sCC, 1)])
        xg3_ready = self.sCC.val

        casts = []
        for ch in range(8):
            casts += [(i["bWqT"][ch * 128:(ch + 1) * 128, :], self.wq_off + ch * 256, 256),
                      (i["bWkT"][ch * 128:(ch + 1) * 128, :], self.wk_off + ch * 256, 256),
                      (i["bWvT"][ch * 128:(ch + 1) * 128, :], self.wv_off + ch * 256, 256)]
        self.wait("gpsimd", self.sT, self.sT.val)
        self.emit_casts(casts, self.WSB)
        wb_done = self.wcast_g
        self.dma("sync", self.BBS[:, 0:2], i["bb_q"][:, :], self.sDs)
        self.dma("sync", self.BBS[:, 2:4], i["bb_k"][:, :], self.sDs)
        self.dma("sync", self.BBS[:, 4:12], i["bb_o"][:, :], self.sDs)
        self.dma("sync", self.STG[0:1, 0:1], i["bs_q"][:, :], self.sDs)
        self.dma("sync", self.STG[0:1, 1:2], i["bs_k"][:, :], self.sDs)
        self.dma("sync", self.STG[0:1, 2:3], i["bs_v"][:, :], self.sDs)
        self.dma("sync", self.STG[0:1, 3:4], i["bs_o"][:, :], self.sDs)
        self.dma("sync", self.STG[0:1, 4:260], i["bb_v"][:, :], self.sDs)
        sc_ld = self.sDs.val
        self.wait("sync", self.sDs, sc_ld)
        self.wait("tensor", self.sDs, sc_ld)
        self.wait("tensor", self.sV, self.sV.val)
        for k in range(4):
            self.mm(PB[0][:, k:k + 1], self.ONE1[:, :], self.STG[0:1, k:k + 1],
                    start=(k == 0), stop=(k == 3))
        self.mm(PB[1][:, 0:256], self.ONE1[:, :], self.STG[0:1, 4:260],
                inc=[(self.sT, 1)])
        self.wait("vector", self.sT, self.sT.val)
        self.v(lambda e: e.tensor_copy(self.BBS[:, 12:16], PB[0][:, 0:4]))
        self.v(lambda e: e.tensor_copy(self.BVB[:, 0:256], PB[1][:, 0:256]),
               [(self.sV, 1)])
        sc_done = self.sV.val

        self.v(lambda e: e.memset(
            self.KV[:, self.VOFF + 64:self.VOFF + 16640:65], 1.0), [(self.sV, 1)])

        self.wait("sync", self.sCC, xg3_ready)
        self.wait("tensor", self.sV, sc_done)
        self.wait("tensor", self.sG, wb_done)
        proj_done = {}
        v_evac_done = 0
        xlb2 = lambda s, ch: self.MISC[:, s * 2048 + ch * W:s * 2048 + (ch + 1) * W]
        xsem = [self.sQ0, self.sQ1]
        xg_ld = {}

        def xg_dma(tg_):
            s = tg_ % 2
            rblk, ccol = (tg_ // 8) * 1024, (tg_ % 8) * W
            if tg_ >= 2:
                self.wait("sync", self.sT, proj_done[tg_ - 2])
            for ch in range(8):
                self.dma("sync", xlb2(s, ch),
                         self.xg3_d[rblk + ch * 128:rblk + (ch + 1) * 128,
                                    ccol:ccol + W], xsem[s])
            xg_ld[tg_] = xsem[s].val
        xg_dma(0)
        xg_dma(1)
        for tg in range(NTG):
            s = tg % 2
            self.wait("tensor", xsem[s], xg_ld[tg])
            if tg > 0:
                self.wait("tensor", self.sV, v_evac_done)
            for ch in range(8):
                for p in range(2):
                    self.mm(PB[0 + p][:, 0:W],
                            wsb[:, self.wq_off + ch * 256 + p * 128:
                                self.wq_off + ch * 256 + (p + 1) * 128],
                            xlb2(s, ch), start=(ch == 0), stop=(ch == 7))
                    self.mm(PB[2 + p][:, 0:W],
                            wsb[:, self.wk_off + ch * 256 + p * 128:
                                self.wk_off + ch * 256 + (p + 1) * 128],
                            xlb2(s, ch), start=(ch == 0), stop=(ch == 7))
                for sub in range(2):
                    self.mm(PB[6 + sub][:, 0:256],
                            xlb2(s, ch)[:, sub * 128:(sub + 1) * 128],
                            wsb[:, self.wv_off + ch * 256:self.wv_off + (ch + 1) * 256],
                            start=(ch == 0), stop=(ch == 7),
                            inc=[(self.sT, 1)] if (ch == 7 and sub == 1) else ())
            proj_done[tg] = self.sT.val
            if tg + 2 < NTG:
                xg_dma(tg + 2)
            self.wait("vector", self.sT, proj_done[tg])
            self.wait("vector", self.sDs, self.sDs.val)
            for p in range(2):
                self.v(lambda e, p=p, tg=tg: e.tensor_scalar(
                    self.KV[:, p * T + tg * W:p * T + (tg + 1) * W],
                    PB[2 + p][:, 0:W], self.BBS[:, 13:14],
                    self.BBS[:, 2 + p:3 + p], ALU.mult, ALU.add))
                self.v(lambda e, p=p: e.tensor_scalar(
                    self.QSTG[:, p * W:(p + 1) * W], PB[0 + p][:, 0:W],
                    self.BBS[:, 12:13], self.BBS[:, 0 + p:1 + p],
                    ALU.mult, ALU.add),
                    [(self.sV, 1)] if p == 1 else ())
            qk_ev = self.sV.val
            self.wait("sync", self.sV, qk_ev)
            for p in range(2):
                self.dma("sync", self.Qa_d[p][:, tg * W:(tg + 1) * W],
                         self.QSTG[:, p * W:(p + 1) * W], self.sDs)
            self.q_dma_done = self.sDs.val
            for sub in range(2):
                j = tg * 2 + sub
                dst = self.KV[:, self.VOFF + j * 260:self.VOFF + (j + 1) * 260]
                dst3 = dst.rearrange("p (h c) -> p h c", h=4)[:, :, 0:64]
                src3 = PB[6 + sub][:, 0:256].rearrange("p (h c) -> p h c", h=4)
                bvb3 = self.BVB[:, 0:256].rearrange("p (h c) -> p h c", h=4)
                self.v(lambda e, d=dst3, s=src3, b=bvb3: e.scalar_tensor_tensor(
                    d, s, self.BBS[:, 14:15], b, ALU.mult, ALU.add),
                    [(self.sV, 1)] if sub == 1 else ())
            v_evac_done = self.sV.val

        self.wait("tensor", self.sV, self.sV.val)
        self.wait("sync", self.sDs, self.sDs.val)
        self.wait("vector", self.sDs, self.q_dma_done)
        self.v(lambda e: e.memset(self.QSTG[:, 0:4096], 0.0), [(self.sV, 1)])
        qz = self.sV.val
        SBK = [0, 1, 3, 5]
        YBK = [2, 4, 6, 7]
        hmg = [(h, mg) for h in range(4) for mg in range(4)]
        self.yg_head_done = {}
        qsem = [self.sQ0, self.sQ1]
        qs_guard = [0, 0]

        def qs_dma(idx2):
            h, mg = hmg[idx2]
            slot = idx2 % 2
            qrow = (h % 2) * 64
            self.wait("sync", self.sT, qs_guard[slot])
            self.dma("sync",
                     self.QSTG[qrow:qrow + 64, slot * 2048:(slot + 1) * 2048],
                     self.Qa_d[h // 2][qrow:qrow + 64, mg * 2048:(mg + 1) * 2048],
                     qsem[slot])
            return qsem[slot].val
        self.wait("sync", self.sV, qz)
        qs_done = {0: qs_dma(0)}
        self.wait("tensor", self.sV, qz)
        fin_v = 0
        scs = lambda k, mq: self.MISC[:, ((k % 2) * 4 + mq) * 512:
                                      ((k % 2) * 4 + mq) * 512 + 512]
        for idx2, (h, mg) in enumerate(hmg):
            slot = idx2 % 2
            qrow = (h % 2) * 64
            kpo = (h // 2) * T
            if idx2 + 1 < len(hmg):
                qs_done[idx2 + 1] = qs_dma(idx2 + 1)
            if mg == 0 and h > 0:
                qrow_old = ((h - 1) % 2) * 64
                self.wait("vector", self.sT, self.sT.val)
                self.v(lambda e, r=qrow_old: e.memset(
                    self.QSTG[r:r + 64, 0:4096], 0.0), [(self.sV, 1)])
                fin_v = self.sV.val
            QSL = self.QSTG[:, slot * 2048:(slot + 1) * 2048]
            self.wait("tensor", qsem[slot], qs_done[idx2])
            if fin_v:
                self.wait("tensor", self.sV, fin_v)
            evA = {}
            evB = {}
            for k in range(NT):
                if k > 0:
                    self.wait("tensor", self.sA, evA[k - 1])
                for mq in range(2):
                    self.mm(PB[SBK[mq]][:, 0:512],
                            self.KV[:, kpo + k * 128:kpo + (k + 1) * 128],
                            QSL[:, mq * 512:(mq + 1) * 512],
                            start=True, stop=True,
                            inc=[(self.sT, 1)] if mq == 1 else ())
                qka = self.sT.val
                if k > 0:
                    self.wait("tensor", self.sA, evB[k - 1])
                for mq in range(2, 4):
                    self.mm(PB[SBK[mq]][:, 0:512],
                            self.KV[:, kpo + k * 128:kpo + (k + 1) * 128],
                            QSL[:, mq * 512:(mq + 1) * 512],
                            start=True, stop=True,
                            inc=[(self.sT, 1)] if mq == 3 else ())
                qkb = self.sT.val
                if k > 0:
                    kp = k - 1
                    for mq in range(4):
                        self.mm(PB[YBK[mq]][0:65, 0:512],
                                self.KV[:, self.VOFF + kp * 260 + h * 65:
                                        self.VOFF + kp * 260 + h * 65 + 65],
                                scs(kp, mq), start=(kp == 0), stop=False)
                self.wait("scalar", self.sT, qka)
                for mq in range(2):
                    self.a(lambda e, b=PB[SBK[mq]], s=scs(k, mq):
                           e.activation(s, b[:, 0:512], AF.Exp, scale=0.125),
                           [(self.sA, 1)] if mq == 1 else ())
                evA[k] = self.sA.val
                self.wait("scalar", self.sT, qkb)
                for mq in range(2, 4):
                    self.a(lambda e, b=PB[SBK[mq]], s=scs(k, mq):
                           e.activation(s, b[:, 0:512], AF.Exp, scale=0.125),
                           [(self.sA, 1)] if mq == 3 else ())
                evB[k] = self.sA.val
            self.wait("tensor", self.sA, evB[NT - 1])
            for mq in range(4):
                self.mm(PB[YBK[mq]][0:65, 0:512],
                        self.KV[:, self.VOFF + (NT - 1) * 260 + h * 65:
                                self.VOFF + (NT - 1) * 260 + h * 65 + 65],
                        scs(NT - 1, mq), start=False, stop=True,
                        inc=[(self.sT, 1)] if mq == 3 else ())
            av_all = self.sT.val
            qs_guard[slot] = av_all
            ybuf = [self.Y2[0:64, 0:512], self.Y2[0:64, 512:1024],
                    self.HCH[0:64, 0:512], self.WPC[0:64, 0:512]]
            self.wait("vector", self.sT, av_all)
            self.wait("vector", self.sDc, self.sDc.val)
            for mq in range(4):
                self.v(lambda e, mq=mq: e.reciprocal(
                    self.F32S[0:1, mq * 512:(mq + 1) * 512],
                    PB[YBK[mq]][64:65, 0:512]),
                    [(self.sV, 1)] if mq == 3 else ())
            rec_done = self.sV.val
            self.wait("tensor", self.sV, rec_done)
            for mq in range(4):
                self.mm(PB[SBK[mq]][0:64, 0:512], self.ONE1[0:1, 0:64],
                        self.F32S[0:1, mq * 512:(mq + 1) * 512],
                        inc=[(self.sT, 1)] if mq == 3 else ())
            bc_done = self.sT.val
            self.wait("vector", self.sT, bc_done)
            cp_val = {}
            for mq in range(4):
                self.v(lambda e, mq=mq: e.tensor_copy(
                    self.F32S[0:64, 2048 + mq * 512:2048 + (mq + 1) * 512],
                    PB[SBK[mq]][0:64, 0:512]), [(self.sV, 1)])
                cp_val[mq] = self.sV.val
            for mq in range(4):
                self.wait("vector", self.sV, cp_val[mq])
                self.v(lambda e, mq=mq: e.tensor_mul(
                    ybuf[mq], PB[YBK[mq]][0:64, 0:512],
                    self.F32S[0:64, 2048 + mq * 512:2048 + (mq + 1) * 512]),
                    [(self.sV, 1)])
                mv = self.sV.val
                self.wait("scalar", self.sV, mv)
                self.dma("scalar",
                         self.Ya_d[h * 64:(h + 1) * 64,
                                   mg * 2048 + mq * 512:mg * 2048 + (mq + 1) * 512],
                         ybuf[mq], self.sDc)
            fin_v = self.sV.val
            self.y_dma_done = self.sDc.val
            if mg == 3:
                self.wait("gpsimd", self.sDc, self.y_dma_done)
                self.on("gpsimd", lambda e, h=h: e.collective_compute(
                    "AllGather", ALU.bypass,
                    replica_groups=[[0, 1, 2, 3], [4, 5, 6, 7]],
                    ins=[self.Ya_d[h * 64:(h + 1) * 64, :].opt()],
                    outs=[self.Yg_d[h * 256:(h + 1) * 256, :].opt()]),
                    [(self.sCC, 1)])
                self.yg_head_done[h] = self.sCC.val
        self.yg_ready = self.sCC.val
        if "p6" in self.dbg:
            self.wait("sync", self.sDc, self.y_dma_done)
            self.tap_dram("Yb", self.Ya_d, (256, T), BF16)
        self.p4_close_attn("b")

    def p7_out(self):
        self.wait("sync", self.sV, self.sV.val)
        self.wait("sync", self.sT, self.sT.val)
        self.wait("sync", self.sA, self.sA.val)
        for ch in range(8):
            self.dma("sync", self.out_d[ch * 128:(ch + 1) * 128, :],
                     self.R[:, ch * TOWN:(ch + 1) * TOWN], self.sDs)
        self.wait("sync", self.sDs, self.sDs.val)


def build_nc(phases=99, dbg=()):
    nc = bass.Bass("TRN2", target_bir_lowering=False, debug=False)
    mk = MK(nc, phases=phases, dbg=dbg)
    mk.build()
    return nc, mk





def kernel(**inputs):
    consts = make_consts()
    in_maps = []
    for c in range(8):
        m = host_prep(inputs, c)
        m.update(consts)
        in_maps.append(m)
    nc, _ = build_nc(phases=8)
    from concourse.bass_utils import run_bass_kernel_spmd
    res = run_bass_kernel_spmd(nc, in_maps, core_ids=list(range(8)))
    out = np.empty((2, T, D), dtype=np.float32)
    for c in range(8):
        b, q = c // 4, c % 4
        out[b, q * TOWN:(q + 1) * TOWN, :] = np.asarray(res.results[c]["out"]).T
    return out
